# version 18
# baseline (speedup 1.0000x reference)
import numpy as np
import ml_dtypes
from contextlib import ExitStack
import concourse.bass as bass
import concourse.mybir as mybir
from concourse.bass_utils import run_bass_kernel_spmd

F32 = mybir.dt.float32
BF16 = mybir.dt.bfloat16
AF = mybir.ActivationFunctionType
ALU = mybir.AluOpType
AX = mybir.AxisListType.X

H = 1024
T = 4096
TC = 512
NCH = T // TC
HT = H // 128
NE = 8
NW = 5
NWT = 2

_CACHE = {}
DBG = False
STREAM_Q = "sync"
SKIP_STREAM_WAITS = False
PROBE_DUMMY_SRC = False
PROBE_TAIL = False
PROBE_SKIP_GATE = False
DBG_SCR = False
PROBE_GATE_LVL = 5
PROBE_SKIP_X = False
STOP = None


class _StopEmit(Exception):
    pass


def build():
    nc = bass.Bass("TRN2", target_bir_lowering=False, debug=False,
                   detect_race_conditions=False)

    ixT = nc.dram_tensor("xT", [H, T], F32, kind="ExternalInput").ap()
    iW0 = nc.dram_tensor("W0", [H, 3 * H], F32, kind="ExternalInput").ap()
    iW1 = nc.dram_tensor("W1", [H, 3 * H], F32, kind="ExternalInput").ap()
    iWT = nc.dram_tensor("WT", [NE, H, H], F32, kind="ExternalInput").ap()
    iBF = nc.dram_tensor("BF", [NE, H], F32, kind="ExternalInput").ap()
    iWG = nc.dram_tensor("WGT", [H, NE], F32, kind="ExternalInput").ap()
    iBG = nc.dram_tensor("BG", [1, NE], F32, kind="ExternalInput").ap()
    iEYE = nc.dram_tensor("EYE", [128, 128], F32, kind="ExternalInput").ap()
    iSEL = nc.dram_tensor("SEL", [NE, NE * 128], BF16, kind="ExternalInput").ap()
    oT = nc.dram_tensor("outT", [H, T], F32, kind="ExternalOutput").ap()
    if DBG:
        dbg_h0 = nc.dram_tensor("dbg_h0", [128, HT * TC], BF16,
                                kind="ExternalOutput").ap()
        dbg_h1 = nc.dram_tensor("dbg_h1", [128, HT * TC], BF16,
                                kind="ExternalOutput").ap()
        dbg_mrb = nc.dram_tensor("dbg_mrb", [128, 2 * TC], BF16,
                                 kind="ExternalOutput").ap()
        dbg_rows = nc.dram_tensor("dbg_rows", [1, 3 * TC], F32,
                                  kind="ExternalOutput").ap()
        dbg_x1 = nc.dram_tensor("dbg_x1", [128, HT * TC], F32,
                                kind="ExternalOutput").ap()
        dbg_x1b = nc.dram_tensor("dbg_x1b", [128, HT * TC], BF16,
                                 kind="ExternalOutput").ap()
        dbg_wq = nc.dram_tensor("dbg_wq", [128, 4 * NE], F32,
                                kind="ExternalOutput").ap()
        dbg_swtb = nc.dram_tensor("dbg_swtb", [NE, TC], BF16,
                                  kind="ExternalOutput").ap()
        dbg_wbb = nc.dram_tensor("dbg_wbb", [128, NE * TC], BF16,
                                 kind="ExternalOutput").ap()
        dbg_y = nc.dram_tensor("dbg_y", [128, HT * TC], BF16,
                               kind="ExternalOutput").ap()
        dbg_m2c = nc.dram_tensor("dbg_m2c", [128, 8], F32,
                                 kind="ExternalOutput").ap()
        dbg_csb = nc.dram_tensor("dbg_csb", [128, NE], F32,
                                 kind="ExternalOutput").ap()
        dbg_bgb = nc.dram_tensor("dbg_bgb", [128, NE], F32,
                                 kind="ExternalOutput").ap()
        dbg_wgt = nc.dram_tensor("dbg_wgt", [128, HT * NE], F32,
                                 kind="ExternalOutput").ap()
        dbg_stg = nc.dram_tensor("dbg_stg", [128, 70], F32,
                                 kind="ExternalOutput").ap()
    sW0 = nc.dram_tensor("scrW0", [8, 128, 3 * H], BF16, kind="Internal").ap()
    sW1 = nc.dram_tensor("scrW1", [8, 128, 3 * H], BF16, kind="Internal").ap()
    sWT = nc.dram_tensor("scrWT", [NE, 128, NE * H], BF16, kind="Internal").ap()

    es = ExitStack()
    SB = lambda n, s, d: es.enter_context(nc.sbuf_tensor(n, s, d))
    PS = lambda n, s: es.enter_context(nc.psum_tensor(n, s, F32))
    SEM = lambda n: es.enter_context(nc.semaphore(n))

    wpool = [SB(f"wp{i}", [128, 3 * H], BF16) for i in range(NW)]
    wtpool = [SB(f"wt{i}", [128, NE * H], BF16) for i in range(NWT)]
    xbuf = [SB(f"x{i}", [128, HT * TC], F32) for i in range(2)]
    h0b = SB("h0", [128, HT * TC], BF16)
    h0b2 = SB("h0x", [128, HT * TC], BF16)
    h1b = SB("h1", [128, HT * TC], BF16)
    xeb = [SB(f"xe{i}", [128, HT * TC], BF16) for i in range(2)]
    wbb = SB("wb", [128, NE * TC], BF16)
    sgb = [SB(f"sg{i}", [128, TC], F32) for i in range(2)]
    ab = [SB(f"a{i}", [128, TC], F32) for i in range(2)]
    shb = [SB(f"sh{i}", [128, TC], F32) for i in range(2)]
    gpb = [SB(f"gp{i}", [128, TC], F32) for i in range(2)]
    gtb = SB("gt", [128, TC], F32)
    vb = SB("v", [128, TC], F32)
    Hb = SB("Hs", [128, TC], F32)
    db = SB("d", [128, TC], F32)
    h2b = SB("h2", [128, TC], F32)
    t2t = SB("t2t", [128, TC], BF16)
    xsq = [SB(f"xsq{i}", [128, TC], BF16) for i in range(2)]
    rows = SB("rows", [1, 3 * TC], F32)
    rows2 = SB("rows2", [1, 3 * TC], F32)
    mrb = SB("mrb", [128, 2 * TC], BF16)
    mrb2 = SB("mrb2", [128, 2 * TC], BF16)
    carry1 = SB("c1s", [128, HT], F32)
    carry2 = SB("c2s", [128, HT], F32)
    sid = SB("sid", [128, 128], F32)
    ssel = SB("ssel", [NE, NE * 128], BF16)
    swgt = SB("swgt", [128, HT * NE], F32)
    swgtb = SB("swgtb", [128, HT * NE], BF16)
    ltg = SB("ltg", [128, TC], F32)
    sbg = SB("sbg", [1, NE], F32)
    bgbb = SB("bgb", [128, 4 * NE], F32)
    csbb = SB("csb", [128, 4 * NE], F32)
    csrow = SB("csr", [1, NE], F32)
    cgrow = SB("cgrow", [1, 8 * NE], F32)
    bfb = SB("bfb", [NE, H], BF16)
    swTb = SB("swTb", [128, TC], BF16)
    gate = SB("gate", [128, 16 * NE], F32)
    rows4 = SB("rows4", [128, 20], F32)
    wq = SB("wq", [128, 4 * NE], F32)
    m2c = SB("m2c", [128, 8], F32)
    onec = SB("onec", [128, 1], F32)
    onecb = SB("onecb", [128, 1], BF16)
    oner = SB("oner", [1, 128], F32)
    epsc = SB("epsc", [1, 1], F32)
    ssink = SB("ssink", [128, 1], F32)
    dbgstg = SB("dbgstg", [128, 70], F32) if DBG else None

    p = [PS(f"p{i}", [128, TC]) for i in range(8)]

    semnames = (["sP", "sV", "sA", "sG", "dC", "dB", "dSt0", "dStT", "dSt1"]
                + [f"dX{i}" for i in range(2)]
                + [f"dWp{i}" for i in range(NW)]
                + [f"dWT{i}" for i in range(NWT)]
                + [f"dF{i}" for i in range(2)]
                + [f"dG{i}" for i in range(2)])
    sems = {k: SEM(k) for k in semnames}

    OPS = {k: [] for k in ("sync", "tensor", "vector", "scalar", "gpsimd")}
    CNT = {k: 0 for k in sems}
    LASTW = {}

    def W(st, sem, val):
        if not val:
            return
        key = (st, sem)
        if LASTW.get(key, 0) >= val:
            return
        LASTW[key] = val
        OPS[st].append(lambda eng, s=sems[sem], v=val: eng.wait_ge(s, v))

    def I(st, fn, sem=None, n=1):
        if sem is None:
            OPS[st].append(fn)
            return None
        CNT[sem] += n
        v = CNT[sem]
        OPS[st].append(lambda eng, f=fn, s=sems[sem], k=n: f(eng).then_inc(s, k))
        return v

    SEM_OF = {"vector": "sV", "scalar": "sA", "tensor": "sP"}

    def VCH(fn):
        v = I("vector", fn, "sV")
        W("vector", "sV", v)
        return v

    def VSYNC():
        W("vector", "sV", CNT["sV"])

    xv = [None] * NCH

    def x_load(c):
        dst = xbuf[c % 2][:].rearrange("p (ht t) -> p ht t", ht=HT)
        src = ixT.rearrange("(ht p) t -> p ht t", p=128)[:, :, c * TC:(c + 1) * TC]
        sm = f"dX{c % 2}"
        return (sm, I("sync", lambda eng, d=dst, s=src: eng.dma_start(d, s),
                      sm, 16))

    xv[0] = x_load(0)
    xv[1] = x_load(1)
    I("sync", lambda eng: eng.dma_start(sid[:], iEYE), "dC", 16)
    I("sync", lambda eng: eng.dma_start(ssel[:], iSEL), "dC", 16)
    I("sync", lambda eng: eng.dma_start(
        swgt[:].rearrange("p (ht e) -> p ht e", ht=HT),
        iWG.rearrange("(ht p) e -> p ht e", p=128)), "dC", 16)
    consts_v = I("sync", lambda eng: eng.dma_start(sbg[:], iBG), "dC", 16)
    bfv = I("gpsimd", lambda eng: eng.dma_start(bfb[:], iBF), "dB", 16)

    I("vector", lambda eng: eng.memset(onec[:], 1.0))
    I("vector", lambda eng: eng.memset(onecb[:], 1.0))
    I("vector", lambda eng: eng.memset(oner[:], 1.0))
    W("vector", "dC", consts_v)
    I("vector", lambda eng: eng.tensor_copy(swgtb[:], swgt[:]))
    bfcast = I("vector", lambda eng: eng.memset(epsc[:], 1e-5), "sV")

    W("tensor", "dC", consts_v)
    W("tensor", "sV", bfcast)
    if not PROBE_SKIP_GATE:
        L = PROBE_GATE_LVL
        if L >= 1:
            for ht in range(HT):
                I("tensor", lambda eng, h=ht: eng.matmul(
                    p[0][0:1, 0:NE], onec[:], swgt[:, h * NE:(h + 1) * NE],
                    start=(h == 0), stop=(h == HT - 1)))
        if L >= 2:
            pv = I("tensor", lambda eng: eng.matmul(
                p[1][:, 0:NE], oner[:], sbg[:], start=True, stop=True), "sP")
        if L >= 3:
            W("vector", "sP", pv)
            VCH(lambda eng: eng.tensor_copy(csrow[:], p[0][0:1, 0:NE]))
            csv = I("vector", lambda eng: eng.tensor_copy(
                ssink[0:1, :], csrow[0:1, 0:1]), "sV")
        if L >= 4:
            W("vector", "sP", pv)
            for i in range(4):
                VCH(lambda eng, i_=i: eng.tensor_copy(
                    cgrow[0:1, i_ * NE:(i_ + 1) * NE], csrow[:]))
                VCH(lambda eng, i_=i: eng.tensor_copy(
                    cgrow[0:1, 32 + i_ * NE:32 + (i_ + 1) * NE],
                    p[1][0:1, 0:NE]))
            csv2 = I("vector", lambda eng: eng.tensor_copy(
                ssink[0:1, :], cgrow[0:1, 0:1]), "sV")
            W("tensor", "sV", csv2)
            pv2 = I("tensor", lambda eng: eng.matmul(
                p[0][:, 0:8 * NE], oner[:], cgrow[:],
                start=True, stop=True), "sP")
        if L >= 5:
            W("vector", "sP", pv2)
            I("vector", lambda eng: eng.tensor_copy(csbb[:], p[0][:, 0:32]))
            I("vector", lambda eng: eng.tensor_copy(bgbb[:], p[0][:, 32:64]),
              "sV")

    def _tail():
        if PROBE_TAIL:
            for st in OPS:
                for _ in range(100):
                    OPS[st].append(lambda eng: eng.nop(cycle_cnt=10000))

    def _chk0(tag):
        if STOP is not None and (-1, tag) >= STOP:
            _tail()
            raise _StopEmit

    for jg in range(8):
        W("gpsimd", "dSt0", 16 * jg)
        I("gpsimd", lambda eng, j=jg: eng.dma_start(
            sW0[j].rearrange("p (ki kj ht c) -> p ki kj ht c",
                             ki=3, kj=1, c=128),
            iW0.rearrange("(ht p) (ki kj c) -> p ki kj ht c",
                          p=128, ki=3, c=128)[:, :, j:j + 1, :, :]),
          "dSt0", 16)
    for jg in range(8):
        W("gpsimd", "dSt0", 128)
        W("gpsimd", "dSt1", 16 * jg)
        I("gpsimd", lambda eng, j=jg: eng.dma_start(
            sW1[j].rearrange("p (ki kj ht c) -> p ki kj ht c",
                             ki=3, kj=1, c=128),
            iW1.rearrange("(ht p) (ki kj c) -> p ki kj ht c",
                          p=128, ki=3, c=128)[:, :, j:j + 1, :, :]),
          "dSt1", 16)
    for eg in range(NE):
        W("gpsimd", "dSt1", 128)
        W("gpsimd", "dStT", 16 * eg)
        I("gpsimd", lambda eng, e_=eg: eng.dma_start(
            sWT[e_].rearrange("p (ot dt c) -> p ot dt c", ot=8, c=128),
            iWT[e_].rearrange("(dt p) (ot c) -> p ot dt c", p=128, c=128)),
          "dStT", 16)

    NG01 = NCH * 16
    w01_load = [None] * NG01
    w01_cons = [None] * NG01

    def w01_issue(gidx):
        if gidx >= NG01:
            return
        ps, j = divmod(gidx % 16, 8)
        W("sync", "dSt1" if ps else "dSt0", 16 * (j + 1))
        if gidx >= NW:
            W("sync", "sP", w01_cons[gidx - NW])
        sm = f"dWp{gidx % NW}"
        scr = sW1 if ps else sW0
        w01_load[gidx] = (sm, I("sync", lambda eng, s=scr, j_=j,
                                sl=wpool[gidx % NW]:
                                eng.dma_start(sl[:], s[j_]), sm, 16))

    NGWT = NCH * NE
    wt_load = [None] * NGWT
    wt_cons = [None] * NGWT

    def wt_issue(gidx):
        if gidx >= NGWT:
            return
        e = gidx % NE
        W("scalar", "dStT", 16 * (e + 1))
        if gidx >= NWT:
            W("scalar", "sP", wt_cons[gidx - NWT])
        sm = f"dWT{gidx % NWT}"
        wt_load[gidx] = (sm, I("scalar", lambda eng, e_=e,
                               sl=wtpool[gidx % NWT]:
                               eng.dma_start(sl[:], sWT[e_]), sm, 16))

    for g in range(NW):
        w01_issue(g)

    bank_free = [0] * 8
    ln_bcast_done = [0]
    ssq_mm_prev = [None] * HT
    st_hist = []
    outdma = [0]
    h0bufs = [h0b, h0b2]
    h0d_cur = [None] * HT
    scan_prev = [0]
    g_prev = [0]
    tail_prev = [0]
    try:

        def ln_stats(c, second, vj2, from_buf, rr):
            xcur = xbuf[c % 2]
            W("tensor", "sV", bank_free[6])
            W("tensor", "sV", bank_free[7])
            if not second:
                W("tensor", *xv[c])
            for ht in range(HT):
                if second:
                    W("tensor", "sV", vj2[ht])
                    I("tensor", lambda eng, h=ht, b_=from_buf: eng.matmul(
                        p[6][0:1, :], onecb[:], b_[:, h * TC:(h + 1) * TC],
                        start=(h == 0), stop=(h == HT - 1)))
                else:
                    I("tensor", lambda eng, h=ht, x_=xcur: eng.matmul(
                        p[6][0:1, :], onec[:], x_[:, h * TC:(h + 1) * TC],
                        start=(h == 0), stop=(h == HT - 1)))
            ssq_cur = [None] * HT
            for ht in range(HT):
                if second:
                    W("scalar", "sV", vj2[ht])
                else:
                    W("scalar", *xv[c])
                W("scalar", "sP", ssq_cur[ht - 2] if ht >= 2 else ssq_mm_prev[ht])
                if second:
                    sq = I("scalar", lambda eng, h=ht, b_=from_buf:
                           eng.activation(
                        xsq[h % 2][:], b_[:, h * TC:(h + 1) * TC], AF.Square),
                        "sA")
                else:
                    sq = I("scalar", lambda eng, h=ht, x_=xcur: eng.activation(
                        xsq[h % 2][:], x_[:, h * TC:(h + 1) * TC], AF.Square),
                        "sA")
                W("tensor", "sA", sq)
                ssq_cur[ht] = I("tensor", lambda eng, h=ht: eng.matmul(
                    p[7][0:1, :], onecb[:], xsq[h % 2][:],
                    start=(h == 0), stop=(h == HT - 1)), "sP")
            for ht in range(HT):
                ssq_mm_prev[ht] = ssq_cur[ht]
            stats_done = ssq_cur[HT - 1]
            mean = rr[0:1, 0:TC]
            var = rr[0:1, TC:2 * TC]
            rstd = rr[0:1, 2 * TC:3 * TC]
            W("vector", "sP", stats_done)
            W("vector", "sP", ln_bcast_done[0])
            I("vector", lambda eng: eng.tensor_scalar(
                mean, p[6][0:1, :], 1.0 / H, None, op0=ALU.mult))
            I("vector", lambda eng: eng.tensor_mul(var, mean, mean))
            I("vector", lambda eng: eng.tensor_scalar(
                rstd, p[7][0:1, :], 1.0 / H, None, op0=ALU.mult))
            var_done = I("vector", lambda eng: eng.tensor_sub(var, rstd, var), "sV")
            W("scalar", "sV", var_done)
            I("scalar", lambda eng: eng.activation(rstd, var, AF.Ln,
                                                   bias=epsc[0:1, :]))
            rstd_done = I("scalar", lambda eng: eng.activation(
                rstd, rstd, AF.Exp, scale=-0.5), "sA")
            return rstd_done, var_done

        def ln_bcast(rstd_done, rr, guard_sv=0):
            mean = rr[0:1, 0:TC]
            rstd = rr[0:1, 2 * TC:3 * TC]
            W("tensor", "sA", rstd_done)
            W("tensor", "sV", guard_sv)
            I("tensor", lambda eng: eng.matmul(
                p[6][:], oner[:], mean, start=True, stop=True))
            ln_bcast_done[0] = I("tensor", lambda eng: eng.matmul(
                p[7][:], oner[:], rstd, start=True, stop=True), "sP")

        def ln1_copy():
            W("vector", "sP", ln_bcast_done[0])
            I("vector", lambda eng: eng.tensor_copy(mrb2[:, 0:TC], p[6][:]))
            bf67 = I("vector", lambda eng: eng.tensor_copy(
                mrb2[:, TC:2 * TC], p[7][:]), "sV")
            bank_free[6] = bank_free[7] = bf67

        def h0_build(c, hts):
            hbn = h0bufs[c % 2]
            xcur = xbuf[c % 2]
            W("vector", *xv[c])
            for ht in hts:
                I("vector", lambda eng, h=ht, x_=xcur: eng.tensor_sub(
                    t2t[:], x_[:, h * TC:(h + 1) * TC], mrb2[:, 0:TC]))
                h0d_cur[ht] = I("vector", lambda eng, h=ht, hb_=hbn:
                                eng.tensor_mul(
                    hb_[:, h * TC:(h + 1) * TC], t2t[:],
                    mrb2[:, TC:2 * TC]), "sV")

        def ln2(c, vj2, hb, emit_next):
            rstd_done, var_done = ln_stats(c, True, vj2, hb, rows)
            bank_free[6] = bank_free[7] = var_done
            var_next = emit_next()
            mean = rows[0:1, 0:TC]
            rstd = rows[0:1, 2 * TC:3 * TC]
            W("tensor", "sV", bank_free[5])
            W("tensor", "sA", rstd_done)
            for i in range(4):
                I("tensor", lambda eng, i_=i: eng.transpose(
                    p[5][:, i_:i_ + 1], mean[:, i_ * 128:(i_ + 1) * 128],
                    sid[0:1, 0:1]))
            tcol_done = None
            for i in range(4):
                tcol_done = I("tensor", lambda eng, i_=i: eng.transpose(
                    p[5][:, 4 + i_:5 + i_], rstd[:, i_ * 128:(i_ + 1) * 128],
                    sid[0:1, 0:1]), "sP" if i == 3 else None)
            W("vector", "sP", tcol_done)
            bank_free[5] = VCH(lambda eng: eng.tensor_copy(
                m2c[:], p[5][:, 0:8]))
            ln_bcast(rstd_done, rows, guard_sv=var_next)
            W("vector", "sP", ln_bcast_done[0])
            I("vector", lambda eng: eng.tensor_copy(mrb[:, 0:TC], p[6][:]))
            bf67 = I("vector", lambda eng: eng.tensor_copy(
                mrb[:, TC:2 * TC], p[7][:]), "sV")
            bank_free[6] = bank_free[7] = bf67

        def _chk(c, tag):
            if STOP is not None and (c, tag) >= STOP:
                raise _StopEmit

        rd0, _ = ln_stats(0, False, None, None, rows2)
        ln_bcast(rd0, rows2)
        ln1_copy()
        h0_build(0, range(HT))

        for c in range(NCH):
            xcur = xbuf[c % 2]
            hb = h0bufs[c % 2]
            vj1 = [None] * HT
            vj2 = [None] * HT
            dep1 = list(h0d_cur)

            def mixer_pass(ps, rhs_buf, carry, vj_out, dep_list):
                for j in range(HT):
                    gidx = c * 16 + ps * 8 + j
                    bs = 3 * (j % 2)
                    W("tensor", *w01_load[gidx])
                    if j >= 2:
                        W("tensor", "sV", vj_out[j - 2])
                    else:
                        for b in (bs, bs + 1, bs + 2):
                            W("tensor", "sV", bank_free[b])
                    if j > 0:
                        W("tensor", "sV", dep_list[HT - 1])
                    z_drain = None
                    for ht in range(HT):
                        if j == 0:
                            W("tensor", "sV", dep_list[ht])
                        for ki in range(3):
                            z_drain = I("tensor", lambda eng, k=ki, h=ht,
                                        g=gidx, b=bs, r=rhs_buf: eng.matmul(
                                p[b + k][:],
                                wpool[g % NW][:, (k * 8 + h) * 128:(k * 8 + h + 1) * 128],
                                r[:, h * TC:(h + 1) * TC],
                                start=(h == 0), stop=(h == HT - 1)),
                                "sP" if (ki == 2 and ht == HT - 1) else None)
                    w01_cons[gidx] = z_drain
                    w01_issue(gidx + NW)
                    par = j % 2
                    if j >= 2:
                        W("scalar", "sV", vj_out[j - 2])
                    W("scalar", "sP", z_drain)
                    I("scalar", lambda eng, pa=par, b=bs: eng.activation(
                        shb[pa][:], p[b][:], AF.Sigmoid))
                    I("scalar", lambda eng, pa=par, b=bs: eng.activation(
                        sgb[pa][:], p[b + 1][:], AF.Sigmoid))
                    I("scalar", lambda eng, pa=par, b=bs: eng.activation(
                        ab[pa][:], p[b + 1][:], AF.Sigmoid, scale=-1.0))
                    s_done = I("scalar", lambda eng, pa=par, b=bs:
                               eng.activation(
                        gpb[pa][:], p[b + 2][:], AF.Sigmoid), "sA")
                    W("vector", "sA", s_done)
                    I("vector", lambda eng, pa=par, b=bs:
                      eng.scalar_tensor_tensor(
                        gtb[:], p[b][:], 0.5, shb[pa][:], op0=ALU.add,
                        op1=ALU.max))
                    I("vector", lambda eng, pa=par: eng.tensor_mul(
                        vb[:], sgb[pa][:], gtb[:]))
                    W("vector", "sG", g_prev[0])
                    if c == 0:
                        scan_prev[0] = I("vector", lambda eng, pa=par:
                                         eng.tensor_tensor_scan(
                            Hb[:], ab[pa][:], vb[:], 0.0,
                            op0=ALU.mult, op1=ALU.add), "sV")
                    else:
                        scan_prev[0] = I("vector", lambda eng, pa=par, jj=j,
                                         ca=carry: eng.tensor_tensor_scan(
                            Hb[:], ab[pa][:], vb[:], ca[:, jj:jj + 1],
                            op0=ALU.mult, op1=ALU.add), "sV")
                    I("vector", lambda eng, jj=j, ca=carry: eng.tensor_copy(
                        ca[:, jj:jj + 1], Hb[:, TC - 1:TC]))
                    W("gpsimd", "sA", s_done)
                    W("gpsimd", "sV", scan_prev[0])
                    W("gpsimd", "sV", tail_prev[0])
                    if ps == 0:
                        I("gpsimd", lambda eng, jj=j, hb_=hb: eng.tensor_sub(
                            db[:], Hb[:], hb_[:, jj * TC:(jj + 1) * TC]))
                        g_prev[0] = I("gpsimd", lambda eng, pa=par:
                                      eng.tensor_mul(
                            db[:], gpb[pa][:], db[:]), "sG")
                        W("vector", "sG", g_prev[0])
                        vj_out[j] = I("vector", lambda eng, jj=j, hb_=hb:
                                      eng.tensor_add(
                            h1b[:, jj * TC:(jj + 1) * TC], db[:],
                            hb_[:, jj * TC:(jj + 1) * TC]), "sV")
                    else:
                        I("gpsimd", lambda eng, jj=j: eng.tensor_sub(
                            db[:], Hb[:], h1b[:, jj * TC:(jj + 1) * TC]))
                        I("gpsimd", lambda eng, pa=par: eng.tensor_mul(
                            db[:], gpb[pa][:], db[:]))
                        I("gpsimd", lambda eng, jj=j: eng.tensor_add(
                            h2b[:], db[:], h1b[:, jj * TC:(jj + 1) * TC]))
                        g_prev[0] = I("gpsimd", lambda eng, jj=j, x_=xcur:
                                      eng.tensor_add(
                            x_[:, jj * TC:(jj + 1) * TC],
                            x_[:, jj * TC:(jj + 1) * TC], h2b[:]), "sG")
                        W("vector", "sG", g_prev[0])
                        vj_out[j] = I("vector", lambda eng, jj=j, x_=xcur,
                                      hb_=hb: eng.tensor_copy(
                            hb_[:, jj * TC:(jj + 1) * TC],
                            x_[:, jj * TC:(jj + 1) * TC]), "sV")
                    tail_prev[0] = vj_out[j]
                    for k in range(3):
                        bank_free[bs + k] = vj_out[j]

            _chk(c, "a_ln1")
            mixer_pass(0, hb, carry1, vj1, dep1)
            _chk(c, "b_pass1")
            if DBG and c == 0:
                W("gpsimd", "sV", vj1[HT - 1])
                I("gpsimd", lambda eng: eng.dma_start(dbg_h0, h0b[:]), "dG0", 16)
                I("gpsimd", lambda eng: eng.dma_start(dbg_h1, h1b[:]), "dG0", 16)
                I("gpsimd", lambda eng: eng.dma_start(dbg_mrb, mrb[:]), "dG0", 16)
                I("gpsimd", lambda eng: eng.dma_start(dbg_csb, csbb[:]), "dG0", 16)
                I("gpsimd", lambda eng: eng.dma_start(dbg_bgb, bgbb[:]), "dG0", 16)
                I("gpsimd", lambda eng: eng.dma_start(dbg_wgt, swgt[:]), "dG0", 16)
                dbgv = I("gpsimd", lambda eng: eng.dma_start(dbg_rows, rows[:]),
                         "dG0", 16)
                W("vector", "dG0", dbgv)
                W("tensor", "dG0", dbgv)
            mixer_pass(1, h1b, carry2, vj2, vj1)
            if DBG and c == 0:
                W("gpsimd", "sV", vj2[HT - 1])
                I("gpsimd", lambda eng: eng.dma_start(dbg_x1, xbuf[0][:]),
                  "dG0", 16)
                dbgv2 = I("gpsimd", lambda eng: eng.dma_start(
                    dbg_x1b, h0b[:]), "dG0", 16)
                W("vector", "dG0", dbgv2)
                W("tensor", "dG0", dbgv2)
                W("scalar", "dG0", dbgv2)
            _chk(c, "c_pass2")

            nxt = {}

            def emit_next():
                if c + 1 < NCH:
                    nxt["rd"], vd = ln_stats(c + 1, False, None, None, rows2)
                    return vd
                return 0
            ln2(c, vj2, hb, emit_next)
            rd_next = nxt.get("rd")
            _chk(c, "d_ln2")

            W("tensor", "sV", bank_free[0])
            W("tensor", "sV", bank_free[1])
            lgT_done = None
            for ht in range(HT):
                lgT_done = I("tensor", lambda eng, h=ht, hb_=hb: eng.matmul(
                    p[0][0:NE, :], swgtb[:, h * NE:(h + 1) * NE],
                    hb_[:, h * TC:(h + 1) * TC],
                    start=(h == 0), stop=(h == HT - 1)),
                    "sP" if ht == HT - 1 else None)
            W("vector", "sP", lgT_done)
            ltg_ev = I("vector", lambda eng: eng.tensor_copy(
                ltg[:], p[0][:]), "sV")
            bank_free[0] = ltg_ev
            W("tensor", "sV", ltg_ev)
            glt_done = None
            for tt in range(4):
                glt_done = I("tensor", lambda eng, t_=tt: eng.transpose(
                    p[1][:, t_ * NE:(t_ + 1) * NE],
                    ltg[0:NE, t_ * 128:(t_ + 1) * 128], sid[0:NE, 0:NE]),
                    "sP" if tt == 3 else None)
            LG = gate[:, 0:32]
            EB = gate[:, 32:64]
            TMP = gate[:, 64:96]
            WU = gate[:, 96:128]
            V3 = lambda t: t.rearrange("p (t e) -> p t e", t=4)
            B4 = lambda c: c.rearrange("p (t o) -> p t o", o=1).to_broadcast(
                [128, 4, NE])
            m4 = m2c[:, 0:4]
            r4 = m2c[:, 4:8]
            W("vector", "sP", glt_done)
            VSYNC()
            VCH(lambda eng: eng.tensor_tensor(
                V3(TMP), V3(csbb[:]), B4(m4), op=ALU.mult))
            VCH(lambda eng: eng.tensor_sub(LG, p[1][:, 0:32], TMP))
            VCH(lambda eng: eng.tensor_tensor(
                V3(LG), V3(LG), B4(r4), op=ALU.mult))
            VCH(lambda eng: eng.tensor_add(LG, LG, bgbb[:]))
            VCH(lambda eng: eng.reduce_max(
                rows4[:, 0:4].rearrange("p (t o) -> p t o", o=1),
                V3(LG), axis=AX, negate=True))
            VCH(lambda eng: eng.tensor_tensor(
                V3(LG), V3(LG), B4(rows4[:, 0:4]), op=ALU.add))
            lgv = I("vector", lambda eng: eng.tensor_copy(
                ssink[:], rows4[:, 0:1]), "sV")
            W("scalar", "sV", lgv)
            I("scalar", lambda eng: eng.activation(EB, LG, AF.Exp))
            gexpv = I("scalar", lambda eng: eng.copy(
                epsc[0:1, :], epsc[0:1, :]), "sA")
            W("vector", "sA", gexpv)
            VCH(lambda eng: eng.reduce_max(
                rows4[:, 12:16].rearrange("p (t o) -> p t o", o=1),
                V3(EB), axis=AX))
            VCH(lambda eng: eng.reduce_sum(
                rows4[:, 4:8].rearrange("p (t o) -> p t o", o=1),
                V3(EB), axis=AX))
            VCH(lambda eng: eng.reciprocal(rows4[:, 8:12], rows4[:, 4:8]))
            VCH(lambda eng: eng.tensor_tensor(
                V3(TMP), V3(EB), B4(rows4[:, 12:16]), op=ALU.is_lt))
            VCH(lambda eng: eng.tensor_mul(TMP, EB, TMP))
            VCH(lambda eng: eng.reduce_max(
                rows4[:, 16:20].rearrange("p (t o) -> p t o", o=1),
                V3(TMP), axis=AX))
            VCH(lambda eng: eng.tensor_tensor(
                V3(TMP), V3(EB), B4(rows4[:, 16:20]), op=ALU.is_ge))
            VCH(lambda eng: eng.tensor_mul(WU, EB, TMP))
            VCH(lambda eng: eng.tensor_tensor(
                V3(wq[:]), V3(WU), B4(rows4[:, 8:12]), op=ALU.mult))
            gvq = I("vector", lambda eng: eng.tensor_copy(
                ssink[:], wq[:, 0:1]), "sV")
            bank_free[1] = gvq

            W("tensor", "sV", bank_free[1])
            wTp_done = None
            W("tensor", "sV", gvq)
            for tt in range(4):
                wTp_done = I("tensor", lambda eng, t_=tt: eng.transpose(
                    p[1][0:NE, t_ * 128:(t_ + 1) * 128],
                    wq[:, t_ * NE:(t_ + 1) * NE], sid[:]),
                    "sP" if tt == 3 else None)
            W("vector", "sP", wTp_done)
            swtb_done = I("vector", lambda eng: eng.tensor_copy(
                swTb[:], p[1][:]), "sV")
            bank_free[1] = swtb_done

            wb_ev = [None] * NE
            for e in range(NE):
                bk = 2 + e % 2
                W("tensor", "sV", bank_free[bk])
                W("tensor", "sV", swtb_done)
                wbv = I("tensor", lambda eng, e_=e, b=bk: eng.matmul(
                    p[b][:], ssel[:, e_ * 128:(e_ + 1) * 128], swTb[0:NE, :],
                    start=True, stop=True), "sP")
                W("vector", "sP", wbv)
                wb_ev[e] = I("vector", lambda eng, e_=e, b=bk: eng.tensor_copy(
                    wbb[:, e_ * TC:(e_ + 1) * TC], p[b][:]), "sV")
                bank_free[bk] = wb_ev[e]

            if rd_next is not None:
                ln_bcast(rd_next, rows2, guard_sv=bank_free[6])

            y_done = None
            for dt in range(HT):
                I("vector", lambda eng, d_=dt, hb_=hb: eng.tensor_sub(
                    t2t[:], hb_[:, d_ * TC:(d_ + 1) * TC], mrb[:, 0:TC]))
                y_done = I("vector", lambda eng, d_=dt: eng.tensor_mul(
                    h1b[:, d_ * TC:(d_ + 1) * TC], t2t[:], mrb[:, TC:2 * TC]), "sV")
            if rd_next is not None:
                ln1_copy()

            xe_done = [None] * NE

            def build_xe(e):
                for dt in range(HT):
                    v = I("vector", lambda eng, e_=e, d_=dt: eng.tensor_mul(
                        xeb[e_ % 2][:, d_ * TC:(d_ + 1) * TC],
                        h1b[:, d_ * TC:(d_ + 1) * TC],
                        wbb[:, e_ * TC:(e_ + 1) * TC]), "sV")
                xe_done[e] = v

            if DBG and c == 0:
                W("gpsimd", "sV", wb_ev[NE - 1])
                W("gpsimd", "sV", y_done)
                I("gpsimd", lambda eng: eng.dma_start(dbg_wq, wq[:]), "dG0", 16)
                I("gpsimd", lambda eng: eng.dma_start(dbg_swtb, swTb[:]),
                  "dG0", 16)
                I("gpsimd", lambda eng: eng.dma_start(dbg_wbb, wbb[:]), "dG0", 16)
                I("gpsimd", lambda eng: eng.dma_start(dbg_y, h1b[:]), "dG0", 16)
                I("gpsimd", lambda eng: eng.dma_start(dbg_stg, dbgstg[:]),
                  "dG0", 16)
                dbgv3 = I("gpsimd", lambda eng: eng.dma_start(
                    dbg_m2c, m2c[:]), "dG0", 16)
                W("vector", "dG0", dbgv3)
                W("tensor", "dG0", dbgv3)
            build_xe(0)
            _chk(c, "f_prep")

            if c == 0:
                for g in range(NWT):
                    wt_issue(g)
            for b in range(8):
                W("tensor", "sV", bank_free[b])
            W("tensor", "sV", wb_ev[NE - 1])
            W("tensor", "dB", bfv)
            for ot in range(8):
                I("tensor", lambda eng, o=ot: eng.matmul(
                    p[o][:], bfb[:, o * 128:(o + 1) * 128], swTb[0:NE, :],
                    start=True, stop=False))
            ffn_e = [None] * NE
            for e in range(NE):
                gwt = c * NE + e
                W("tensor", *wt_load[gwt])
                W("tensor", "sV", xe_done[e])
                for ot in range(8):
                    for dt in range(HT):
                        ffn_e[e] = I("tensor", lambda eng, e_=e, o=ot, d_=dt,
                                     g=gwt: eng.matmul(
                            p[o][:],
                            wtpool[g % NWT][:, (o * 8 + d_) * 128:(o * 8 + d_ + 1) * 128],
                            xeb[e_ % 2][:, d_ * TC:(d_ + 1) * TC],
                            start=False, stop=(e_ == NE - 1 and d_ == HT - 1)),
                            "sP" if (ot == 7 and dt == HT - 1) else None)
                wt_cons[gwt] = ffn_e[e]
                wt_issue(gwt + NWT)
                if e + 1 < NE:
                    if e >= 1:
                        W("vector", "sP", ffn_e[e - 1])
                    build_xe(e + 1)
                    if rd_next is not None and e < 4:
                        h0_build(c + 1, [2 * e, 2 * e + 1])

            for ot in (6, 7, 0, 1, 2, 3, 4, 5):
                W("vector", "sP", ffn_e[NE - 1])
                ev = I("vector", lambda eng, o=ot, x_=xcur: eng.tensor_add(
                    x_[:, o * TC:(o + 1) * TC], p[o][:],
                    x_[:, o * TC:(o + 1) * TC]), "sV")
                bank_free[ot] = ev
                W("sync", "sV", ev)
                dst = oT.rearrange("(ot p) t -> p ot t", p=128)[
                    :, ot:ot + 1, c * TC:(c + 1) * TC]
                sm = f"dG{ot % 2}"
                st = (sm, I("sync", lambda eng, d=dst, o=ot, x_=xcur:
                            eng.dma_start(
                    d, x_[:, o * TC:(o + 1) * TC].rearrange(
                        "p (a t) -> p a t", a=1)), sm, 16))
                st_hist.append(st)
                outdma[0] += 1

            _chk(c, "g_chunk")
            if c + 2 < NCH:
                W("sync", "sV", bank_free[5])
                W("sync", *st_hist[-1])
                W("sync", *st_hist[-2])
                xv[c + 2] = x_load(c + 2)

    except _StopEmit:
        pass

    W("sync", "dG0", CNT["dG0"])
    W("sync", "dG1", CNT["dG1"])

    with nc.Block() as blk:
        @blk.sync
        def _(eng):
            for f in OPS["sync"]:
                f(eng)

        @blk.tensor
        def _(eng):
            for f in OPS["tensor"]:
                f(eng)

        @blk.vector
        def _(eng):
            for f in OPS["vector"]:
                f(eng)

        @blk.scalar
        def _(eng):
            for f in OPS["scalar"]:
                f(eng)

        @blk.gpsimd
        def _(eng):
            for f in OPS["gpsimd"]:
                f(eng)

    es.close()
    return nc


def kernel(L0_attn_h, L0_attn_W0, L0_attn_W1, L0_ffn_W, L0_ffn_b,
           L0_ffn_Wg, L0_ffn_bg):
    if "nc" not in _CACHE:
        _CACHE["nc"] = build()
    nc = _CACHE["nc"]

    f32 = lambda a: np.ascontiguousarray(np.asarray(a, dtype=np.float32))
    xh = f32(L0_attn_h)
    W0 = f32(L0_attn_W0)
    W1 = f32(L0_attn_W1)
    WT = np.ascontiguousarray(f32(L0_ffn_W).transpose(0, 2, 1))
    BF = f32(L0_ffn_b)
    WG = np.ascontiguousarray(f32(L0_ffn_Wg).T)
    BG = f32(L0_ffn_bg).reshape(1, NE)
    EYE = np.eye(128, dtype=np.float32)
    SEL = np.zeros((NE, NE * 128), dtype=ml_dtypes.bfloat16)
    for e in range(NE):
        SEL[e, e * 128:(e + 1) * 128] = 1
    shared = dict(W0=W0, W1=W1, WT=WT, BF=BF, WGT=WG, BG=BG, EYE=EYE, SEL=SEL)
    in_maps = []
    for b in range(8):
        m = dict(shared)
        m["xT"] = np.ascontiguousarray(xh[b].T)
        in_maps.append(m)
    res = run_bass_kernel_spmd(nc, in_maps, core_ids=list(range(8)))
    out = np.stack([np.asarray(r["outT"]).T for r in res.results], axis=0)
    return np.ascontiguousarray(out, dtype=np.float32)


# version 19
# speedup vs baseline: 1.3672x; 1.3672x over previous
import numpy as np
import ml_dtypes
from contextlib import ExitStack
import concourse.bass as bass
import concourse.mybir as mybir
from concourse.bass_utils import run_bass_kernel_spmd

F32 = mybir.dt.float32
BF16 = mybir.dt.bfloat16
AF = mybir.ActivationFunctionType
ALU = mybir.AluOpType
AX = mybir.AxisListType.X

H = 1024
T = 4096
TC = 512
NCH = T // TC
HT = H // 128
NE = 8
NW = 5
NWT = 2

_CACHE = {}
DBG = False
STREAM_Q = "sync"
SKIP_STREAM_WAITS = False
PROBE_DUMMY_SRC = False
PROBE_TAIL = False
PROBE_SKIP_GATE = False
DBG_SCR = False
PROBE_GATE_LVL = 5
PROBE_SKIP_X = False
STOP = None


class _StopEmit(Exception):
    pass


def build():
    nc = bass.Bass("TRN2", target_bir_lowering=False, debug=False,
                   detect_race_conditions=False)

    ixT = nc.dram_tensor("xT", [H, T], F32, kind="ExternalInput").ap()
    iW0 = nc.dram_tensor("W0", [H, 3 * H], F32, kind="ExternalInput").ap()
    iW1 = nc.dram_tensor("W1", [H, 3 * H], F32, kind="ExternalInput").ap()
    iWT = nc.dram_tensor("WT", [NE, H, H], F32, kind="ExternalInput").ap()
    iBF = nc.dram_tensor("BF", [NE, H], F32, kind="ExternalInput").ap()
    iWG = nc.dram_tensor("WGT", [H, NE], F32, kind="ExternalInput").ap()
    iBG = nc.dram_tensor("BG", [1, NE], F32, kind="ExternalInput").ap()
    iEYE = nc.dram_tensor("EYE", [128, 128], F32, kind="ExternalInput").ap()
    iSEL = nc.dram_tensor("SEL", [NE, NE * 128], BF16, kind="ExternalInput").ap()
    oT = nc.dram_tensor("outT", [H, T], F32, kind="ExternalOutput").ap()
    if DBG:
        dbg_h0 = nc.dram_tensor("dbg_h0", [128, HT * TC], BF16,
                                kind="ExternalOutput").ap()
        dbg_h1 = nc.dram_tensor("dbg_h1", [128, HT * TC], BF16,
                                kind="ExternalOutput").ap()
        dbg_mrb = nc.dram_tensor("dbg_mrb", [128, 2 * TC], BF16,
                                 kind="ExternalOutput").ap()
        dbg_rows = nc.dram_tensor("dbg_rows", [1, 3 * TC], F32,
                                  kind="ExternalOutput").ap()
        dbg_x1 = nc.dram_tensor("dbg_x1", [128, HT * TC], F32,
                                kind="ExternalOutput").ap()
        dbg_x1b = nc.dram_tensor("dbg_x1b", [128, HT * TC], BF16,
                                 kind="ExternalOutput").ap()
        dbg_wq = nc.dram_tensor("dbg_wq", [128, 4 * NE], F32,
                                kind="ExternalOutput").ap()
        dbg_swtb = nc.dram_tensor("dbg_swtb", [NE, TC], BF16,
                                  kind="ExternalOutput").ap()
        dbg_wbb = nc.dram_tensor("dbg_wbb", [128, NE * TC], BF16,
                                 kind="ExternalOutput").ap()
        dbg_y = nc.dram_tensor("dbg_y", [128, HT * TC], BF16,
                               kind="ExternalOutput").ap()
        dbg_m2c = nc.dram_tensor("dbg_m2c", [128, 8], F32,
                                 kind="ExternalOutput").ap()
        dbg_csb = nc.dram_tensor("dbg_csb", [128, NE], F32,
                                 kind="ExternalOutput").ap()
        dbg_bgb = nc.dram_tensor("dbg_bgb", [128, NE], F32,
                                 kind="ExternalOutput").ap()
        dbg_wgt = nc.dram_tensor("dbg_wgt", [128, HT * NE], F32,
                                 kind="ExternalOutput").ap()
        dbg_stg = nc.dram_tensor("dbg_stg", [128, 70], F32,
                                 kind="ExternalOutput").ap()
    sW0 = nc.dram_tensor("scrW0", [8, 128, 3 * H], BF16, kind="Internal").ap()
    sW1 = nc.dram_tensor("scrW1", [8, 128, 3 * H], BF16, kind="Internal").ap()
    sWT = nc.dram_tensor("scrWT", [NE, 128, NE * H], BF16, kind="Internal").ap()

    es = ExitStack()
    SB = lambda n, s, d: es.enter_context(nc.sbuf_tensor(n, s, d))
    PS = lambda n, s: es.enter_context(nc.psum_tensor(n, s, F32))
    SEM = lambda n: es.enter_context(nc.semaphore(n))

    wpool = [SB(f"wp{i}", [128, 3 * H], BF16) for i in range(NW)]
    wtpool = [SB(f"wt{i}", [128, NE * H], BF16) for i in range(NWT)]
    xbuf = [SB(f"x{i}", [128, HT * TC], F32) for i in range(2)]
    h0b = SB("h0", [128, HT * TC], BF16)
    h0b2 = SB("h0x", [128, HT * TC], BF16)
    h1b = SB("h1", [128, HT * TC], BF16)
    xeb = [SB(f"xe{i}", [128, HT * TC], BF16) for i in range(2)]
    wbb = SB("wb", [128, NE * TC], BF16)
    sgb = [SB(f"sg{i}", [128, TC], F32) for i in range(2)]
    ab = [SB(f"a{i}", [128, TC], F32) for i in range(2)]
    shb = [SB(f"sh{i}", [128, TC], F32) for i in range(2)]
    gpb = [SB(f"gp{i}", [128, TC], F32) for i in range(2)]
    gtb = SB("gt", [128, TC], F32)
    vb = SB("v", [128, TC], F32)
    Hb = SB("Hs", [128, TC], F32)
    db = SB("d", [128, TC], F32)
    h2b = SB("h2", [128, TC], F32)
    t2t = SB("t2t", [128, TC], BF16)
    xsq = [SB(f"xsq{i}", [128, TC], BF16) for i in range(2)]
    rows = SB("rows", [1, 3 * TC], F32)
    mrb = SB("mrb", [128, 2 * TC], BF16)
    mrb2 = SB("mrb2", [128, 2 * TC], BF16)
    carry1 = SB("c1s", [128, HT], F32)
    carry2 = SB("c2s", [128, HT], F32)
    sid = SB("sid", [128, 128], F32)
    ssel = SB("ssel", [NE, NE * 128], BF16)
    swgt = SB("swgt", [128, HT * NE], F32)
    swgtb = SB("swgtb", [128, HT * NE], BF16)
    ltg = SB("ltg", [128, TC], F32)
    sbg = SB("sbg", [1, NE], F32)
    bgbb = SB("bgb", [128, 4 * NE], F32)
    csbb = SB("csb", [128, 4 * NE], F32)
    csrow = SB("csr", [1, NE], F32)
    cgrow = SB("cgrow", [1, 8 * NE], F32)
    bfb = SB("bfb", [NE, H], BF16)
    swTb = SB("swTb", [128, TC], BF16)
    gate = SB("gate", [128, 16 * NE], F32)
    rows4 = SB("rows4", [128, 20], F32)
    wq = SB("wq", [128, 4 * NE], F32)
    m2c = SB("m2c", [128, 8], F32)
    onec = SB("onec", [128, 1], F32)
    onecb = SB("onecb", [128, 1], BF16)
    oner = SB("oner", [1, 128], F32)
    epsc = SB("epsc", [1, 1], F32)
    ssink = SB("ssink", [128, 1], F32)
    dbgstg = SB("dbgstg", [128, 70], F32) if DBG else None

    p = [PS(f"p{i}", [128, TC]) for i in range(8)]

    semnames = (["sP", "sV", "sA", "dC", "dB", "dSt0", "dStT", "dSt1"]
                + [f"dX{i}" for i in range(2)]
                + [f"dWp{i}" for i in range(NW)]
                + [f"dWT{i}" for i in range(NWT)]
                + [f"dF{i}" for i in range(2)]
                + [f"dG{i}" for i in range(2)])
    sems = {k: SEM(k) for k in semnames}

    OPS = {k: [] for k in ("sync", "tensor", "vector", "scalar", "gpsimd")}
    CNT = {k: 0 for k in sems}
    LASTW = {}

    def W(st, sem, val):
        if not val:
            return
        key = (st, sem)
        if LASTW.get(key, 0) >= val:
            return
        LASTW[key] = val
        OPS[st].append(lambda eng, s=sems[sem], v=val: eng.wait_ge(s, v))

    def I(st, fn, sem=None, n=1):
        if sem is None:
            OPS[st].append(fn)
            return None
        CNT[sem] += n
        v = CNT[sem]
        OPS[st].append(lambda eng, f=fn, s=sems[sem], k=n: f(eng).then_inc(s, k))
        return v

    SEM_OF = {"vector": "sV", "scalar": "sA", "tensor": "sP"}

    def VCH(fn):
        v = I("vector", fn, "sV")
        W("vector", "sV", v)
        return v

    def VSYNC():
        W("vector", "sV", CNT["sV"])

    xv = [None] * NCH

    def x_load(c):
        dst = xbuf[c % 2][:].rearrange("p (ht t) -> p ht t", ht=HT)
        src = ixT.rearrange("(ht p) t -> p ht t", p=128)[:, :, c * TC:(c + 1) * TC]
        sm = f"dX{c % 2}"
        return (sm, I("sync", lambda eng, d=dst, s=src: eng.dma_start(d, s),
                      sm, 16))

    xv[0] = x_load(0)
    xv[1] = x_load(1)
    I("sync", lambda eng: eng.dma_start(sid[:], iEYE), "dC", 16)
    I("sync", lambda eng: eng.dma_start(ssel[:], iSEL), "dC", 16)
    I("sync", lambda eng: eng.dma_start(
        swgt[:].rearrange("p (ht e) -> p ht e", ht=HT),
        iWG.rearrange("(ht p) e -> p ht e", p=128)), "dC", 16)
    consts_v = I("sync", lambda eng: eng.dma_start(sbg[:], iBG), "dC", 16)
    bfv = I("gpsimd", lambda eng: eng.dma_start(bfb[:], iBF), "dB", 16)

    I("vector", lambda eng: eng.memset(onec[:], 1.0))
    I("vector", lambda eng: eng.memset(onecb[:], 1.0))
    I("vector", lambda eng: eng.memset(oner[:], 1.0))
    W("vector", "dC", consts_v)
    I("vector", lambda eng: eng.tensor_copy(swgtb[:], swgt[:]))
    bfcast = I("vector", lambda eng: eng.memset(epsc[:], 1e-5), "sV")

    W("tensor", "dC", consts_v)
    W("tensor", "sV", bfcast)
    if not PROBE_SKIP_GATE:
        L = PROBE_GATE_LVL
        if L >= 1:
            for ht in range(HT):
                I("tensor", lambda eng, h=ht: eng.matmul(
                    p[0][0:1, 0:NE], onec[:], swgt[:, h * NE:(h + 1) * NE],
                    start=(h == 0), stop=(h == HT - 1)))
        if L >= 2:
            pv = I("tensor", lambda eng: eng.matmul(
                p[1][:, 0:NE], oner[:], sbg[:], start=True, stop=True), "sP")
        if L >= 3:
            W("vector", "sP", pv)
            VCH(lambda eng: eng.tensor_copy(csrow[:], p[0][0:1, 0:NE]))
            csv = I("vector", lambda eng: eng.tensor_copy(
                ssink[0:1, :], csrow[0:1, 0:1]), "sV")
        if L >= 4:
            W("vector", "sP", pv)
            for i in range(4):
                VCH(lambda eng, i_=i: eng.tensor_copy(
                    cgrow[0:1, i_ * NE:(i_ + 1) * NE], csrow[:]))
                VCH(lambda eng, i_=i: eng.tensor_copy(
                    cgrow[0:1, 32 + i_ * NE:32 + (i_ + 1) * NE],
                    p[1][0:1, 0:NE]))
            csv2 = I("vector", lambda eng: eng.tensor_copy(
                ssink[0:1, :], cgrow[0:1, 0:1]), "sV")
            W("tensor", "sV", csv2)
            pv2 = I("tensor", lambda eng: eng.matmul(
                p[0][:, 0:8 * NE], oner[:], cgrow[:],
                start=True, stop=True), "sP")
        if L >= 5:
            W("vector", "sP", pv2)
            I("vector", lambda eng: eng.tensor_copy(csbb[:], p[0][:, 0:32]))
            I("vector", lambda eng: eng.tensor_copy(bgbb[:], p[0][:, 32:64]),
              "sV")

    def _tail():
        if PROBE_TAIL:
            for st in OPS:
                for _ in range(100):
                    OPS[st].append(lambda eng: eng.nop(cycle_cnt=10000))

    def _chk0(tag):
        if STOP is not None and (-1, tag) >= STOP:
            _tail()
            raise _StopEmit

    for jg in range(8):
        W("gpsimd", "dSt0", 16 * jg)
        I("gpsimd", lambda eng, j=jg: eng.dma_start(
            sW0[j].rearrange("p (ki kj ht c) -> p ki kj ht c",
                             ki=3, kj=1, c=128),
            iW0.rearrange("(ht p) (ki kj c) -> p ki kj ht c",
                          p=128, ki=3, c=128)[:, :, j:j + 1, :, :]),
          "dSt0", 16)
    for jg in range(8):
        W("gpsimd", "dSt0", 128)
        W("gpsimd", "dSt1", 16 * jg)
        I("gpsimd", lambda eng, j=jg: eng.dma_start(
            sW1[j].rearrange("p (ki kj ht c) -> p ki kj ht c",
                             ki=3, kj=1, c=128),
            iW1.rearrange("(ht p) (ki kj c) -> p ki kj ht c",
                          p=128, ki=3, c=128)[:, :, j:j + 1, :, :]),
          "dSt1", 16)
    for eg in range(NE):
        W("gpsimd", "dSt1", 128)
        W("gpsimd", "dStT", 16 * eg)
        I("gpsimd", lambda eng, e_=eg: eng.dma_start(
            sWT[e_].rearrange("p (ot dt c) -> p ot dt c", ot=8, c=128),
            iWT[e_].rearrange("(dt p) (ot c) -> p ot dt c", p=128, c=128)),
          "dStT", 16)

    NG01 = NCH * 16
    w01_load = [None] * NG01
    w01_cons = [None] * NG01

    def w01_issue(gidx):
        if gidx >= NG01:
            return
        ps, j = divmod(gidx % 16, 8)
        W("sync", "dSt1" if ps else "dSt0", 16 * (j + 1))
        if gidx >= NW:
            W("sync", "sP", w01_cons[gidx - NW])
        sm = f"dWp{gidx % NW}"
        scr = sW1 if ps else sW0
        w01_load[gidx] = (sm, I("sync", lambda eng, s=scr, j_=j,
                                sl=wpool[gidx % NW]:
                                eng.dma_start(sl[:], s[j_]), sm, 16))

    NGWT = NCH * NE
    wt_load = [None] * NGWT
    wt_cons = [None] * NGWT

    def wt_issue(gidx):
        if gidx >= NGWT:
            return
        e = gidx % NE
        W("scalar", "dStT", 16 * (e + 1))
        if gidx >= NWT:
            W("scalar", "sP", wt_cons[gidx - NWT])
        sm = f"dWT{gidx % NWT}"
        wt_load[gidx] = (sm, I("scalar", lambda eng, e_=e,
                               sl=wtpool[gidx % NWT]:
                               eng.dma_start(sl[:], sWT[e_]), sm, 16))

    for g in range(NW):
        w01_issue(g)

    bank_free = [0] * 8
    ln_bcast_done = [0]
    ssq_mm_prev = [None] * HT
    st_hist = []
    outdma = [0]
    h0bufs = [h0b, h0b2]
    h0d_cur = [None] * HT
    try:

        def ln_stats(c, second, vj2, from_buf):
            xcur = xbuf[c % 2]
            W("tensor", "sV", bank_free[6])
            W("tensor", "sV", bank_free[7])
            if not second:
                W("tensor", *xv[c])
            for ht in range(HT):
                if second:
                    W("tensor", "sV", vj2[ht])
                    I("tensor", lambda eng, h=ht, b_=from_buf: eng.matmul(
                        p[6][0:1, :], onecb[:], b_[:, h * TC:(h + 1) * TC],
                        start=(h == 0), stop=(h == HT - 1)))
                else:
                    I("tensor", lambda eng, h=ht, x_=xcur: eng.matmul(
                        p[6][0:1, :], onec[:], x_[:, h * TC:(h + 1) * TC],
                        start=(h == 0), stop=(h == HT - 1)))
            ssq_cur = [None] * HT
            for ht in range(HT):
                if second:
                    W("scalar", "sV", vj2[ht])
                else:
                    W("scalar", *xv[c])
                W("scalar", "sP", ssq_cur[ht - 2] if ht >= 2 else ssq_mm_prev[ht])
                if second:
                    sq = I("scalar", lambda eng, h=ht, b_=from_buf:
                           eng.activation(
                        xsq[h % 2][:], b_[:, h * TC:(h + 1) * TC], AF.Square),
                        "sA")
                else:
                    sq = I("scalar", lambda eng, h=ht, x_=xcur: eng.activation(
                        xsq[h % 2][:], x_[:, h * TC:(h + 1) * TC], AF.Square),
                        "sA")
                W("tensor", "sA", sq)
                ssq_cur[ht] = I("tensor", lambda eng, h=ht: eng.matmul(
                    p[7][0:1, :], onecb[:], xsq[h % 2][:],
                    start=(h == 0), stop=(h == HT - 1)), "sP")
            for ht in range(HT):
                ssq_mm_prev[ht] = ssq_cur[ht]
            stats_done = ssq_cur[HT - 1]
            mean = rows[0:1, 0:TC]
            var = rows[0:1, TC:2 * TC]
            rstd = rows[0:1, 2 * TC:3 * TC]
            W("vector", "sP", stats_done)
            W("vector", "sP", ln_bcast_done[0])
            I("vector", lambda eng: eng.tensor_scalar(
                mean, p[6][0:1, :], 1.0 / H, None, op0=ALU.mult))
            I("vector", lambda eng: eng.tensor_mul(var, mean, mean))
            I("vector", lambda eng: eng.tensor_scalar(
                rstd, p[7][0:1, :], 1.0 / H, None, op0=ALU.mult))
            var_done = I("vector", lambda eng: eng.tensor_sub(var, rstd, var), "sV")
            W("scalar", "sV", var_done)
            I("scalar", lambda eng: eng.activation(rstd, var, AF.Ln,
                                                   bias=epsc[0:1, :]))
            rstd_done = I("scalar", lambda eng: eng.activation(
                rstd, rstd, AF.Exp, scale=-0.5), "sA")
            return rstd_done

        def ln_bcast(rstd_done):
            mean = rows[0:1, 0:TC]
            rstd = rows[0:1, 2 * TC:3 * TC]
            W("tensor", "sA", rstd_done)
            I("tensor", lambda eng: eng.matmul(
                p[6][:], oner[:], mean, start=True, stop=True))
            ln_bcast_done[0] = I("tensor", lambda eng: eng.matmul(
                p[7][:], oner[:], rstd, start=True, stop=True), "sP")

        def ln1_copy():
            W("vector", "sP", ln_bcast_done[0])
            I("vector", lambda eng: eng.tensor_copy(mrb2[:, 0:TC], p[6][:]))
            bf67 = I("vector", lambda eng: eng.tensor_copy(
                mrb2[:, TC:2 * TC], p[7][:]), "sV")
            bank_free[6] = bank_free[7] = bf67

        def h0_build(c, hts):
            hbn = h0bufs[c % 2]
            xcur = xbuf[c % 2]
            W("vector", *xv[c])
            for ht in hts:
                I("vector", lambda eng, h=ht, x_=xcur: eng.tensor_sub(
                    t2t[:], x_[:, h * TC:(h + 1) * TC], mrb2[:, 0:TC]))
                h0d_cur[ht] = I("vector", lambda eng, h=ht, hb_=hbn:
                                eng.tensor_mul(
                    hb_[:, h * TC:(h + 1) * TC], t2t[:],
                    mrb2[:, TC:2 * TC]), "sV")

        def ln2(c, vj2, hb):
            rstd_done = ln_stats(c, True, vj2, hb)
            mean = rows[0:1, 0:TC]
            rstd = rows[0:1, 2 * TC:3 * TC]
            W("tensor", "sV", bank_free[5])
            W("tensor", "sA", rstd_done)
            for i in range(4):
                I("tensor", lambda eng, i_=i: eng.transpose(
                    p[5][:, i_:i_ + 1], mean[:, i_ * 128:(i_ + 1) * 128],
                    sid[0:1, 0:1]))
            tcol_done = None
            for i in range(4):
                tcol_done = I("tensor", lambda eng, i_=i: eng.transpose(
                    p[5][:, 4 + i_:5 + i_], rstd[:, i_ * 128:(i_ + 1) * 128],
                    sid[0:1, 0:1]), "sP" if i == 3 else None)
            W("vector", "sP", tcol_done)
            bank_free[5] = VCH(lambda eng: eng.tensor_copy(
                m2c[:], p[5][:, 0:8]))
            ln_bcast(rstd_done)
            W("vector", "sP", ln_bcast_done[0])
            I("vector", lambda eng: eng.tensor_copy(mrb[:, 0:TC], p[6][:]))
            bf67 = I("vector", lambda eng: eng.tensor_copy(
                mrb[:, TC:2 * TC], p[7][:]), "sV")
            bank_free[6] = bank_free[7] = bf67

        def _chk(c, tag):
            if STOP is not None and (c, tag) >= STOP:
                raise _StopEmit

        rd0 = ln_stats(0, False, None, None)
        ln_bcast(rd0)
        ln1_copy()
        h0_build(0, range(HT))

        for c in range(NCH):
            xcur = xbuf[c % 2]
            hb = h0bufs[c % 2]
            vj1 = [None] * HT
            vj2 = [None] * HT
            dep1 = list(h0d_cur)

            def mixer_pass(ps, rhs_buf, carry, vj_out, dep_list):
                for j in range(HT):
                    gidx = c * 16 + ps * 8 + j
                    bs = 3 * (j % 2)
                    W("tensor", *w01_load[gidx])
                    if j >= 2:
                        W("tensor", "sV", vj_out[j - 2])
                    else:
                        for b in (bs, bs + 1, bs + 2):
                            W("tensor", "sV", bank_free[b])
                    if j > 0:
                        W("tensor", "sV", dep_list[HT - 1])
                    z_drain = None
                    for ht in range(HT):
                        if j == 0:
                            W("tensor", "sV", dep_list[ht])
                        for ki in range(3):
                            z_drain = I("tensor", lambda eng, k=ki, h=ht,
                                        g=gidx, b=bs, r=rhs_buf: eng.matmul(
                                p[b + k][:],
                                wpool[g % NW][:, (k * 8 + h) * 128:(k * 8 + h + 1) * 128],
                                r[:, h * TC:(h + 1) * TC],
                                start=(h == 0), stop=(h == HT - 1)),
                                "sP" if (ki == 2 and ht == HT - 1) else None)
                    w01_cons[gidx] = z_drain
                    w01_issue(gidx + NW)
                    par = j % 2
                    if j >= 2:
                        W("scalar", "sV", vj_out[j - 2])
                    W("scalar", "sP", z_drain)
                    I("scalar", lambda eng, pa=par, b=bs: eng.activation(
                        shb[pa][:], p[b][:], AF.Sigmoid))
                    I("scalar", lambda eng, pa=par, b=bs: eng.activation(
                        sgb[pa][:], p[b + 1][:], AF.Sigmoid))
                    I("scalar", lambda eng, pa=par, b=bs: eng.activation(
                        ab[pa][:], p[b + 1][:], AF.Sigmoid, scale=-1.0))
                    s_done = I("scalar", lambda eng, pa=par, b=bs: eng.activation(
                        gpb[pa][:], p[b + 2][:], AF.Sigmoid), "sA")
                    W("vector", "sA", s_done)
                    I("vector", lambda eng, pa=par, b=bs: eng.scalar_tensor_tensor(
                        gtb[:], p[b][:], 0.5, shb[pa][:], op0=ALU.add, op1=ALU.max))
                    I("vector", lambda eng, pa=par: eng.tensor_mul(
                        vb[:], sgb[pa][:], gtb[:]))
                    if c == 0:
                        I("vector", lambda eng, pa=par: eng.tensor_tensor_scan(
                            Hb[:], ab[pa][:], vb[:], 0.0, op0=ALU.mult, op1=ALU.add))
                    else:
                        I("vector", lambda eng, pa=par, jj=j, ca=carry:
                          eng.tensor_tensor_scan(
                            Hb[:], ab[pa][:], vb[:], ca[:, jj:jj + 1],
                            op0=ALU.mult, op1=ALU.add))
                    I("vector", lambda eng, jj=j, ca=carry: eng.tensor_copy(
                        ca[:, jj:jj + 1], Hb[:, TC - 1:TC]))
                    if ps == 0:
                        I("vector", lambda eng, jj=j, hb_=hb: eng.tensor_sub(
                            db[:], Hb[:], hb_[:, jj * TC:(jj + 1) * TC]))
                        I("vector", lambda eng, pa=par: eng.tensor_mul(
                            db[:], gpb[pa][:], db[:]))
                        vj_out[j] = I("vector", lambda eng, jj=j, hb_=hb:
                                      eng.tensor_add(
                            h1b[:, jj * TC:(jj + 1) * TC], db[:],
                            hb_[:, jj * TC:(jj + 1) * TC]), "sV")
                    else:
                        I("vector", lambda eng, jj=j: eng.tensor_sub(
                            db[:], Hb[:], h1b[:, jj * TC:(jj + 1) * TC]))
                        I("vector", lambda eng, pa=par: eng.tensor_mul(
                            db[:], gpb[pa][:], db[:]))
                        I("vector", lambda eng, jj=j: eng.tensor_add(
                            h2b[:], db[:], h1b[:, jj * TC:(jj + 1) * TC]))
                        I("vector", lambda eng, jj=j, x_=xcur: eng.tensor_add(
                            x_[:, jj * TC:(jj + 1) * TC],
                            x_[:, jj * TC:(jj + 1) * TC], h2b[:]))
                        vj_out[j] = I("vector", lambda eng, jj=j, x_=xcur,
                                      hb_=hb: eng.tensor_copy(
                            hb_[:, jj * TC:(jj + 1) * TC],
                            x_[:, jj * TC:(jj + 1) * TC]), "sV")
                    for k in range(3):
                        bank_free[bs + k] = vj_out[j]

            _chk(c, "a_ln1")
            mixer_pass(0, hb, carry1, vj1, dep1)
            _chk(c, "b_pass1")
            if DBG and c == 0:
                W("gpsimd", "sV", vj1[HT - 1])
                I("gpsimd", lambda eng: eng.dma_start(dbg_h0, h0b[:]), "dG0", 16)
                I("gpsimd", lambda eng: eng.dma_start(dbg_h1, h1b[:]), "dG0", 16)
                I("gpsimd", lambda eng: eng.dma_start(dbg_mrb, mrb[:]), "dG0", 16)
                I("gpsimd", lambda eng: eng.dma_start(dbg_csb, csbb[:]), "dG0", 16)
                I("gpsimd", lambda eng: eng.dma_start(dbg_bgb, bgbb[:]), "dG0", 16)
                I("gpsimd", lambda eng: eng.dma_start(dbg_wgt, swgt[:]), "dG0", 16)
                dbgv = I("gpsimd", lambda eng: eng.dma_start(dbg_rows, rows[:]),
                         "dG0", 16)
                W("vector", "dG0", dbgv)
                W("tensor", "dG0", dbgv)
            mixer_pass(1, h1b, carry2, vj2, vj1)
            if DBG and c == 0:
                W("gpsimd", "sV", vj2[HT - 1])
                I("gpsimd", lambda eng: eng.dma_start(dbg_x1, xbuf[0][:]),
                  "dG0", 16)
                dbgv2 = I("gpsimd", lambda eng: eng.dma_start(
                    dbg_x1b, h0b[:]), "dG0", 16)
                W("vector", "dG0", dbgv2)
                W("tensor", "dG0", dbgv2)
                W("scalar", "dG0", dbgv2)
            _chk(c, "c_pass2")

            ln2(c, vj2, hb)
            _chk(c, "d_ln2")

            W("tensor", "sV", bank_free[0])
            W("tensor", "sV", bank_free[1])
            lgT_done = None
            for ht in range(HT):
                lgT_done = I("tensor", lambda eng, h=ht, hb_=hb: eng.matmul(
                    p[0][0:NE, :], swgtb[:, h * NE:(h + 1) * NE],
                    hb_[:, h * TC:(h + 1) * TC],
                    start=(h == 0), stop=(h == HT - 1)),
                    "sP" if ht == HT - 1 else None)
            W("vector", "sP", lgT_done)
            ltg_ev = I("vector", lambda eng: eng.tensor_copy(
                ltg[:], p[0][:]), "sV")
            bank_free[0] = ltg_ev
            W("tensor", "sV", ltg_ev)
            glt_done = None
            for tt in range(4):
                glt_done = I("tensor", lambda eng, t_=tt: eng.transpose(
                    p[1][:, t_ * NE:(t_ + 1) * NE],
                    ltg[0:NE, t_ * 128:(t_ + 1) * 128], sid[0:NE, 0:NE]),
                    "sP" if tt == 3 else None)
            LG = gate[:, 0:32]
            EB = gate[:, 32:64]
            TMP = gate[:, 64:96]
            WU = gate[:, 96:128]
            V3 = lambda t: t.rearrange("p (t e) -> p t e", t=4)
            B4 = lambda c: c.rearrange("p (t o) -> p t o", o=1).to_broadcast(
                [128, 4, NE])
            m4 = m2c[:, 0:4]
            r4 = m2c[:, 4:8]
            W("vector", "sP", glt_done)
            VSYNC()
            VCH(lambda eng: eng.tensor_tensor(
                V3(TMP), V3(csbb[:]), B4(m4), op=ALU.mult))
            VCH(lambda eng: eng.tensor_sub(LG, p[1][:, 0:32], TMP))
            VCH(lambda eng: eng.tensor_tensor(
                V3(LG), V3(LG), B4(r4), op=ALU.mult))
            VCH(lambda eng: eng.tensor_add(LG, LG, bgbb[:]))
            VCH(lambda eng: eng.reduce_max(
                rows4[:, 0:4].rearrange("p (t o) -> p t o", o=1),
                V3(LG), axis=AX, negate=True))
            VCH(lambda eng: eng.tensor_tensor(
                V3(LG), V3(LG), B4(rows4[:, 0:4]), op=ALU.add))
            lgv = I("vector", lambda eng: eng.tensor_copy(
                ssink[:], rows4[:, 0:1]), "sV")
            W("scalar", "sV", lgv)
            I("scalar", lambda eng: eng.activation(EB, LG, AF.Exp))
            gexpv = I("scalar", lambda eng: eng.copy(
                epsc[0:1, :], epsc[0:1, :]), "sA")
            W("vector", "sA", gexpv)
            VCH(lambda eng: eng.reduce_max(
                rows4[:, 12:16].rearrange("p (t o) -> p t o", o=1),
                V3(EB), axis=AX))
            VCH(lambda eng: eng.reduce_sum(
                rows4[:, 4:8].rearrange("p (t o) -> p t o", o=1),
                V3(EB), axis=AX))
            VCH(lambda eng: eng.reciprocal(rows4[:, 8:12], rows4[:, 4:8]))
            VCH(lambda eng: eng.tensor_tensor(
                V3(TMP), V3(EB), B4(rows4[:, 12:16]), op=ALU.is_lt))
            VCH(lambda eng: eng.tensor_mul(TMP, EB, TMP))
            VCH(lambda eng: eng.reduce_max(
                rows4[:, 16:20].rearrange("p (t o) -> p t o", o=1),
                V3(TMP), axis=AX))
            VCH(lambda eng: eng.tensor_tensor(
                V3(TMP), V3(EB), B4(rows4[:, 16:20]), op=ALU.is_ge))
            VCH(lambda eng: eng.tensor_mul(WU, EB, TMP))
            VCH(lambda eng: eng.tensor_tensor(
                V3(wq[:]), V3(WU), B4(rows4[:, 8:12]), op=ALU.mult))
            gvq = I("vector", lambda eng: eng.tensor_copy(
                ssink[:], wq[:, 0:1]), "sV")
            bank_free[1] = gvq

            W("tensor", "sV", bank_free[1])
            wTp_done = None
            W("tensor", "sV", gvq)
            for tt in range(4):
                wTp_done = I("tensor", lambda eng, t_=tt: eng.transpose(
                    p[1][0:NE, t_ * 128:(t_ + 1) * 128],
                    wq[:, t_ * NE:(t_ + 1) * NE], sid[:]),
                    "sP" if tt == 3 else None)
            W("vector", "sP", wTp_done)
            swtb_done = I("vector", lambda eng: eng.tensor_copy(
                swTb[:], p[1][:]), "sV")
            bank_free[1] = swtb_done

            rd_next = (ln_stats(c + 1, False, None, None)
                       if c + 1 < NCH else None)

            wb_ev = [None] * NE
            for e in range(NE):
                bk = 2 + e % 2
                W("tensor", "sV", bank_free[bk])
                W("tensor", "sV", swtb_done)
                wbv = I("tensor", lambda eng, e_=e, b=bk: eng.matmul(
                    p[b][:], ssel[:, e_ * 128:(e_ + 1) * 128], swTb[0:NE, :],
                    start=True, stop=True), "sP")
                W("vector", "sP", wbv)
                wb_ev[e] = I("vector", lambda eng, e_=e, b=bk: eng.tensor_copy(
                    wbb[:, e_ * TC:(e_ + 1) * TC], p[b][:]), "sV")
                bank_free[bk] = wb_ev[e]

            if rd_next is not None:
                ln_bcast(rd_next)

            y_done = None
            for dt in range(HT):
                I("vector", lambda eng, d_=dt, hb_=hb: eng.tensor_sub(
                    t2t[:], hb_[:, d_ * TC:(d_ + 1) * TC], mrb[:, 0:TC]))
                y_done = I("vector", lambda eng, d_=dt: eng.tensor_mul(
                    h1b[:, d_ * TC:(d_ + 1) * TC], t2t[:], mrb[:, TC:2 * TC]), "sV")
            if rd_next is not None:
                ln1_copy()

            xe_done = [None] * NE

            def build_xe(e):
                for dt in range(HT):
                    v = I("vector", lambda eng, e_=e, d_=dt: eng.tensor_mul(
                        xeb[e_ % 2][:, d_ * TC:(d_ + 1) * TC],
                        h1b[:, d_ * TC:(d_ + 1) * TC],
                        wbb[:, e_ * TC:(e_ + 1) * TC]), "sV")
                xe_done[e] = v

            if DBG and c == 0:
                W("gpsimd", "sV", wb_ev[NE - 1])
                W("gpsimd", "sV", y_done)
                I("gpsimd", lambda eng: eng.dma_start(dbg_wq, wq[:]), "dG0", 16)
                I("gpsimd", lambda eng: eng.dma_start(dbg_swtb, swTb[:]),
                  "dG0", 16)
                I("gpsimd", lambda eng: eng.dma_start(dbg_wbb, wbb[:]), "dG0", 16)
                I("gpsimd", lambda eng: eng.dma_start(dbg_y, h1b[:]), "dG0", 16)
                I("gpsimd", lambda eng: eng.dma_start(dbg_stg, dbgstg[:]),
                  "dG0", 16)
                dbgv3 = I("gpsimd", lambda eng: eng.dma_start(
                    dbg_m2c, m2c[:]), "dG0", 16)
                W("vector", "dG0", dbgv3)
                W("tensor", "dG0", dbgv3)
            build_xe(0)
            _chk(c, "f_prep")

            if c == 0:
                for g in range(NWT):
                    wt_issue(g)
            for b in range(8):
                W("tensor", "sV", bank_free[b])
            W("tensor", "sV", wb_ev[NE - 1])
            W("tensor", "dB", bfv)
            for ot in range(8):
                I("tensor", lambda eng, o=ot: eng.matmul(
                    p[o][:], bfb[:, o * 128:(o + 1) * 128], swTb[0:NE, :],
                    start=True, stop=False))
            ffn_e = [None] * NE
            for e in range(NE):
                gwt = c * NE + e
                W("tensor", *wt_load[gwt])
                W("tensor", "sV", xe_done[e])
                for ot in range(8):
                    for dt in range(HT):
                        ffn_e[e] = I("tensor", lambda eng, e_=e, o=ot, d_=dt,
                                     g=gwt: eng.matmul(
                            p[o][:],
                            wtpool[g % NWT][:, (o * 8 + d_) * 128:(o * 8 + d_ + 1) * 128],
                            xeb[e_ % 2][:, d_ * TC:(d_ + 1) * TC],
                            start=False, stop=(e_ == NE - 1 and d_ == HT - 1)),
                            "sP" if (ot == 7 and dt == HT - 1) else None)
                wt_cons[gwt] = ffn_e[e]
                wt_issue(gwt + NWT)
                if e + 1 < NE:
                    if e >= 1:
                        W("vector", "sP", ffn_e[e - 1])
                    build_xe(e + 1)
                    if rd_next is not None and e < 4:
                        h0_build(c + 1, [2 * e, 2 * e + 1])

            for ot in (6, 7, 0, 1, 2, 3, 4, 5):
                W("vector", "sP", ffn_e[NE - 1])
                ev = I("vector", lambda eng, o=ot, x_=xcur: eng.tensor_add(
                    x_[:, o * TC:(o + 1) * TC], p[o][:],
                    x_[:, o * TC:(o + 1) * TC]), "sV")
                bank_free[ot] = ev
                W("scalar", "sV", ev)
                dst = oT.rearrange("(ot p) t -> p ot t", p=128)[
                    :, ot:ot + 1, c * TC:(c + 1) * TC]
                sm = f"dG{ot % 2}"
                st = (sm, I("scalar", lambda eng, d=dst, o=ot, x_=xcur:
                            eng.dma_start(
                    d, x_[:, o * TC:(o + 1) * TC].rearrange(
                        "p (a t) -> p a t", a=1)), sm, 16))
                st_hist.append(st)
                outdma[0] += 1

            _chk(c, "g_chunk")
            if c + 2 < NCH:
                W("sync", "sV", bank_free[5])
                W("sync", *st_hist[-1])
                W("sync", *st_hist[-2])
                xv[c + 2] = x_load(c + 2)

    except _StopEmit:
        pass

    W("scalar", "dG0", CNT["dG0"])
    W("scalar", "dG1", CNT["dG1"])

    with nc.Block() as blk:
        @blk.sync
        def _(eng):
            for f in OPS["sync"]:
                f(eng)

        @blk.tensor
        def _(eng):
            for f in OPS["tensor"]:
                f(eng)

        @blk.vector
        def _(eng):
            for f in OPS["vector"]:
                f(eng)

        @blk.scalar
        def _(eng):
            for f in OPS["scalar"]:
                f(eng)

        @blk.gpsimd
        def _(eng):
            for f in OPS["gpsimd"]:
                f(eng)

    es.close()
    return nc


def kernel(L0_attn_h, L0_attn_W0, L0_attn_W1, L0_ffn_W, L0_ffn_b,
           L0_ffn_Wg, L0_ffn_bg):
    if "nc" not in _CACHE:
        _CACHE["nc"] = build()
    nc = _CACHE["nc"]

    f32 = lambda a: np.ascontiguousarray(np.asarray(a, dtype=np.float32))
    xh = f32(L0_attn_h)
    W0 = f32(L0_attn_W0)
    W1 = f32(L0_attn_W1)
    WT = np.ascontiguousarray(f32(L0_ffn_W).transpose(0, 2, 1))
    BF = f32(L0_ffn_b)
    WG = np.ascontiguousarray(f32(L0_ffn_Wg).T)
    BG = f32(L0_ffn_bg).reshape(1, NE)
    EYE = np.eye(128, dtype=np.float32)
    SEL = np.zeros((NE, NE * 128), dtype=ml_dtypes.bfloat16)
    for e in range(NE):
        SEL[e, e * 128:(e + 1) * 128] = 1
    shared = dict(W0=W0, W1=W1, WT=WT, BF=BF, WGT=WG, BG=BG, EYE=EYE, SEL=SEL)
    in_maps = []
    for b in range(8):
        m = dict(shared)
        m["xT"] = np.ascontiguousarray(xh[b].T)
        in_maps.append(m)
    res = run_bass_kernel_spmd(nc, in_maps, core_ids=list(range(8)))
    out = np.stack([np.asarray(r["outT"]).T for r in res.results], axis=0)
    return np.ascontiguousarray(out, dtype=np.float32)


# version 20
# speedup vs baseline: 1.4732x; 1.0775x over previous
import numpy as np
import ml_dtypes
from contextlib import ExitStack
import concourse.bass as bass
import concourse.mybir as mybir
from concourse.bass_utils import run_bass_kernel_spmd

F32 = mybir.dt.float32
BF16 = mybir.dt.bfloat16
AF = mybir.ActivationFunctionType
ALU = mybir.AluOpType
AX = mybir.AxisListType.X

H = 1024
T = 4096
TC = 512
NCH = T // TC
HT = H // 128
NE = 8
NW = 5
NWT = 2

_CACHE = {}
DBG = False
STREAM_Q = "sync"
SKIP_STREAM_WAITS = False
PROBE_DUMMY_SRC = False
PROBE_TAIL = False
PROBE_SKIP_GATE = False
DBG_SCR = False
PROBE_GATE_LVL = 5
PROBE_SKIP_X = False
STOP = None


class _StopEmit(Exception):
    pass


def build():
    nc = bass.Bass("TRN2", target_bir_lowering=False, debug=False,
                   detect_race_conditions=False)

    ixT = nc.dram_tensor("xT", [H, T], F32, kind="ExternalInput").ap()
    iW0 = nc.dram_tensor("W0", [H, 3 * H], F32, kind="ExternalInput").ap()
    iW1 = nc.dram_tensor("W1", [H, 3 * H], F32, kind="ExternalInput").ap()
    iWT = nc.dram_tensor("WT", [NE, H, H], F32, kind="ExternalInput").ap()
    iBF = nc.dram_tensor("BF", [NE, H], F32, kind="ExternalInput").ap()
    iWG = nc.dram_tensor("WGT", [H, NE], F32, kind="ExternalInput").ap()
    iBG = nc.dram_tensor("BG", [1, NE], F32, kind="ExternalInput").ap()
    iEYE = nc.dram_tensor("EYE", [128, 128], F32, kind="ExternalInput").ap()
    iSEL = nc.dram_tensor("SEL", [NE, NE * 128], BF16, kind="ExternalInput").ap()
    oT = nc.dram_tensor("outT", [H, T], F32, kind="ExternalOutput").ap()
    if DBG:
        dbg_h0 = nc.dram_tensor("dbg_h0", [128, HT * TC], BF16,
                                kind="ExternalOutput").ap()
        dbg_h1 = nc.dram_tensor("dbg_h1", [128, HT * TC], BF16,
                                kind="ExternalOutput").ap()
        dbg_mrb = nc.dram_tensor("dbg_mrb", [128, 2 * TC], BF16,
                                 kind="ExternalOutput").ap()
        dbg_rows = nc.dram_tensor("dbg_rows", [1, 3 * TC], F32,
                                  kind="ExternalOutput").ap()
        dbg_x1 = nc.dram_tensor("dbg_x1", [128, HT * TC], F32,
                                kind="ExternalOutput").ap()
        dbg_x1b = nc.dram_tensor("dbg_x1b", [128, HT * TC], BF16,
                                 kind="ExternalOutput").ap()
        dbg_wq = nc.dram_tensor("dbg_wq", [128, 4 * NE], F32,
                                kind="ExternalOutput").ap()
        dbg_swtb = nc.dram_tensor("dbg_swtb", [NE, TC], BF16,
                                  kind="ExternalOutput").ap()
        dbg_wbb = nc.dram_tensor("dbg_wbb", [128, NE * TC], BF16,
                                 kind="ExternalOutput").ap()
        dbg_y = nc.dram_tensor("dbg_y", [128, HT * TC], BF16,
                               kind="ExternalOutput").ap()
        dbg_m2c = nc.dram_tensor("dbg_m2c", [128, 8], F32,
                                 kind="ExternalOutput").ap()
        dbg_csb = nc.dram_tensor("dbg_csb", [128, NE], F32,
                                 kind="ExternalOutput").ap()
        dbg_bgb = nc.dram_tensor("dbg_bgb", [128, NE], F32,
                                 kind="ExternalOutput").ap()
        dbg_wgt = nc.dram_tensor("dbg_wgt", [128, HT * NE], F32,
                                 kind="ExternalOutput").ap()
        dbg_stg = nc.dram_tensor("dbg_stg", [128, 70], F32,
                                 kind="ExternalOutput").ap()
    sW0 = nc.dram_tensor("scrW0", [8, 128, 3 * H], BF16, kind="Internal").ap()
    sW1 = nc.dram_tensor("scrW1", [8, 128, 3 * H], BF16, kind="Internal").ap()
    sWT = nc.dram_tensor("scrWT", [NE, 128, NE * H], BF16, kind="Internal").ap()

    es = ExitStack()
    SB = lambda n, s, d: es.enter_context(nc.sbuf_tensor(n, s, d))
    PS = lambda n, s: es.enter_context(nc.psum_tensor(n, s, F32))
    SEM = lambda n: es.enter_context(nc.semaphore(n))

    wpool = [SB(f"wp{i}", [128, 3 * H], BF16) for i in range(NW)]
    wtpool = [SB(f"wt{i}", [128, NE * H], BF16) for i in range(NWT)]
    xbuf = [SB(f"x{i}", [128, HT * TC], F32) for i in range(2)]
    h0b = SB("h0", [128, HT * TC], BF16)
    h0b2 = SB("h0x", [128, HT * TC], BF16)
    h1b = SB("h1", [128, HT * TC], BF16)
    xeb = [SB(f"xe{i}", [128, HT * TC], BF16) for i in range(2)]
    wbb = SB("wb", [128, NE * TC], BF16)
    sgb = [SB(f"sg{i}", [128, TC], F32) for i in range(2)]
    ab = [SB(f"a{i}", [128, TC], F32) for i in range(2)]
    shb = [SB(f"sh{i}", [128, TC], F32) for i in range(2)]
    gpb = [SB(f"gp{i}", [128, TC], F32) for i in range(2)]
    gtb = SB("gt", [128, TC], F32)
    vb = SB("v", [128, TC], F32)
    Hb = SB("Hs", [128, TC], F32)
    db = SB("d", [128, TC], F32)
    h2b = SB("h2", [128, TC], F32)
    t2t = SB("t2t", [128, TC], BF16)
    xsq = [SB(f"xsq{i}", [128, TC], BF16) for i in range(2)]
    rows = SB("rows", [1, 3 * TC], F32)
    mrb = SB("mrb", [128, 2 * TC], BF16)
    mrb2 = SB("mrb2", [128, 2 * TC], BF16)
    carry1 = SB("c1s", [128, HT], F32)
    carry2 = SB("c2s", [128, HT], F32)
    sid = SB("sid", [128, 128], F32)
    ssel = SB("ssel", [NE, NE * 128], BF16)
    swgt = SB("swgt", [128, HT * NE], F32)
    swgtb = SB("swgtb", [128, HT * NE], BF16)
    ltg = SB("ltg", [128, TC], F32)
    sbg = SB("sbg", [1, NE], F32)
    bgbb = SB("bgb", [128, 4 * NE], F32)
    csbb = SB("csb", [128, 4 * NE], F32)
    csrow = SB("csr", [1, NE], F32)
    cgrow = SB("cgrow", [1, 8 * NE], F32)
    bfb = SB("bfb", [NE, H], BF16)
    swTb = SB("swTb", [128, TC], BF16)
    gate = SB("gate", [128, 16 * NE], F32)
    rows4 = SB("rows4", [128, 20], F32)
    wq = SB("wq", [128, 4 * NE], F32)
    m2c = SB("m2c", [128, 8], F32)
    onec = SB("onec", [128, 1], F32)
    onecb = SB("onecb", [128, 1], BF16)
    oner = SB("oner", [1, 128], F32)
    epsc = SB("epsc", [1, 1], F32)
    ssink = SB("ssink", [128, 1], F32)
    dbgstg = SB("dbgstg", [128, 70], F32) if DBG else None

    p = [PS(f"p{i}", [128, TC]) for i in range(8)]

    semnames = (["sP", "sV", "sA", "dC", "dB", "dSt0", "dStT", "dSt1"]
                + [f"dX{i}" for i in range(2)]
                + [f"dWp{i}" for i in range(NW)]
                + [f"dWT{i}" for i in range(NWT)]
                + [f"dF{i}" for i in range(2)]
                + [f"dG{i}" for i in range(2)])
    sems = {k: SEM(k) for k in semnames}

    OPS = {k: [] for k in ("sync", "tensor", "vector", "scalar", "gpsimd")}
    CNT = {k: 0 for k in sems}
    LASTW = {}

    def W(st, sem, val):
        if not val:
            return
        key = (st, sem)
        if LASTW.get(key, 0) >= val:
            return
        LASTW[key] = val
        OPS[st].append(lambda eng, s=sems[sem], v=val: eng.wait_ge(s, v))

    def I(st, fn, sem=None, n=1):
        if sem is None:
            OPS[st].append(fn)
            return None
        CNT[sem] += n
        v = CNT[sem]
        OPS[st].append(lambda eng, f=fn, s=sems[sem], k=n: f(eng).then_inc(s, k))
        return v

    SEM_OF = {"vector": "sV", "scalar": "sA", "tensor": "sP"}

    def VCH(fn):
        v = I("vector", fn, "sV")
        W("vector", "sV", v)
        return v

    def VSYNC():
        W("vector", "sV", CNT["sV"])

    xv = [None] * NCH

    def x_load(c):
        dst = xbuf[c % 2][:].rearrange("p (ht t) -> p ht t", ht=HT)
        src = ixT.rearrange("(ht p) t -> p ht t", p=128)[:, :, c * TC:(c + 1) * TC]
        sm = f"dX{c % 2}"
        return (sm, I("sync", lambda eng, d=dst, s=src: eng.dma_start(d, s),
                      sm, 16))

    xv[0] = x_load(0)
    xv[1] = x_load(1)
    I("sync", lambda eng: eng.dma_start(sid[:], iEYE), "dC", 16)
    I("sync", lambda eng: eng.dma_start(ssel[:], iSEL), "dC", 16)
    I("sync", lambda eng: eng.dma_start(
        swgt[:].rearrange("p (ht e) -> p ht e", ht=HT),
        iWG.rearrange("(ht p) e -> p ht e", p=128)), "dC", 16)
    consts_v = I("sync", lambda eng: eng.dma_start(sbg[:], iBG), "dC", 16)
    bfv = I("gpsimd", lambda eng: eng.dma_start(bfb[:], iBF), "dB", 16)

    I("vector", lambda eng: eng.memset(onec[:], 1.0))
    I("vector", lambda eng: eng.memset(onecb[:], 1.0))
    I("vector", lambda eng: eng.memset(oner[:], 1.0))
    W("vector", "dC", consts_v)
    I("vector", lambda eng: eng.tensor_copy(swgtb[:], swgt[:]))
    bfcast = I("vector", lambda eng: eng.memset(epsc[:], 1e-5), "sV")

    W("tensor", "dC", consts_v)
    W("tensor", "sV", bfcast)
    if not PROBE_SKIP_GATE:
        L = PROBE_GATE_LVL
        if L >= 1:
            for ht in range(HT):
                I("tensor", lambda eng, h=ht: eng.matmul(
                    p[0][0:1, 0:NE], onec[:], swgt[:, h * NE:(h + 1) * NE],
                    start=(h == 0), stop=(h == HT - 1)))
        if L >= 2:
            pv = I("tensor", lambda eng: eng.matmul(
                p[1][:, 0:NE], oner[:], sbg[:], start=True, stop=True), "sP")
        if L >= 3:
            W("vector", "sP", pv)
            VCH(lambda eng: eng.tensor_copy(csrow[:], p[0][0:1, 0:NE]))
            csv = I("vector", lambda eng: eng.tensor_copy(
                ssink[0:1, :], csrow[0:1, 0:1]), "sV")
        if L >= 4:
            W("vector", "sP", pv)
            for i in range(4):
                VCH(lambda eng, i_=i: eng.tensor_copy(
                    cgrow[0:1, i_ * NE:(i_ + 1) * NE], csrow[:]))
                VCH(lambda eng, i_=i: eng.tensor_copy(
                    cgrow[0:1, 32 + i_ * NE:32 + (i_ + 1) * NE],
                    p[1][0:1, 0:NE]))
            csv2 = I("vector", lambda eng: eng.tensor_copy(
                ssink[0:1, :], cgrow[0:1, 0:1]), "sV")
            W("tensor", "sV", csv2)
            pv2 = I("tensor", lambda eng: eng.matmul(
                p[0][:, 0:8 * NE], oner[:], cgrow[:],
                start=True, stop=True), "sP")
        if L >= 5:
            W("vector", "sP", pv2)
            I("vector", lambda eng: eng.tensor_copy(csbb[:], p[0][:, 0:32]))
            I("vector", lambda eng: eng.tensor_copy(bgbb[:], p[0][:, 32:64]),
              "sV")

    def _tail():
        if PROBE_TAIL:
            for st in OPS:
                for _ in range(100):
                    OPS[st].append(lambda eng: eng.nop(cycle_cnt=10000))

    def _chk0(tag):
        if STOP is not None and (-1, tag) >= STOP:
            _tail()
            raise _StopEmit

    for jg in range(8):
        W("gpsimd", "dSt0", 16 * jg)
        I("gpsimd", lambda eng, j=jg: eng.dma_start(
            sW0[j].rearrange("p (ki kj ht c) -> p ki kj ht c",
                             ki=3, kj=1, c=128),
            iW0.rearrange("(ht p) (ki kj c) -> p ki kj ht c",
                          p=128, ki=3, c=128)[:, :, j:j + 1, :, :]),
          "dSt0", 16)
    for jg in range(8):
        W("gpsimd", "dSt0", 128)
        W("gpsimd", "dSt1", 16 * jg)
        I("gpsimd", lambda eng, j=jg: eng.dma_start(
            sW1[j].rearrange("p (ki kj ht c) -> p ki kj ht c",
                             ki=3, kj=1, c=128),
            iW1.rearrange("(ht p) (ki kj c) -> p ki kj ht c",
                          p=128, ki=3, c=128)[:, :, j:j + 1, :, :]),
          "dSt1", 16)
    for eg in range(NE):
        W("gpsimd", "dSt1", 128)
        W("gpsimd", "dStT", 16 * eg)
        I("gpsimd", lambda eng, e_=eg: eng.dma_start(
            sWT[e_].rearrange("p (ot dt c) -> p ot dt c", ot=8, c=128),
            iWT[e_].rearrange("(dt p) (ot c) -> p ot dt c", p=128, c=128)),
          "dStT", 16)

    NG01 = NCH * 16
    w01_load = [None] * NG01
    w01_cons = [None] * NG01

    def w01_issue(gidx):
        if gidx >= NG01:
            return
        ps, j = divmod(gidx % 16, 8)
        W("sync", "dSt1" if ps else "dSt0", 16 * (j + 1))
        if gidx >= NW:
            W("sync", "sP", w01_cons[gidx - NW])
        sm = f"dWp{gidx % NW}"
        scr = sW1 if ps else sW0
        w01_load[gidx] = (sm, I("sync", lambda eng, s=scr, j_=j,
                                sl=wpool[gidx % NW]:
                                eng.dma_start(sl[:], s[j_]), sm, 16))

    NGWT = NCH * NE
    wt_load = [None] * NGWT
    wt_cons = [None] * NGWT

    def wt_issue(gidx):
        if gidx >= NGWT:
            return
        e = gidx % NE
        W("scalar", "dStT", 16 * (e + 1))
        if gidx >= NWT:
            W("scalar", "sP", wt_cons[gidx - NWT])
        sm = f"dWT{gidx % NWT}"
        wt_load[gidx] = (sm, I("scalar", lambda eng, e_=e,
                               sl=wtpool[gidx % NWT]:
                               eng.dma_start(sl[:], sWT[e_]), sm, 16))

    for g in range(NW):
        w01_issue(g)

    bank_free = [0] * 8
    ln_bcast_done = [0]
    ssq_mm_prev = [None] * HT
    st_hist = []
    outdma = [0]
    h0bufs = [h0b, h0b2]
    h0d_cur = [None] * HT
    try:

        def ln_stats(c, second, vj2, from_buf):
            xcur = xbuf[c % 2]
            W("tensor", "sV", bank_free[6])
            W("tensor", "sV", bank_free[7])
            if not second:
                W("tensor", *xv[c])
            for ht in range(HT):
                if second:
                    W("tensor", "sV", vj2[ht])
                    I("tensor", lambda eng, h=ht, b_=from_buf: eng.matmul(
                        p[6][0:1, :], onecb[:], b_[:, h * TC:(h + 1) * TC],
                        start=(h == 0), stop=(h == HT - 1)))
                else:
                    I("tensor", lambda eng, h=ht, x_=xcur: eng.matmul(
                        p[6][0:1, :], onec[:], x_[:, h * TC:(h + 1) * TC],
                        start=(h == 0), stop=(h == HT - 1)))
            ssq_cur = [None] * HT
            for ht in range(HT):
                if second:
                    W("scalar", "sV", vj2[ht])
                else:
                    W("scalar", *xv[c])
                W("scalar", "sP", ssq_cur[ht - 2] if ht >= 2 else ssq_mm_prev[ht])
                if second:
                    sq = I("scalar", lambda eng, h=ht, b_=from_buf:
                           eng.activation(
                        xsq[h % 2][:], b_[:, h * TC:(h + 1) * TC], AF.Square),
                        "sA")
                else:
                    sq = I("scalar", lambda eng, h=ht, x_=xcur: eng.activation(
                        xsq[h % 2][:], x_[:, h * TC:(h + 1) * TC], AF.Square),
                        "sA")
                W("tensor", "sA", sq)
                ssq_cur[ht] = I("tensor", lambda eng, h=ht: eng.matmul(
                    p[7][0:1, :], onecb[:], xsq[h % 2][:],
                    start=(h == 0), stop=(h == HT - 1)), "sP")
            for ht in range(HT):
                ssq_mm_prev[ht] = ssq_cur[ht]
            stats_done = ssq_cur[HT - 1]
            mean = rows[0:1, 0:TC]
            var = rows[0:1, TC:2 * TC]
            rstd = rows[0:1, 2 * TC:3 * TC]
            W("vector", "sP", stats_done)
            W("vector", "sP", ln_bcast_done[0])
            I("vector", lambda eng: eng.tensor_scalar(
                mean, p[6][0:1, :], 1.0 / H, None, op0=ALU.mult))
            I("vector", lambda eng: eng.tensor_mul(var, mean, mean))
            I("vector", lambda eng: eng.tensor_scalar(
                rstd, p[7][0:1, :], 1.0 / H, None, op0=ALU.mult))
            var_done = I("vector", lambda eng: eng.tensor_sub(var, rstd, var), "sV")
            W("scalar", "sV", var_done)
            I("scalar", lambda eng: eng.activation(rstd, var, AF.Ln,
                                                   bias=epsc[0:1, :]))
            rstd_done = I("scalar", lambda eng: eng.activation(
                rstd, rstd, AF.Exp, scale=-0.5), "sA")
            return rstd_done

        def ln_bcast(rstd_done):
            mean = rows[0:1, 0:TC]
            rstd = rows[0:1, 2 * TC:3 * TC]
            W("tensor", "sA", rstd_done)
            I("tensor", lambda eng: eng.matmul(
                p[6][:], oner[:], mean, start=True, stop=True))
            ln_bcast_done[0] = I("tensor", lambda eng: eng.matmul(
                p[7][:], oner[:], rstd, start=True, stop=True), "sP")

        def ln1_copy():
            W("vector", "sP", ln_bcast_done[0])
            I("vector", lambda eng: eng.tensor_copy(mrb2[:, 0:TC], p[6][:]))
            bf67 = I("vector", lambda eng: eng.tensor_copy(
                mrb2[:, TC:2 * TC], p[7][:]), "sV")
            bank_free[6] = bank_free[7] = bf67

        def h0_build(c, hts):
            hbn = h0bufs[c % 2]
            xcur = xbuf[c % 2]
            W("vector", *xv[c])
            for ht in hts:
                I("vector", lambda eng, h=ht, x_=xcur: eng.tensor_sub(
                    t2t[:], x_[:, h * TC:(h + 1) * TC], mrb2[:, 0:TC]))
                h0d_cur[ht] = I("vector", lambda eng, h=ht, hb_=hbn:
                                eng.tensor_mul(
                    hb_[:, h * TC:(h + 1) * TC], t2t[:],
                    mrb2[:, TC:2 * TC]), "sV")

        def ln2(c, vj2, hb):
            rstd_done = ln_stats(c, True, vj2, hb)
            mean = rows[0:1, 0:TC]
            rstd = rows[0:1, 2 * TC:3 * TC]
            W("tensor", "sV", bank_free[5])
            W("tensor", "sA", rstd_done)
            for i in range(4):
                I("tensor", lambda eng, i_=i: eng.transpose(
                    p[5][:, i_:i_ + 1], mean[:, i_ * 128:(i_ + 1) * 128],
                    sid[0:1, 0:1]))
            tcol_done = None
            for i in range(4):
                tcol_done = I("tensor", lambda eng, i_=i: eng.transpose(
                    p[5][:, 4 + i_:5 + i_], rstd[:, i_ * 128:(i_ + 1) * 128],
                    sid[0:1, 0:1]), "sP" if i == 3 else None)
            W("vector", "sP", tcol_done)
            bank_free[5] = VCH(lambda eng: eng.tensor_copy(
                m2c[:], p[5][:, 0:8]))
            ln_bcast(rstd_done)
            W("vector", "sP", ln_bcast_done[0])
            I("vector", lambda eng: eng.tensor_copy(mrb[:, 0:TC], p[6][:]))
            bf67 = I("vector", lambda eng: eng.tensor_copy(
                mrb[:, TC:2 * TC], p[7][:]), "sV")
            bank_free[6] = bank_free[7] = bf67

        def _chk(c, tag):
            if STOP is not None and (c, tag) >= STOP:
                raise _StopEmit

        rd0 = ln_stats(0, False, None, None)
        ln_bcast(rd0)
        ln1_copy()
        h0_build(0, range(HT))

        for c in range(NCH):
            xcur = xbuf[c % 2]
            hb = h0bufs[c % 2]
            vj1 = [None] * HT
            vj2 = [None] * HT
            dep1 = list(h0d_cur)

            def mixer_pass(ps, rhs_buf, carry, vj_out, dep_list):
                for j in range(HT):
                    gidx = c * 16 + ps * 8 + j
                    bs = 3 * (j % 2)
                    W("tensor", *w01_load[gidx])
                    if j >= 2:
                        W("tensor", "sV", vj_out[j - 2])
                    else:
                        for b in (bs, bs + 1, bs + 2):
                            W("tensor", "sV", bank_free[b])
                    if j > 0:
                        W("tensor", "sV", dep_list[HT - 1])
                    zd = [None] * 3
                    for ki in (1, 0, 2):
                        for ht in range(HT):
                            if j == 0 and ki == 1:
                                W("tensor", "sV", dep_list[ht])
                            zd[ki] = I("tensor", lambda eng, k=ki, h=ht,
                                       g=gidx, b=bs, r=rhs_buf: eng.matmul(
                                p[b + k][:],
                                wpool[g % NW][:, (k * 8 + h) * 128:(k * 8 + h + 1) * 128],
                                r[:, h * TC:(h + 1) * TC],
                                start=(h == 0), stop=(h == HT - 1)),
                                "sP" if ht == HT - 1 else None)
                    z_drain = zd[2]
                    w01_cons[gidx] = z_drain
                    w01_issue(gidx + NW)
                    par = j % 2
                    if j >= 2:
                        W("scalar", "sV", vj_out[j - 2])
                    W("scalar", "sP", zd[1])
                    I("scalar", lambda eng, pa=par, b=bs: eng.activation(
                        sgb[pa][:], p[b + 1][:], AF.Sigmoid))
                    I("scalar", lambda eng, pa=par, b=bs: eng.activation(
                        ab[pa][:], p[b + 1][:], AF.Sigmoid, scale=-1.0))
                    W("scalar", "sP", zd[0])
                    sh_done = I("scalar", lambda eng, pa=par, b=bs:
                                eng.activation(
                        shb[pa][:], p[b][:], AF.Sigmoid), "sA")
                    W("scalar", "sP", zd[2])
                    s_done = I("scalar", lambda eng, pa=par, b=bs: eng.activation(
                        gpb[pa][:], p[b + 2][:], AF.Sigmoid), "sA")
                    W("vector", "sA", sh_done)
                    W("vector", "sP", zd[0])
                    I("vector", lambda eng, pa=par, b=bs: eng.scalar_tensor_tensor(
                        gtb[:], p[b][:], 0.5, shb[pa][:], op0=ALU.add, op1=ALU.max))
                    I("vector", lambda eng, pa=par: eng.tensor_mul(
                        vb[:], sgb[pa][:], gtb[:]))
                    if c == 0:
                        I("vector", lambda eng, pa=par: eng.tensor_tensor_scan(
                            Hb[:], ab[pa][:], vb[:], 0.0, op0=ALU.mult, op1=ALU.add))
                    else:
                        I("vector", lambda eng, pa=par, jj=j, ca=carry:
                          eng.tensor_tensor_scan(
                            Hb[:], ab[pa][:], vb[:], ca[:, jj:jj + 1],
                            op0=ALU.mult, op1=ALU.add))
                    I("vector", lambda eng, jj=j, ca=carry: eng.tensor_copy(
                        ca[:, jj:jj + 1], Hb[:, TC - 1:TC]))
                    W("vector", "sA", s_done)
                    if ps == 0:
                        I("vector", lambda eng, jj=j, hb_=hb: eng.tensor_sub(
                            db[:], Hb[:], hb_[:, jj * TC:(jj + 1) * TC]))
                        I("vector", lambda eng, pa=par: eng.tensor_mul(
                            db[:], gpb[pa][:], db[:]))
                        vj_out[j] = I("vector", lambda eng, jj=j, hb_=hb:
                                      eng.tensor_add(
                            h1b[:, jj * TC:(jj + 1) * TC], db[:],
                            hb_[:, jj * TC:(jj + 1) * TC]), "sV")
                    else:
                        I("vector", lambda eng, jj=j: eng.tensor_sub(
                            db[:], Hb[:], h1b[:, jj * TC:(jj + 1) * TC]))
                        I("vector", lambda eng, pa=par: eng.tensor_mul(
                            db[:], gpb[pa][:], db[:]))
                        I("vector", lambda eng, jj=j: eng.tensor_add(
                            h2b[:], db[:], h1b[:, jj * TC:(jj + 1) * TC]))
                        I("vector", lambda eng, jj=j, x_=xcur: eng.tensor_add(
                            x_[:, jj * TC:(jj + 1) * TC],
                            x_[:, jj * TC:(jj + 1) * TC], h2b[:]))
                        vj_out[j] = I("vector", lambda eng, jj=j, x_=xcur,
                                      hb_=hb: eng.tensor_copy(
                            hb_[:, jj * TC:(jj + 1) * TC],
                            x_[:, jj * TC:(jj + 1) * TC]), "sV")
                    for k in range(3):
                        bank_free[bs + k] = vj_out[j]

            _chk(c, "a_ln1")
            mixer_pass(0, hb, carry1, vj1, dep1)
            _chk(c, "b_pass1")
            if DBG and c == 0:
                W("gpsimd", "sV", vj1[HT - 1])
                I("gpsimd", lambda eng: eng.dma_start(dbg_h0, h0b[:]), "dG0", 16)
                I("gpsimd", lambda eng: eng.dma_start(dbg_h1, h1b[:]), "dG0", 16)
                I("gpsimd", lambda eng: eng.dma_start(dbg_mrb, mrb[:]), "dG0", 16)
                I("gpsimd", lambda eng: eng.dma_start(dbg_csb, csbb[:]), "dG0", 16)
                I("gpsimd", lambda eng: eng.dma_start(dbg_bgb, bgbb[:]), "dG0", 16)
                I("gpsimd", lambda eng: eng.dma_start(dbg_wgt, swgt[:]), "dG0", 16)
                dbgv = I("gpsimd", lambda eng: eng.dma_start(dbg_rows, rows[:]),
                         "dG0", 16)
                W("vector", "dG0", dbgv)
                W("tensor", "dG0", dbgv)
            mixer_pass(1, h1b, carry2, vj2, vj1)
            if DBG and c == 0:
                W("gpsimd", "sV", vj2[HT - 1])
                I("gpsimd", lambda eng: eng.dma_start(dbg_x1, xbuf[0][:]),
                  "dG0", 16)
                dbgv2 = I("gpsimd", lambda eng: eng.dma_start(
                    dbg_x1b, h0b[:]), "dG0", 16)
                W("vector", "dG0", dbgv2)
                W("tensor", "dG0", dbgv2)
                W("scalar", "dG0", dbgv2)
            _chk(c, "c_pass2")

            ln2(c, vj2, hb)
            _chk(c, "d_ln2")

            W("tensor", "sV", bank_free[0])
            W("tensor", "sV", bank_free[1])
            lgT_done = None
            for ht in range(HT):
                lgT_done = I("tensor", lambda eng, h=ht, hb_=hb: eng.matmul(
                    p[0][0:NE, :], swgtb[:, h * NE:(h + 1) * NE],
                    hb_[:, h * TC:(h + 1) * TC],
                    start=(h == 0), stop=(h == HT - 1)),
                    "sP" if ht == HT - 1 else None)
            W("vector", "sP", lgT_done)
            ltg_ev = I("vector", lambda eng: eng.tensor_copy(
                ltg[:], p[0][:]), "sV")
            bank_free[0] = ltg_ev
            W("tensor", "sV", ltg_ev)
            glt_done = None
            for tt in range(4):
                glt_done = I("tensor", lambda eng, t_=tt: eng.transpose(
                    p[1][:, t_ * NE:(t_ + 1) * NE],
                    ltg[0:NE, t_ * 128:(t_ + 1) * 128], sid[0:NE, 0:NE]),
                    "sP" if tt == 3 else None)
            LG = gate[:, 0:32]
            EB = gate[:, 32:64]
            TMP = gate[:, 64:96]
            WU = gate[:, 96:128]
            V3 = lambda t: t.rearrange("p (t e) -> p t e", t=4)
            B4 = lambda c: c.rearrange("p (t o) -> p t o", o=1).to_broadcast(
                [128, 4, NE])
            m4 = m2c[:, 0:4]
            r4 = m2c[:, 4:8]
            W("vector", "sP", glt_done)
            VSYNC()
            VCH(lambda eng: eng.tensor_tensor(
                V3(TMP), V3(csbb[:]), B4(m4), op=ALU.mult))
            VCH(lambda eng: eng.tensor_sub(LG, p[1][:, 0:32], TMP))
            VCH(lambda eng: eng.tensor_tensor(
                V3(LG), V3(LG), B4(r4), op=ALU.mult))
            VCH(lambda eng: eng.tensor_add(LG, LG, bgbb[:]))
            VCH(lambda eng: eng.reduce_max(
                rows4[:, 0:4].rearrange("p (t o) -> p t o", o=1),
                V3(LG), axis=AX, negate=True))
            VCH(lambda eng: eng.tensor_tensor(
                V3(LG), V3(LG), B4(rows4[:, 0:4]), op=ALU.add))
            lgv = I("vector", lambda eng: eng.tensor_copy(
                ssink[:], rows4[:, 0:1]), "sV")
            W("scalar", "sV", lgv)
            I("scalar", lambda eng: eng.activation(EB, LG, AF.Exp))
            gexpv = I("scalar", lambda eng: eng.copy(
                epsc[0:1, :], epsc[0:1, :]), "sA")
            W("vector", "sA", gexpv)
            VCH(lambda eng: eng.reduce_max(
                rows4[:, 12:16].rearrange("p (t o) -> p t o", o=1),
                V3(EB), axis=AX))
            VCH(lambda eng: eng.reduce_sum(
                rows4[:, 4:8].rearrange("p (t o) -> p t o", o=1),
                V3(EB), axis=AX))
            VCH(lambda eng: eng.reciprocal(rows4[:, 8:12], rows4[:, 4:8]))
            VCH(lambda eng: eng.tensor_tensor(
                V3(TMP), V3(EB), B4(rows4[:, 12:16]), op=ALU.is_lt))
            VCH(lambda eng: eng.tensor_mul(TMP, EB, TMP))
            VCH(lambda eng: eng.reduce_max(
                rows4[:, 16:20].rearrange("p (t o) -> p t o", o=1),
                V3(TMP), axis=AX))
            VCH(lambda eng: eng.tensor_tensor(
                V3(TMP), V3(EB), B4(rows4[:, 16:20]), op=ALU.is_ge))
            VCH(lambda eng: eng.tensor_mul(WU, EB, TMP))
            VCH(lambda eng: eng.tensor_tensor(
                V3(wq[:]), V3(WU), B4(rows4[:, 8:12]), op=ALU.mult))
            gvq = I("vector", lambda eng: eng.tensor_copy(
                ssink[:], wq[:, 0:1]), "sV")
            bank_free[1] = gvq

            W("tensor", "sV", bank_free[1])
            wTp_done = None
            W("tensor", "sV", gvq)
            for tt in range(4):
                wTp_done = I("tensor", lambda eng, t_=tt: eng.transpose(
                    p[1][0:NE, t_ * 128:(t_ + 1) * 128],
                    wq[:, t_ * NE:(t_ + 1) * NE], sid[:]),
                    "sP" if tt == 3 else None)
            W("vector", "sP", wTp_done)
            swtb_done = I("vector", lambda eng: eng.tensor_copy(
                swTb[:], p[1][:]), "sV")
            bank_free[1] = swtb_done

            rd_next = (ln_stats(c + 1, False, None, None)
                       if c + 1 < NCH else None)

            wb_ev = [None] * NE
            for e in range(NE):
                bk = 2 + e % 2
                W("tensor", "sV", bank_free[bk])
                W("tensor", "sV", swtb_done)
                wbv = I("tensor", lambda eng, e_=e, b=bk: eng.matmul(
                    p[b][:], ssel[:, e_ * 128:(e_ + 1) * 128], swTb[0:NE, :],
                    start=True, stop=True), "sP")
                W("vector", "sP", wbv)
                wb_ev[e] = I("vector", lambda eng, e_=e, b=bk: eng.tensor_copy(
                    wbb[:, e_ * TC:(e_ + 1) * TC], p[b][:]), "sV")
                bank_free[bk] = wb_ev[e]

            if rd_next is not None:
                ln_bcast(rd_next)

            y_done = None
            for dt in range(HT):
                I("vector", lambda eng, d_=dt, hb_=hb: eng.tensor_sub(
                    t2t[:], hb_[:, d_ * TC:(d_ + 1) * TC], mrb[:, 0:TC]))
                y_done = I("vector", lambda eng, d_=dt: eng.tensor_mul(
                    h1b[:, d_ * TC:(d_ + 1) * TC], t2t[:], mrb[:, TC:2 * TC]), "sV")
            if rd_next is not None:
                ln1_copy()

            xe_done = [None] * NE

            def build_xe(e):
                for dt in range(HT):
                    v = I("vector", lambda eng, e_=e, d_=dt: eng.tensor_mul(
                        xeb[e_ % 2][:, d_ * TC:(d_ + 1) * TC],
                        h1b[:, d_ * TC:(d_ + 1) * TC],
                        wbb[:, e_ * TC:(e_ + 1) * TC]), "sV")
                xe_done[e] = v

            if DBG and c == 0:
                W("gpsimd", "sV", wb_ev[NE - 1])
                W("gpsimd", "sV", y_done)
                I("gpsimd", lambda eng: eng.dma_start(dbg_wq, wq[:]), "dG0", 16)
                I("gpsimd", lambda eng: eng.dma_start(dbg_swtb, swTb[:]),
                  "dG0", 16)
                I("gpsimd", lambda eng: eng.dma_start(dbg_wbb, wbb[:]), "dG0", 16)
                I("gpsimd", lambda eng: eng.dma_start(dbg_y, h1b[:]), "dG0", 16)
                I("gpsimd", lambda eng: eng.dma_start(dbg_stg, dbgstg[:]),
                  "dG0", 16)
                dbgv3 = I("gpsimd", lambda eng: eng.dma_start(
                    dbg_m2c, m2c[:]), "dG0", 16)
                W("vector", "dG0", dbgv3)
                W("tensor", "dG0", dbgv3)
            build_xe(0)
            _chk(c, "f_prep")

            if c == 0:
                for g in range(NWT):
                    wt_issue(g)
            for b in range(8):
                W("tensor", "sV", bank_free[b])
            W("tensor", "sV", wb_ev[NE - 1])
            W("tensor", "dB", bfv)
            for ot in range(8):
                I("tensor", lambda eng, o=ot: eng.matmul(
                    p[o][:], bfb[:, o * 128:(o + 1) * 128], swTb[0:NE, :],
                    start=True, stop=False))
            ffn_e = [None] * NE
            for e in range(NE):
                gwt = c * NE + e
                W("tensor", *wt_load[gwt])
                W("tensor", "sV", xe_done[e])
                for ot in range(8):
                    for dt in range(HT):
                        ffn_e[e] = I("tensor", lambda eng, e_=e, o=ot, d_=dt,
                                     g=gwt: eng.matmul(
                            p[o][:],
                            wtpool[g % NWT][:, (o * 8 + d_) * 128:(o * 8 + d_ + 1) * 128],
                            xeb[e_ % 2][:, d_ * TC:(d_ + 1) * TC],
                            start=False, stop=(e_ == NE - 1 and d_ == HT - 1)),
                            "sP" if (ot == 7 and dt == HT - 1) else None)
                wt_cons[gwt] = ffn_e[e]
                wt_issue(gwt + NWT)
                if e + 1 < NE:
                    if e >= 1:
                        W("vector", "sP", ffn_e[e - 1])
                    build_xe(e + 1)
                    if rd_next is not None and e < 4:
                        h0_build(c + 1, [2 * e, 2 * e + 1])

            for ot in (6, 7, 0, 1, 2, 3, 4, 5):
                W("vector", "sP", ffn_e[NE - 1])
                ev = I("vector", lambda eng, o=ot, x_=xcur: eng.tensor_add(
                    x_[:, o * TC:(o + 1) * TC], p[o][:],
                    x_[:, o * TC:(o + 1) * TC]), "sV")
                bank_free[ot] = ev
                W("scalar", "sV", ev)
                dst = oT.rearrange("(ot p) t -> p ot t", p=128)[
                    :, ot:ot + 1, c * TC:(c + 1) * TC]
                sm = f"dG{ot % 2}"
                st = (sm, I("scalar", lambda eng, d=dst, o=ot, x_=xcur:
                            eng.dma_start(
                    d, x_[:, o * TC:(o + 1) * TC].rearrange(
                        "p (a t) -> p a t", a=1)), sm, 16))
                st_hist.append(st)
                outdma[0] += 1

            _chk(c, "g_chunk")
            if c + 2 < NCH:
                W("sync", "sV", bank_free[5])
                W("sync", *st_hist[-1])
                W("sync", *st_hist[-2])
                xv[c + 2] = x_load(c + 2)

    except _StopEmit:
        pass

    W("scalar", "dG0", CNT["dG0"])
    W("scalar", "dG1", CNT["dG1"])

    with nc.Block() as blk:
        @blk.sync
        def _(eng):
            for f in OPS["sync"]:
                f(eng)

        @blk.tensor
        def _(eng):
            for f in OPS["tensor"]:
                f(eng)

        @blk.vector
        def _(eng):
            for f in OPS["vector"]:
                f(eng)

        @blk.scalar
        def _(eng):
            for f in OPS["scalar"]:
                f(eng)

        @blk.gpsimd
        def _(eng):
            for f in OPS["gpsimd"]:
                f(eng)

    es.close()
    return nc


def kernel(L0_attn_h, L0_attn_W0, L0_attn_W1, L0_ffn_W, L0_ffn_b,
           L0_ffn_Wg, L0_ffn_bg):
    if "nc" not in _CACHE:
        _CACHE["nc"] = build()
    nc = _CACHE["nc"]

    f32 = lambda a: np.ascontiguousarray(np.asarray(a, dtype=np.float32))
    xh = f32(L0_attn_h)
    W0 = f32(L0_attn_W0)
    W1 = f32(L0_attn_W1)
    WT = np.ascontiguousarray(f32(L0_ffn_W).transpose(0, 2, 1))
    BF = f32(L0_ffn_b)
    WG = np.ascontiguousarray(f32(L0_ffn_Wg).T)
    BG = f32(L0_ffn_bg).reshape(1, NE)
    EYE = np.eye(128, dtype=np.float32)
    SEL = np.zeros((NE, NE * 128), dtype=ml_dtypes.bfloat16)
    for e in range(NE):
        SEL[e, e * 128:(e + 1) * 128] = 1
    shared = dict(W0=W0, W1=W1, WT=WT, BF=BF, WGT=WG, BG=BG, EYE=EYE, SEL=SEL)
    in_maps = []
    for b in range(8):
        m = dict(shared)
        m["xT"] = np.ascontiguousarray(xh[b].T)
        in_maps.append(m)
    res = run_bass_kernel_spmd(nc, in_maps, core_ids=list(range(8)))
    out = np.stack([np.asarray(r["outT"]).T for r in res.results], axis=0)
    return np.ascontiguousarray(out, dtype=np.float32)


# version 21
# speedup vs baseline: 1.4789x; 1.0039x over previous
import numpy as np
import ml_dtypes
from contextlib import ExitStack
import concourse.bass as bass
import concourse.mybir as mybir
from concourse.bass_utils import run_bass_kernel_spmd

F32 = mybir.dt.float32
BF16 = mybir.dt.bfloat16
AF = mybir.ActivationFunctionType
ALU = mybir.AluOpType
AX = mybir.AxisListType.X

H = 1024
T = 4096
TC = 512
NCH = T // TC
HT = H // 128
NE = 8
NW = 5
NWT = 2

_CACHE = {}
DBG = False
STREAM_Q = "sync"
SKIP_STREAM_WAITS = False
PROBE_DUMMY_SRC = False
PROBE_TAIL = False
PROBE_SKIP_GATE = False
DBG_SCR = False
PROBE_GATE_LVL = 5
PROBE_SKIP_X = False
STOP = None


class _StopEmit(Exception):
    pass


def build():
    nc = bass.Bass("TRN2", target_bir_lowering=False, debug=False,
                   detect_race_conditions=False)

    ixT = nc.dram_tensor("xT", [H, T], F32, kind="ExternalInput").ap()
    iW0 = nc.dram_tensor("W0", [H, 3 * H], F32, kind="ExternalInput").ap()
    iW1 = nc.dram_tensor("W1", [H, 3 * H], F32, kind="ExternalInput").ap()
    iWT = nc.dram_tensor("WT", [NE, H, H], F32, kind="ExternalInput").ap()
    iBF = nc.dram_tensor("BF", [NE, H], F32, kind="ExternalInput").ap()
    iWG = nc.dram_tensor("WGT", [H, NE], F32, kind="ExternalInput").ap()
    iBG = nc.dram_tensor("BG", [1, NE], F32, kind="ExternalInput").ap()
    iEYE = nc.dram_tensor("EYE", [128, 128], F32, kind="ExternalInput").ap()
    iSEL = nc.dram_tensor("SEL", [NE, NE * 128], BF16, kind="ExternalInput").ap()
    oT = nc.dram_tensor("outT", [H, T], F32, kind="ExternalOutput").ap()
    if DBG:
        dbg_h0 = nc.dram_tensor("dbg_h0", [128, HT * TC], BF16,
                                kind="ExternalOutput").ap()
        dbg_h1 = nc.dram_tensor("dbg_h1", [128, HT * TC], BF16,
                                kind="ExternalOutput").ap()
        dbg_mrb = nc.dram_tensor("dbg_mrb", [128, 2 * TC], BF16,
                                 kind="ExternalOutput").ap()
        dbg_rows = nc.dram_tensor("dbg_rows", [1, 3 * TC], F32,
                                  kind="ExternalOutput").ap()
        dbg_x1 = nc.dram_tensor("dbg_x1", [128, HT * TC], F32,
                                kind="ExternalOutput").ap()
        dbg_x1b = nc.dram_tensor("dbg_x1b", [128, HT * TC], BF16,
                                 kind="ExternalOutput").ap()
        dbg_wq = nc.dram_tensor("dbg_wq", [128, 4 * NE], F32,
                                kind="ExternalOutput").ap()
        dbg_swtb = nc.dram_tensor("dbg_swtb", [NE, TC], BF16,
                                  kind="ExternalOutput").ap()
        dbg_wbb = nc.dram_tensor("dbg_wbb", [128, NE * TC], BF16,
                                 kind="ExternalOutput").ap()
        dbg_y = nc.dram_tensor("dbg_y", [128, HT * TC], BF16,
                               kind="ExternalOutput").ap()
        dbg_m2c = nc.dram_tensor("dbg_m2c", [128, 8], F32,
                                 kind="ExternalOutput").ap()
        dbg_csb = nc.dram_tensor("dbg_csb", [128, NE], F32,
                                 kind="ExternalOutput").ap()
        dbg_bgb = nc.dram_tensor("dbg_bgb", [128, NE], F32,
                                 kind="ExternalOutput").ap()
        dbg_wgt = nc.dram_tensor("dbg_wgt", [128, HT * NE], F32,
                                 kind="ExternalOutput").ap()
        dbg_stg = nc.dram_tensor("dbg_stg", [128, 70], F32,
                                 kind="ExternalOutput").ap()
    sW0 = nc.dram_tensor("scrW0", [8, 128, 3 * H], BF16, kind="Internal").ap()
    sW1 = nc.dram_tensor("scrW1", [8, 128, 3 * H], BF16, kind="Internal").ap()
    sWT = nc.dram_tensor("scrWT", [NE, 128, NE * H], BF16, kind="Internal").ap()

    es = ExitStack()
    SB = lambda n, s, d: es.enter_context(nc.sbuf_tensor(n, s, d))
    PS = lambda n, s: es.enter_context(nc.psum_tensor(n, s, F32))
    SEM = lambda n: es.enter_context(nc.semaphore(n))

    wpool = [SB(f"wp{i}", [128, 3 * H], BF16) for i in range(NW)]
    wtpool = [SB(f"wt{i}", [128, NE * H], BF16) for i in range(NWT)]
    xbuf = [SB(f"x{i}", [128, HT * TC], F32) for i in range(2)]
    h0b = SB("h0", [128, HT * TC], BF16)
    h0b2 = SB("h0x", [128, HT * TC], BF16)
    h1b = SB("h1", [128, HT * TC], BF16)
    xeb = [SB(f"xe{i}", [128, HT * TC], BF16) for i in range(2)]
    wbb = SB("wb", [128, NE * TC], BF16)
    sgb = [SB(f"sg{i}", [128, TC], F32) for i in range(2)]
    ab = [SB(f"a{i}", [128, TC], F32) for i in range(2)]
    shb = [SB(f"sh{i}", [128, TC], F32) for i in range(2)]
    gpb = [SB(f"gp{i}", [128, TC], F32) for i in range(2)]
    gtb = SB("gt", [128, TC], F32)
    vb = SB("v", [128, TC], F32)
    Hb = SB("Hs", [128, TC], F32)
    db = SB("d", [128, TC], F32)
    h2b = SB("h2", [128, TC], F32)
    t2t = SB("t2t", [128, TC], BF16)
    xsq = [SB(f"xsq{i}", [128, TC], BF16) for i in range(2)]
    rows = SB("rows", [1, 3 * TC], F32)
    mrb = SB("mrb", [128, 2 * TC], BF16)
    mrb2 = SB("mrb2", [128, 2 * TC], BF16)
    carry1 = SB("c1s", [128, HT], F32)
    carry2 = SB("c2s", [128, HT], F32)
    sid = SB("sid", [128, 128], F32)
    ssel = SB("ssel", [NE, NE * 128], BF16)
    swgt = SB("swgt", [128, HT * NE], F32)
    swgtb = SB("swgtb", [128, HT * NE], BF16)
    ltg = SB("ltg", [128, TC], F32)
    sbg = SB("sbg", [1, NE], F32)
    bgbb = SB("bgb", [128, 4 * NE], F32)
    csbb = SB("csb", [128, 4 * NE], F32)
    csrow = SB("csr", [1, NE], F32)
    cgrow = SB("cgrow", [1, 8 * NE], F32)
    bfb = SB("bfb", [NE, H], BF16)
    swTb = SB("swTb", [128, TC], BF16)
    gate = SB("gate", [128, 16 * NE], F32)
    rows4 = SB("rows4", [128, 20], F32)
    wq = SB("wq", [128, 4 * NE], F32)
    m2c = SB("m2c", [128, 8], F32)
    onec = SB("onec", [128, 1], F32)
    onecb = SB("onecb", [128, 1], BF16)
    oner = SB("oner", [1, 128], F32)
    epsc = SB("epsc", [1, 1], F32)
    ssink = SB("ssink", [128, 1], F32)
    dbgstg = SB("dbgstg", [128, 70], F32) if DBG else None

    p = [PS(f"p{i}", [128, TC]) for i in range(8)]

    semnames = (["sP", "sV", "sA", "dC", "dB", "dSt0", "dStT", "dSt1"]
                + [f"dX{i}" for i in range(2)]
                + [f"dWp{i}" for i in range(NW)]
                + [f"dWT{i}" for i in range(NWT)]
                + [f"dF{i}" for i in range(2)]
                + [f"dG{i}" for i in range(2)])
    sems = {k: SEM(k) for k in semnames}

    OPS = {k: [] for k in ("sync", "tensor", "vector", "scalar", "gpsimd")}
    CNT = {k: 0 for k in sems}
    LASTW = {}

    def W(st, sem, val):
        if not val:
            return
        key = (st, sem)
        if LASTW.get(key, 0) >= val:
            return
        LASTW[key] = val
        OPS[st].append(lambda eng, s=sems[sem], v=val: eng.wait_ge(s, v))

    def I(st, fn, sem=None, n=1):
        if sem is None:
            OPS[st].append(fn)
            return None
        CNT[sem] += n
        v = CNT[sem]
        OPS[st].append(lambda eng, f=fn, s=sems[sem], k=n: f(eng).then_inc(s, k))
        return v

    SEM_OF = {"vector": "sV", "scalar": "sA", "tensor": "sP"}

    def VCH(fn):
        v = I("vector", fn, "sV")
        W("vector", "sV", v)
        return v

    def VSYNC():
        W("vector", "sV", CNT["sV"])

    xv = [None] * NCH

    def x_load(c):
        dst = xbuf[c % 2][:].rearrange("p (ht t) -> p ht t", ht=HT)
        src = ixT.rearrange("(ht p) t -> p ht t", p=128)[:, :, c * TC:(c + 1) * TC]
        sm = f"dX{c % 2}"
        return (sm, I("sync", lambda eng, d=dst, s=src: eng.dma_start(d, s),
                      sm, 16))

    xv[0] = x_load(0)
    xv[1] = x_load(1)
    I("sync", lambda eng: eng.dma_start(sid[:], iEYE), "dC", 16)
    I("sync", lambda eng: eng.dma_start(ssel[:], iSEL), "dC", 16)
    I("sync", lambda eng: eng.dma_start(
        swgt[:].rearrange("p (ht e) -> p ht e", ht=HT),
        iWG.rearrange("(ht p) e -> p ht e", p=128)), "dC", 16)
    consts_v = I("sync", lambda eng: eng.dma_start(sbg[:], iBG), "dC", 16)
    bfv = I("gpsimd", lambda eng: eng.dma_start(bfb[:], iBF), "dB", 16)

    I("vector", lambda eng: eng.memset(onec[:], 1.0))
    I("vector", lambda eng: eng.memset(onecb[:], 1.0))
    I("vector", lambda eng: eng.memset(oner[:], 1.0))
    W("vector", "dC", consts_v)
    I("vector", lambda eng: eng.tensor_copy(swgtb[:], swgt[:]))
    bfcast = I("vector", lambda eng: eng.memset(epsc[:], 1e-5), "sV")

    W("tensor", "dC", consts_v)
    W("tensor", "sV", bfcast)
    if not PROBE_SKIP_GATE:
        L = PROBE_GATE_LVL
        if L >= 1:
            for ht in range(HT):
                I("tensor", lambda eng, h=ht: eng.matmul(
                    p[0][0:1, 0:NE], onec[:], swgt[:, h * NE:(h + 1) * NE],
                    start=(h == 0), stop=(h == HT - 1)))
        if L >= 2:
            pv = I("tensor", lambda eng: eng.matmul(
                p[1][:, 0:NE], oner[:], sbg[:], start=True, stop=True), "sP")
        if L >= 3:
            W("vector", "sP", pv)
            VCH(lambda eng: eng.tensor_copy(csrow[:], p[0][0:1, 0:NE]))
            csv = I("vector", lambda eng: eng.tensor_copy(
                ssink[0:1, :], csrow[0:1, 0:1]), "sV")
        if L >= 4:
            W("vector", "sP", pv)
            for i in range(4):
                VCH(lambda eng, i_=i: eng.tensor_copy(
                    cgrow[0:1, i_ * NE:(i_ + 1) * NE], csrow[:]))
                VCH(lambda eng, i_=i: eng.tensor_copy(
                    cgrow[0:1, 32 + i_ * NE:32 + (i_ + 1) * NE],
                    p[1][0:1, 0:NE]))
            csv2 = I("vector", lambda eng: eng.tensor_copy(
                ssink[0:1, :], cgrow[0:1, 0:1]), "sV")
            W("tensor", "sV", csv2)
            pv2 = I("tensor", lambda eng: eng.matmul(
                p[0][:, 0:8 * NE], oner[:], cgrow[:],
                start=True, stop=True), "sP")
        if L >= 5:
            W("vector", "sP", pv2)
            I("vector", lambda eng: eng.tensor_copy(csbb[:], p[0][:, 0:32]))
            I("vector", lambda eng: eng.tensor_copy(bgbb[:], p[0][:, 32:64]),
              "sV")

    def _tail():
        if PROBE_TAIL:
            for st in OPS:
                for _ in range(100):
                    OPS[st].append(lambda eng: eng.nop(cycle_cnt=10000))

    def _chk0(tag):
        if STOP is not None and (-1, tag) >= STOP:
            _tail()
            raise _StopEmit

    for jg in range(8):
        W("gpsimd", "dSt0", 16 * jg)
        I("gpsimd", lambda eng, j=jg: eng.dma_start(
            sW0[j].rearrange("p (ki kj ht c) -> p ki kj ht c",
                             ki=3, kj=1, c=128),
            iW0.rearrange("(ht p) (ki kj c) -> p ki kj ht c",
                          p=128, ki=3, c=128)[:, :, j:j + 1, :, :]),
          "dSt0", 16)
    for jg in range(8):
        W("gpsimd", "dSt0", 128)
        W("gpsimd", "dSt1", 16 * jg)
        I("gpsimd", lambda eng, j=jg: eng.dma_start(
            sW1[j].rearrange("p (ki kj ht c) -> p ki kj ht c",
                             ki=3, kj=1, c=128),
            iW1.rearrange("(ht p) (ki kj c) -> p ki kj ht c",
                          p=128, ki=3, c=128)[:, :, j:j + 1, :, :]),
          "dSt1", 16)
    for eg in range(NE):
        W("gpsimd", "dSt1", 128)
        W("gpsimd", "dStT", 16 * eg)
        I("gpsimd", lambda eng, e_=eg: eng.dma_start(
            sWT[e_].rearrange("p (ot dt c) -> p ot dt c", ot=8, c=128),
            iWT[e_].rearrange("(dt p) (ot c) -> p ot dt c", p=128, c=128)),
          "dStT", 16)

    NG01 = NCH * 16
    w01_load = [None] * NG01
    w01_cons = [None] * NG01

    def w01_issue(gidx):
        if gidx >= NG01:
            return
        ps, j = divmod(gidx % 16, 8)
        W("sync", "dSt1" if ps else "dSt0", 16 * (j + 1))
        if gidx >= NW:
            W("sync", "sP", w01_cons[gidx - NW])
        sm = f"dWp{gidx % NW}"
        scr = sW1 if ps else sW0
        w01_load[gidx] = (sm, I("sync", lambda eng, s=scr, j_=j,
                                sl=wpool[gidx % NW]:
                                eng.dma_start(sl[:], s[j_]), sm, 16))

    NGWT = NCH * NE
    wt_load = [None] * NGWT
    wt_cons = [None] * NGWT

    def wt_issue(gidx):
        if gidx >= NGWT:
            return
        e = gidx % NE
        W("scalar", "dStT", 16 * (e + 1))
        if gidx >= NWT:
            W("scalar", "sP", wt_cons[gidx - NWT])
        sm = f"dWT{gidx % NWT}"
        wt_load[gidx] = (sm, I("scalar", lambda eng, e_=e,
                               sl=wtpool[gidx % NWT]:
                               eng.dma_start(sl[:], sWT[e_]), sm, 16))

    for g in range(NW):
        w01_issue(g)

    bank_free = [0] * 8
    ln_bcast_done = [0]
    ssq_mm_prev = [None] * HT
    st_hist = []
    outdma = [0]
    h0bufs = [h0b, h0b2]
    h0d_cur = [None] * HT
    try:

        def ln_stats(c, second, vj2, from_buf):
            xcur = xbuf[c % 2]
            W("tensor", "sV", bank_free[6])
            W("tensor", "sV", bank_free[7])
            if not second:
                W("tensor", *xv[c])
            for ht in range(HT):
                if second:
                    W("tensor", "sV", vj2[ht])
                    I("tensor", lambda eng, h=ht, b_=from_buf: eng.matmul(
                        p[6][0:1, :], onecb[:], b_[:, h * TC:(h + 1) * TC],
                        start=(h == 0), stop=(h == HT - 1)))
                else:
                    I("tensor", lambda eng, h=ht, x_=xcur: eng.matmul(
                        p[6][0:1, :], onec[:], x_[:, h * TC:(h + 1) * TC],
                        start=(h == 0), stop=(h == HT - 1)))
            ssq_cur = [None] * HT
            for ht in range(HT):
                if second:
                    W("scalar", "sV", vj2[ht])
                else:
                    W("scalar", *xv[c])
                W("scalar", "sP", ssq_cur[ht - 2] if ht >= 2 else ssq_mm_prev[ht])
                if second:
                    sq = I("scalar", lambda eng, h=ht, b_=from_buf:
                           eng.activation(
                        xsq[h % 2][:], b_[:, h * TC:(h + 1) * TC], AF.Square),
                        "sA")
                else:
                    sq = I("scalar", lambda eng, h=ht, x_=xcur: eng.activation(
                        xsq[h % 2][:], x_[:, h * TC:(h + 1) * TC], AF.Square),
                        "sA")
                W("tensor", "sA", sq)
                ssq_cur[ht] = I("tensor", lambda eng, h=ht: eng.matmul(
                    p[7][0:1, :], onecb[:], xsq[h % 2][:],
                    start=(h == 0), stop=(h == HT - 1)), "sP")
            for ht in range(HT):
                ssq_mm_prev[ht] = ssq_cur[ht]
            stats_done = ssq_cur[HT - 1]
            mean = rows[0:1, 0:TC]
            var = rows[0:1, TC:2 * TC]
            rstd = rows[0:1, 2 * TC:3 * TC]
            W("vector", "sP", stats_done)
            W("vector", "sP", ln_bcast_done[0])
            I("vector", lambda eng: eng.tensor_scalar(
                mean, p[6][0:1, :], 1.0 / H, None, op0=ALU.mult))
            I("vector", lambda eng: eng.tensor_mul(var, mean, mean))
            I("vector", lambda eng: eng.tensor_scalar(
                rstd, p[7][0:1, :], 1.0 / H, None, op0=ALU.mult))
            var_done = I("vector", lambda eng: eng.tensor_sub(var, rstd, var), "sV")
            W("scalar", "sV", var_done)
            I("scalar", lambda eng: eng.activation(rstd, var, AF.Ln,
                                                   bias=epsc[0:1, :]))
            rstd_done = I("scalar", lambda eng: eng.activation(
                rstd, rstd, AF.Exp, scale=-0.5), "sA")
            return rstd_done

        def ln_bcast(rstd_done):
            mean = rows[0:1, 0:TC]
            rstd = rows[0:1, 2 * TC:3 * TC]
            W("tensor", "sA", rstd_done)
            I("tensor", lambda eng: eng.matmul(
                p[6][:], oner[:], mean, start=True, stop=True))
            ln_bcast_done[0] = I("tensor", lambda eng: eng.matmul(
                p[7][:], oner[:], rstd, start=True, stop=True), "sP")

        def ln1_copy():
            W("vector", "sP", ln_bcast_done[0])
            I("vector", lambda eng: eng.tensor_copy(mrb2[:, 0:TC], p[6][:]))
            bf67 = I("vector", lambda eng: eng.tensor_copy(
                mrb2[:, TC:2 * TC], p[7][:]), "sV")
            bank_free[6] = bank_free[7] = bf67

        def h0_build(c, hts):
            hbn = h0bufs[c % 2]
            xcur = xbuf[c % 2]
            W("vector", *xv[c])
            for ht in hts:
                I("vector", lambda eng, h=ht, x_=xcur: eng.tensor_sub(
                    t2t[:], x_[:, h * TC:(h + 1) * TC], mrb2[:, 0:TC]))
                h0d_cur[ht] = I("vector", lambda eng, h=ht, hb_=hbn:
                                eng.tensor_mul(
                    hb_[:, h * TC:(h + 1) * TC], t2t[:],
                    mrb2[:, TC:2 * TC]), "sV")

        def ln2(c, vj2, hb):
            rstd_done = ln_stats(c, True, vj2, hb)
            mean = rows[0:1, 0:TC]
            rstd = rows[0:1, 2 * TC:3 * TC]
            W("tensor", "sV", bank_free[5])
            W("tensor", "sA", rstd_done)
            for i in range(4):
                I("tensor", lambda eng, i_=i: eng.transpose(
                    p[5][:, i_:i_ + 1], mean[:, i_ * 128:(i_ + 1) * 128],
                    sid[0:1, 0:1]))
            tcol_done = None
            for i in range(4):
                tcol_done = I("tensor", lambda eng, i_=i: eng.transpose(
                    p[5][:, 4 + i_:5 + i_], rstd[:, i_ * 128:(i_ + 1) * 128],
                    sid[0:1, 0:1]), "sP" if i == 3 else None)
            W("vector", "sP", tcol_done)
            bank_free[5] = VCH(lambda eng: eng.tensor_copy(
                m2c[:], p[5][:, 0:8]))
            ln_bcast(rstd_done)
            W("vector", "sP", ln_bcast_done[0])
            I("vector", lambda eng: eng.tensor_copy(mrb[:, 0:TC], p[6][:]))
            bf67 = I("vector", lambda eng: eng.tensor_copy(
                mrb[:, TC:2 * TC], p[7][:]), "sV")
            bank_free[6] = bank_free[7] = bf67

        def _chk(c, tag):
            if STOP is not None and (c, tag) >= STOP:
                raise _StopEmit

        rd0 = ln_stats(0, False, None, None)
        ln_bcast(rd0)
        ln1_copy()
        h0_build(0, range(HT))

        for c in range(NCH):
            xcur = xbuf[c % 2]
            hb = h0bufs[c % 2]
            vj1 = [None] * HT
            vj2 = [None] * HT
            dep1 = list(h0d_cur)

            def mixer_pass(ps, rhs_buf, carry, vj_out, dep_list):
                for j in range(HT):
                    gidx = c * 16 + ps * 8 + j
                    bs = 3 * (j % 2)
                    W("tensor", *w01_load[gidx])
                    if j >= 2:
                        W("tensor", "sV", vj_out[j - 2])
                    else:
                        for b in (bs, bs + 1, bs + 2):
                            W("tensor", "sV", bank_free[b])
                    if j > 0:
                        W("tensor", "sV", dep_list[HT - 1])
                    zd = [None] * 3
                    for ki in (1, 0, 2):
                        for ht in range(HT):
                            if j == 0 and ki == 1:
                                W("tensor", "sV", dep_list[ht])
                            zd[ki] = I("tensor", lambda eng, k=ki, h=ht,
                                       g=gidx, b=bs, r=rhs_buf: eng.matmul(
                                p[b + k][:],
                                wpool[g % NW][:, (k * 8 + h) * 128:(k * 8 + h + 1) * 128],
                                r[:, h * TC:(h + 1) * TC],
                                start=(h == 0), stop=(h == HT - 1)),
                                "sP" if ht == HT - 1 else None)
                    z_drain = zd[2]
                    w01_cons[gidx] = z_drain
                    w01_issue(gidx + NW)
                    par = j % 2
                    if j >= 2:
                        W("scalar", "sV", vj_out[j - 2])
                    W("scalar", "sP", zd[1])
                    I("scalar", lambda eng, pa=par, b=bs: eng.activation(
                        sgb[pa][:], p[b + 1][:], AF.Sigmoid))
                    I("scalar", lambda eng, pa=par, b=bs: eng.activation(
                        ab[pa][:], p[b + 1][:], AF.Sigmoid, scale=-1.0))
                    W("scalar", "sP", zd[0])
                    sh_done = I("scalar", lambda eng, pa=par, b=bs:
                                eng.activation(
                        shb[pa][:], p[b][:], AF.Sigmoid), "sA")
                    W("scalar", "sP", zd[2])
                    s_done = I("scalar", lambda eng, pa=par, b=bs: eng.activation(
                        gpb[pa][:], p[b + 2][:], AF.Sigmoid), "sA")
                    W("vector", "sA", sh_done)
                    W("vector", "sP", zd[0])
                    I("vector", lambda eng, pa=par, b=bs: eng.scalar_tensor_tensor(
                        gtb[:], p[b][:], 0.5, shb[pa][:], op0=ALU.add, op1=ALU.max))
                    I("vector", lambda eng, pa=par: eng.tensor_mul(
                        vb[:], sgb[pa][:], gtb[:]))
                    if c == 0:
                        I("vector", lambda eng, pa=par: eng.tensor_tensor_scan(
                            Hb[:], ab[pa][:], vb[:], 0.0, op0=ALU.mult, op1=ALU.add))
                    else:
                        I("vector", lambda eng, pa=par, jj=j, ca=carry:
                          eng.tensor_tensor_scan(
                            Hb[:], ab[pa][:], vb[:], ca[:, jj:jj + 1],
                            op0=ALU.mult, op1=ALU.add))
                    I("vector", lambda eng, jj=j, ca=carry: eng.tensor_copy(
                        ca[:, jj:jj + 1], Hb[:, TC - 1:TC]))
                    W("vector", "sA", s_done)
                    if ps == 0:
                        I("vector", lambda eng, jj=j, hb_=hb: eng.tensor_sub(
                            db[:], Hb[:], hb_[:, jj * TC:(jj + 1) * TC]))
                        I("vector", lambda eng, pa=par: eng.tensor_mul(
                            db[:], gpb[pa][:], db[:]))
                        vj_out[j] = I("vector", lambda eng, jj=j, hb_=hb:
                                      eng.tensor_add(
                            h1b[:, jj * TC:(jj + 1) * TC], db[:],
                            hb_[:, jj * TC:(jj + 1) * TC]), "sV")
                    else:
                        I("vector", lambda eng, jj=j: eng.tensor_sub(
                            db[:], Hb[:], h1b[:, jj * TC:(jj + 1) * TC]))
                        I("vector", lambda eng, pa=par: eng.tensor_mul(
                            db[:], gpb[pa][:], db[:]))
                        I("vector", lambda eng, jj=j: eng.tensor_add(
                            h2b[:], db[:], h1b[:, jj * TC:(jj + 1) * TC]))
                        I("vector", lambda eng, jj=j, x_=xcur: eng.tensor_add(
                            x_[:, jj * TC:(jj + 1) * TC],
                            x_[:, jj * TC:(jj + 1) * TC], h2b[:]))
                        vj_out[j] = I("vector", lambda eng, jj=j, x_=xcur,
                                      hb_=hb: eng.tensor_copy(
                            hb_[:, jj * TC:(jj + 1) * TC],
                            x_[:, jj * TC:(jj + 1) * TC]), "sV")
                    for k in range(3):
                        bank_free[bs + k] = vj_out[j]

            _chk(c, "a_ln1")
            mixer_pass(0, hb, carry1, vj1, dep1)
            _chk(c, "b_pass1")
            if DBG and c == 0:
                W("gpsimd", "sV", vj1[HT - 1])
                I("gpsimd", lambda eng: eng.dma_start(dbg_h0, h0b[:]), "dG0", 16)
                I("gpsimd", lambda eng: eng.dma_start(dbg_h1, h1b[:]), "dG0", 16)
                I("gpsimd", lambda eng: eng.dma_start(dbg_mrb, mrb[:]), "dG0", 16)
                I("gpsimd", lambda eng: eng.dma_start(dbg_csb, csbb[:]), "dG0", 16)
                I("gpsimd", lambda eng: eng.dma_start(dbg_bgb, bgbb[:]), "dG0", 16)
                I("gpsimd", lambda eng: eng.dma_start(dbg_wgt, swgt[:]), "dG0", 16)
                dbgv = I("gpsimd", lambda eng: eng.dma_start(dbg_rows, rows[:]),
                         "dG0", 16)
                W("vector", "dG0", dbgv)
                W("tensor", "dG0", dbgv)
            mixer_pass(1, h1b, carry2, vj2, vj1)
            if DBG and c == 0:
                W("gpsimd", "sV", vj2[HT - 1])
                I("gpsimd", lambda eng: eng.dma_start(dbg_x1, xbuf[0][:]),
                  "dG0", 16)
                dbgv2 = I("gpsimd", lambda eng: eng.dma_start(
                    dbg_x1b, h0b[:]), "dG0", 16)
                W("vector", "dG0", dbgv2)
                W("tensor", "dG0", dbgv2)
                W("scalar", "dG0", dbgv2)
            _chk(c, "c_pass2")

            W("tensor", "sV", bank_free[0])
            W("tensor", "sV", bank_free[1])
            lgT_done = None
            for ht in range(HT):
                W("tensor", "sV", vj2[ht])
                lgT_done = I("tensor", lambda eng, h=ht, hb_=hb: eng.matmul(
                    p[0][0:NE, :], swgtb[:, h * NE:(h + 1) * NE],
                    hb_[:, h * TC:(h + 1) * TC],
                    start=(h == 0), stop=(h == HT - 1)),
                    "sP" if ht == HT - 1 else None)
            W("vector", "sP", lgT_done)
            ltg_ev = I("vector", lambda eng: eng.tensor_copy(
                ltg[:], p[0][:]), "sV")
            bank_free[0] = ltg_ev
            W("tensor", "sV", ltg_ev)
            glt_done = None
            for tt in range(4):
                glt_done = I("tensor", lambda eng, t_=tt: eng.transpose(
                    p[1][:, t_ * NE:(t_ + 1) * NE],
                    ltg[0:NE, t_ * 128:(t_ + 1) * 128], sid[0:NE, 0:NE]),
                    "sP" if tt == 3 else None)

            ln2(c, vj2, hb)
            _chk(c, "d_ln2")

            LG = gate[:, 0:32]
            EB = gate[:, 32:64]
            TMP = gate[:, 64:96]
            WU = gate[:, 96:128]
            V3 = lambda t: t.rearrange("p (t e) -> p t e", t=4)
            B4 = lambda c: c.rearrange("p (t o) -> p t o", o=1).to_broadcast(
                [128, 4, NE])
            m4 = m2c[:, 0:4]
            r4 = m2c[:, 4:8]
            W("vector", "sP", glt_done)
            VSYNC()
            VCH(lambda eng: eng.tensor_tensor(
                V3(TMP), V3(csbb[:]), B4(m4), op=ALU.mult))
            VCH(lambda eng: eng.tensor_sub(LG, p[1][:, 0:32], TMP))
            VCH(lambda eng: eng.tensor_tensor(
                V3(LG), V3(LG), B4(r4), op=ALU.mult))
            VCH(lambda eng: eng.tensor_add(LG, LG, bgbb[:]))
            VCH(lambda eng: eng.reduce_max(
                rows4[:, 0:4].rearrange("p (t o) -> p t o", o=1),
                V3(LG), axis=AX, negate=True))
            VCH(lambda eng: eng.tensor_tensor(
                V3(LG), V3(LG), B4(rows4[:, 0:4]), op=ALU.add))
            lgv = I("vector", lambda eng: eng.tensor_copy(
                ssink[:], rows4[:, 0:1]), "sV")
            W("scalar", "sV", lgv)
            I("scalar", lambda eng: eng.activation(EB, LG, AF.Exp))
            gexpv = I("scalar", lambda eng: eng.copy(
                epsc[0:1, :], epsc[0:1, :]), "sA")
            W("vector", "sA", gexpv)
            VCH(lambda eng: eng.reduce_max(
                rows4[:, 12:16].rearrange("p (t o) -> p t o", o=1),
                V3(EB), axis=AX))
            VCH(lambda eng: eng.reduce_sum(
                rows4[:, 4:8].rearrange("p (t o) -> p t o", o=1),
                V3(EB), axis=AX))
            VCH(lambda eng: eng.reciprocal(rows4[:, 8:12], rows4[:, 4:8]))
            VCH(lambda eng: eng.tensor_tensor(
                V3(TMP), V3(EB), B4(rows4[:, 12:16]), op=ALU.is_lt))
            VCH(lambda eng: eng.tensor_mul(TMP, EB, TMP))
            VCH(lambda eng: eng.reduce_max(
                rows4[:, 16:20].rearrange("p (t o) -> p t o", o=1),
                V3(TMP), axis=AX))
            VCH(lambda eng: eng.tensor_tensor(
                V3(TMP), V3(EB), B4(rows4[:, 16:20]), op=ALU.is_ge))
            VCH(lambda eng: eng.tensor_mul(WU, EB, TMP))
            VCH(lambda eng: eng.tensor_tensor(
                V3(wq[:]), V3(WU), B4(rows4[:, 8:12]), op=ALU.mult))
            gvq = I("vector", lambda eng: eng.tensor_copy(
                ssink[:], wq[:, 0:1]), "sV")
            bank_free[1] = gvq

            W("tensor", "sV", bank_free[1])
            wTp_done = None
            W("tensor", "sV", gvq)
            for tt in range(4):
                wTp_done = I("tensor", lambda eng, t_=tt: eng.transpose(
                    p[1][0:NE, t_ * 128:(t_ + 1) * 128],
                    wq[:, t_ * NE:(t_ + 1) * NE], sid[:]),
                    "sP" if tt == 3 else None)
            W("vector", "sP", wTp_done)
            swtb_done = I("vector", lambda eng: eng.tensor_copy(
                swTb[:], p[1][:]), "sV")
            bank_free[1] = swtb_done

            rd_next = (ln_stats(c + 1, False, None, None)
                       if c + 1 < NCH else None)

            wb_ev = [None] * NE
            for e in range(NE):
                bk = 2 + e % 2
                W("tensor", "sV", bank_free[bk])
                W("tensor", "sV", swtb_done)
                wbv = I("tensor", lambda eng, e_=e, b=bk: eng.matmul(
                    p[b][:], ssel[:, e_ * 128:(e_ + 1) * 128], swTb[0:NE, :],
                    start=True, stop=True), "sP")
                W("vector", "sP", wbv)
                wb_ev[e] = I("vector", lambda eng, e_=e, b=bk: eng.tensor_copy(
                    wbb[:, e_ * TC:(e_ + 1) * TC], p[b][:]), "sV")
                bank_free[bk] = wb_ev[e]

            if rd_next is not None:
                ln_bcast(rd_next)

            y_done = None
            for dt in range(HT):
                I("vector", lambda eng, d_=dt, hb_=hb: eng.tensor_sub(
                    t2t[:], hb_[:, d_ * TC:(d_ + 1) * TC], mrb[:, 0:TC]))
                y_done = I("vector", lambda eng, d_=dt: eng.tensor_mul(
                    h1b[:, d_ * TC:(d_ + 1) * TC], t2t[:], mrb[:, TC:2 * TC]), "sV")
            if rd_next is not None:
                ln1_copy()

            xe_done = [None] * NE

            def build_xe(e):
                for dt in range(HT):
                    v = I("vector", lambda eng, e_=e, d_=dt: eng.tensor_mul(
                        xeb[e_ % 2][:, d_ * TC:(d_ + 1) * TC],
                        h1b[:, d_ * TC:(d_ + 1) * TC],
                        wbb[:, e_ * TC:(e_ + 1) * TC]), "sV")
                xe_done[e] = v

            if DBG and c == 0:
                W("gpsimd", "sV", wb_ev[NE - 1])
                W("gpsimd", "sV", y_done)
                I("gpsimd", lambda eng: eng.dma_start(dbg_wq, wq[:]), "dG0", 16)
                I("gpsimd", lambda eng: eng.dma_start(dbg_swtb, swTb[:]),
                  "dG0", 16)
                I("gpsimd", lambda eng: eng.dma_start(dbg_wbb, wbb[:]), "dG0", 16)
                I("gpsimd", lambda eng: eng.dma_start(dbg_y, h1b[:]), "dG0", 16)
                I("gpsimd", lambda eng: eng.dma_start(dbg_stg, dbgstg[:]),
                  "dG0", 16)
                dbgv3 = I("gpsimd", lambda eng: eng.dma_start(
                    dbg_m2c, m2c[:]), "dG0", 16)
                W("vector", "dG0", dbgv3)
                W("tensor", "dG0", dbgv3)
            build_xe(0)
            _chk(c, "f_prep")

            if c == 0:
                for g in range(NWT):
                    wt_issue(g)
            for b in range(8):
                W("tensor", "sV", bank_free[b])
            W("tensor", "sV", wb_ev[NE - 1])
            W("tensor", "dB", bfv)
            for ot in range(8):
                I("tensor", lambda eng, o=ot: eng.matmul(
                    p[o][:], bfb[:, o * 128:(o + 1) * 128], swTb[0:NE, :],
                    start=True, stop=False))
            ffn_e = [None] * NE
            for e in range(NE):
                gwt = c * NE + e
                W("tensor", *wt_load[gwt])
                W("tensor", "sV", xe_done[e])
                for ot in range(8):
                    for dt in range(HT):
                        ffn_e[e] = I("tensor", lambda eng, e_=e, o=ot, d_=dt,
                                     g=gwt: eng.matmul(
                            p[o][:],
                            wtpool[g % NWT][:, (o * 8 + d_) * 128:(o * 8 + d_ + 1) * 128],
                            xeb[e_ % 2][:, d_ * TC:(d_ + 1) * TC],
                            start=False, stop=(e_ == NE - 1 and d_ == HT - 1)),
                            "sP" if (ot == 7 and dt == HT - 1) else None)
                wt_cons[gwt] = ffn_e[e]
                wt_issue(gwt + NWT)
                if e + 1 < NE:
                    if e >= 1:
                        W("vector", "sP", ffn_e[e - 1])
                    build_xe(e + 1)
                    if rd_next is not None and e < 4:
                        h0_build(c + 1, [2 * e, 2 * e + 1])

            for ot in (6, 7, 0, 1, 2, 3, 4, 5):
                W("vector", "sP", ffn_e[NE - 1])
                ev = I("vector", lambda eng, o=ot, x_=xcur: eng.tensor_add(
                    x_[:, o * TC:(o + 1) * TC], p[o][:],
                    x_[:, o * TC:(o + 1) * TC]), "sV")
                bank_free[ot] = ev
                W("scalar", "sV", ev)
                dst = oT.rearrange("(ot p) t -> p ot t", p=128)[
                    :, ot:ot + 1, c * TC:(c + 1) * TC]
                sm = f"dG{ot % 2}"
                st = (sm, I("scalar", lambda eng, d=dst, o=ot, x_=xcur:
                            eng.dma_start(
                    d, x_[:, o * TC:(o + 1) * TC].rearrange(
                        "p (a t) -> p a t", a=1)), sm, 16))
                st_hist.append(st)
                outdma[0] += 1

            _chk(c, "g_chunk")
            if c + 2 < NCH:
                W("sync", "sV", bank_free[5])
                W("sync", *st_hist[-1])
                W("sync", *st_hist[-2])
                xv[c + 2] = x_load(c + 2)

    except _StopEmit:
        pass

    W("scalar", "dG0", CNT["dG0"])
    W("scalar", "dG1", CNT["dG1"])

    with nc.Block() as blk:
        @blk.sync
        def _(eng):
            for f in OPS["sync"]:
                f(eng)

        @blk.tensor
        def _(eng):
            for f in OPS["tensor"]:
                f(eng)

        @blk.vector
        def _(eng):
            for f in OPS["vector"]:
                f(eng)

        @blk.scalar
        def _(eng):
            for f in OPS["scalar"]:
                f(eng)

        @blk.gpsimd
        def _(eng):
            for f in OPS["gpsimd"]:
                f(eng)

    es.close()
    return nc


def kernel(L0_attn_h, L0_attn_W0, L0_attn_W1, L0_ffn_W, L0_ffn_b,
           L0_ffn_Wg, L0_ffn_bg):
    if "nc" not in _CACHE:
        _CACHE["nc"] = build()
    nc = _CACHE["nc"]

    f32 = lambda a: np.ascontiguousarray(np.asarray(a, dtype=np.float32))
    xh = f32(L0_attn_h)
    W0 = f32(L0_attn_W0)
    W1 = f32(L0_attn_W1)
    WT = np.ascontiguousarray(f32(L0_ffn_W).transpose(0, 2, 1))
    BF = f32(L0_ffn_b)
    WG = np.ascontiguousarray(f32(L0_ffn_Wg).T)
    BG = f32(L0_ffn_bg).reshape(1, NE)
    EYE = np.eye(128, dtype=np.float32)
    SEL = np.zeros((NE, NE * 128), dtype=ml_dtypes.bfloat16)
    for e in range(NE):
        SEL[e, e * 128:(e + 1) * 128] = 1
    shared = dict(W0=W0, W1=W1, WT=WT, BF=BF, WGT=WG, BG=BG, EYE=EYE, SEL=SEL)
    in_maps = []
    for b in range(8):
        m = dict(shared)
        m["xT"] = np.ascontiguousarray(xh[b].T)
        in_maps.append(m)
    res = run_bass_kernel_spmd(nc, in_maps, core_ids=list(range(8)))
    out = np.stack([np.asarray(r["outT"]).T for r in res.results], axis=0)
    return np.ascontiguousarray(out, dtype=np.float32)


# version 22
# speedup vs baseline: 1.5114x; 1.0220x over previous
import numpy as np
import ml_dtypes
from contextlib import ExitStack
import concourse.bass as bass
import concourse.mybir as mybir
from concourse.bass_utils import run_bass_kernel_spmd

F32 = mybir.dt.float32
BF16 = mybir.dt.bfloat16
AF = mybir.ActivationFunctionType
ALU = mybir.AluOpType
AX = mybir.AxisListType.X

H = 1024
T = 4096
TC = 512
NCH = T // TC
HT = H // 128
NE = 8
NW = 5
NWT = 2

_CACHE = {}
DBG = False
STREAM_Q = "sync"
SKIP_STREAM_WAITS = False
PROBE_DUMMY_SRC = False
PROBE_TAIL = False
PROBE_SKIP_GATE = False
DBG_SCR = False
PROBE_GATE_LVL = 5
PROBE_SKIP_X = False
STOP = None


class _StopEmit(Exception):
    pass


def build():
    nc = bass.Bass("TRN2", target_bir_lowering=False, debug=False,
                   detect_race_conditions=False)

    ixT = nc.dram_tensor("xT", [H, T], F32, kind="ExternalInput").ap()
    iW0 = nc.dram_tensor("W0", [H, 3 * H], F32, kind="ExternalInput").ap()
    iW1 = nc.dram_tensor("W1", [H, 3 * H], F32, kind="ExternalInput").ap()
    iWT = nc.dram_tensor("WT", [NE, H, H], F32, kind="ExternalInput").ap()
    iBF = nc.dram_tensor("BF", [NE, H], F32, kind="ExternalInput").ap()
    iWG = nc.dram_tensor("WGT", [H, NE], F32, kind="ExternalInput").ap()
    iBG = nc.dram_tensor("BG", [1, NE], F32, kind="ExternalInput").ap()
    iEYE = nc.dram_tensor("EYE", [128, 128], F32, kind="ExternalInput").ap()
    iSEL = nc.dram_tensor("SEL", [NE, NE * 128], BF16, kind="ExternalInput").ap()
    oT = nc.dram_tensor("outT", [H, T], F32, kind="ExternalOutput").ap()
    if DBG:
        dbg_h0 = nc.dram_tensor("dbg_h0", [128, HT * TC], BF16,
                                kind="ExternalOutput").ap()
        dbg_h1 = nc.dram_tensor("dbg_h1", [128, HT * TC], BF16,
                                kind="ExternalOutput").ap()
        dbg_mrb = nc.dram_tensor("dbg_mrb", [128, 2 * TC], BF16,
                                 kind="ExternalOutput").ap()
        dbg_rows = nc.dram_tensor("dbg_rows", [1, 3 * TC], F32,
                                  kind="ExternalOutput").ap()
        dbg_x1 = nc.dram_tensor("dbg_x1", [128, HT * TC], F32,
                                kind="ExternalOutput").ap()
        dbg_x1b = nc.dram_tensor("dbg_x1b", [128, HT * TC], BF16,
                                 kind="ExternalOutput").ap()
        dbg_wq = nc.dram_tensor("dbg_wq", [128, 4 * NE], F32,
                                kind="ExternalOutput").ap()
        dbg_swtb = nc.dram_tensor("dbg_swtb", [NE, TC], BF16,
                                  kind="ExternalOutput").ap()
        dbg_wbb = nc.dram_tensor("dbg_wbb", [128, NE * TC], BF16,
                                 kind="ExternalOutput").ap()
        dbg_y = nc.dram_tensor("dbg_y", [128, HT * TC], BF16,
                               kind="ExternalOutput").ap()
        dbg_m2c = nc.dram_tensor("dbg_m2c", [128, 8], F32,
                                 kind="ExternalOutput").ap()
        dbg_csb = nc.dram_tensor("dbg_csb", [128, NE], F32,
                                 kind="ExternalOutput").ap()
        dbg_bgb = nc.dram_tensor("dbg_bgb", [128, NE], F32,
                                 kind="ExternalOutput").ap()
        dbg_wgt = nc.dram_tensor("dbg_wgt", [128, HT * NE], F32,
                                 kind="ExternalOutput").ap()
        dbg_stg = nc.dram_tensor("dbg_stg", [128, 70], F32,
                                 kind="ExternalOutput").ap()
    sW0 = nc.dram_tensor("scrW0", [8, 128, 3 * H], BF16, kind="Internal").ap()
    sW1 = nc.dram_tensor("scrW1", [8, 128, 3 * H], BF16, kind="Internal").ap()
    sWT = nc.dram_tensor("scrWT", [NE, 128, NE * H], BF16, kind="Internal").ap()

    es = ExitStack()
    SB = lambda n, s, d: es.enter_context(nc.sbuf_tensor(n, s, d))
    PS = lambda n, s: es.enter_context(nc.psum_tensor(n, s, F32))
    SEM = lambda n: es.enter_context(nc.semaphore(n))

    wpool = [SB(f"wp{i}", [128, 3 * H], BF16) for i in range(NW)]
    wtpool = [SB(f"wt{i}", [128, NE * H], BF16) for i in range(NWT)]
    xbuf = [SB(f"x{i}", [128, HT * TC], F32) for i in range(2)]
    h0b = SB("h0", [128, HT * TC], BF16)
    h0b2 = SB("h0x", [128, HT * TC], BF16)
    h1b = SB("h1", [128, HT * TC], BF16)
    xeb = [SB(f"xe{i}", [128, HT * TC], BF16) for i in range(2)]
    wbb = SB("wb", [128, NE * TC], BF16)
    sgb = [SB(f"sg{i}", [128, TC], F32) for i in range(2)]
    ab = [SB(f"a{i}", [128, TC], F32) for i in range(2)]
    shb = [SB(f"sh{i}", [128, TC], F32) for i in range(2)]
    gpb = [SB(f"gp{i}", [128, TC], F32) for i in range(2)]
    gtb = SB("gt", [128, TC], F32)
    vb = SB("v", [128, TC], F32)
    Hb = SB("Hs", [128, TC], F32)
    db = SB("d", [128, TC], F32)
    h2b = SB("h2", [128, TC], F32)
    t2t = SB("t2t", [128, TC], BF16)
    xsq = [SB(f"xsq{i}", [128, TC], BF16) for i in range(2)]
    rows = SB("rows", [1, 3 * TC], F32)
    mrb = SB("mrb", [128, 2 * TC], BF16)
    mrb2 = SB("mrb2", [128, 2 * TC], BF16)
    carry1 = SB("c1s", [128, HT], F32)
    carry2 = SB("c2s", [128, HT], F32)
    sid = SB("sid", [128, 128], F32)
    ssel = SB("ssel", [NE, NE * 128], BF16)
    swgt = SB("swgt", [128, HT * NE], F32)
    swgtb = SB("swgtb", [128, HT * NE], BF16)
    ltg = SB("ltg", [128, TC], F32)
    sbg = SB("sbg", [1, NE], F32)
    bgbb = SB("bgb", [128, 4 * NE], F32)
    csbb = SB("csb", [128, 4 * NE], F32)
    csrow = SB("csr", [1, NE], F32)
    cgrow = SB("cgrow", [1, 8 * NE], F32)
    bfb = SB("bfb", [NE, H], BF16)
    swTb = SB("swTb", [128, TC], BF16)
    gate = SB("gate", [128, 16 * NE], F32)
    rows4 = SB("rows4", [128, 20], F32)
    wq = SB("wq", [128, 4 * NE], F32)
    m2c = SB("m2c", [128, 8], F32)
    onec = SB("onec", [128, 1], F32)
    onecb = SB("onecb", [128, 1], BF16)
    oner = SB("oner", [1, 128], F32)
    epsc = SB("epsc", [1, 1], F32)
    ssink = SB("ssink", [128, 1], F32)
    dbgstg = SB("dbgstg", [128, 70], F32) if DBG else None

    p = [PS(f"p{i}", [128, TC]) for i in range(8)]

    semnames = (["sP", "sV", "sA", "dC", "dB", "dSt0", "dStT", "dSt1",
                 "dSu0", "dSuT", "dSu1"]
                + [f"dX{i}" for i in range(2)]
                + [f"dWp{i}" for i in range(NW)]
                + [f"dWT{i}" for i in range(NWT)]
                + [f"dF{i}" for i in range(2)]
                + [f"dG{i}" for i in range(2)])
    sems = {k: SEM(k) for k in semnames}

    OPS = {k: [] for k in ("sync", "tensor", "vector", "scalar", "gpsimd")}
    CNT = {k: 0 for k in sems}
    LASTW = {}

    def W(st, sem, val):
        if not val:
            return
        key = (st, sem)
        if LASTW.get(key, 0) >= val:
            return
        LASTW[key] = val
        OPS[st].append(lambda eng, s=sems[sem], v=val: eng.wait_ge(s, v))

    def I(st, fn, sem=None, n=1):
        if sem is None:
            OPS[st].append(fn)
            return None
        CNT[sem] += n
        v = CNT[sem]
        OPS[st].append(lambda eng, f=fn, s=sems[sem], k=n: f(eng).then_inc(s, k))
        return v

    SEM_OF = {"vector": "sV", "scalar": "sA", "tensor": "sP"}

    def VCH(fn):
        v = I("vector", fn, "sV")
        W("vector", "sV", v)
        return v

    def VSYNC():
        W("vector", "sV", CNT["sV"])

    xv = [None] * NCH

    def x_load(c):
        dst = xbuf[c % 2][:].rearrange("p (ht t) -> p ht t", ht=HT)
        src = ixT.rearrange("(ht p) t -> p ht t", p=128)[:, :, c * TC:(c + 1) * TC]
        sm = f"dX{c % 2}"
        return (sm, I("sync", lambda eng, d=dst, s=src: eng.dma_start(d, s),
                      sm, 16))

    xv[0] = x_load(0)
    xv[1] = x_load(1)
    I("sync", lambda eng: eng.dma_start(sid[:], iEYE), "dC", 16)
    I("sync", lambda eng: eng.dma_start(ssel[:], iSEL), "dC", 16)
    I("sync", lambda eng: eng.dma_start(
        swgt[:].rearrange("p (ht e) -> p ht e", ht=HT),
        iWG.rearrange("(ht p) e -> p ht e", p=128)), "dC", 16)
    consts_v = I("sync", lambda eng: eng.dma_start(sbg[:], iBG), "dC", 16)
    bfv = I("gpsimd", lambda eng: eng.dma_start(bfb[:], iBF), "dB", 16)

    I("vector", lambda eng: eng.memset(onec[:], 1.0))
    I("vector", lambda eng: eng.memset(onecb[:], 1.0))
    I("vector", lambda eng: eng.memset(oner[:], 1.0))
    W("vector", "dC", consts_v)
    I("vector", lambda eng: eng.tensor_copy(swgtb[:], swgt[:]))
    bfcast = I("vector", lambda eng: eng.memset(epsc[:], 1e-5), "sV")

    W("tensor", "dC", consts_v)
    W("tensor", "sV", bfcast)
    if not PROBE_SKIP_GATE:
        L = PROBE_GATE_LVL
        if L >= 1:
            for ht in range(HT):
                I("tensor", lambda eng, h=ht: eng.matmul(
                    p[0][0:1, 0:NE], onec[:], swgt[:, h * NE:(h + 1) * NE],
                    start=(h == 0), stop=(h == HT - 1)))
        if L >= 2:
            pv = I("tensor", lambda eng: eng.matmul(
                p[1][:, 0:NE], oner[:], sbg[:], start=True, stop=True), "sP")
        if L >= 3:
            W("vector", "sP", pv)
            VCH(lambda eng: eng.tensor_copy(csrow[:], p[0][0:1, 0:NE]))
            csv = I("vector", lambda eng: eng.tensor_copy(
                ssink[0:1, :], csrow[0:1, 0:1]), "sV")
        if L >= 4:
            W("vector", "sP", pv)
            for i in range(4):
                VCH(lambda eng, i_=i: eng.tensor_copy(
                    cgrow[0:1, i_ * NE:(i_ + 1) * NE], csrow[:]))
                VCH(lambda eng, i_=i: eng.tensor_copy(
                    cgrow[0:1, 32 + i_ * NE:32 + (i_ + 1) * NE],
                    p[1][0:1, 0:NE]))
            csv2 = I("vector", lambda eng: eng.tensor_copy(
                ssink[0:1, :], cgrow[0:1, 0:1]), "sV")
            W("tensor", "sV", csv2)
            pv2 = I("tensor", lambda eng: eng.matmul(
                p[0][:, 0:8 * NE], oner[:], cgrow[:],
                start=True, stop=True), "sP")
        if L >= 5:
            W("vector", "sP", pv2)
            I("vector", lambda eng: eng.tensor_copy(csbb[:], p[0][:, 0:32]))
            I("vector", lambda eng: eng.tensor_copy(bgbb[:], p[0][:, 32:64]),
              "sV")

    def _tail():
        if PROBE_TAIL:
            for st in OPS:
                for _ in range(100):
                    OPS[st].append(lambda eng: eng.nop(cycle_cnt=10000))

    def _chk0(tag):
        if STOP is not None and (-1, tag) >= STOP:
            _tail()
            raise _StopEmit

    for jg in range(8):
        sm = "dSt0" if jg % 2 == 0 else "dSu0"
        W("gpsimd", sm, 16 * (jg // 2))
        I("gpsimd", lambda eng, j=jg: eng.dma_start(
            sW0[j].rearrange("p (ki kj ht c) -> p ki kj ht c",
                             ki=3, kj=1, c=128),
            iW0.rearrange("(ht p) (ki kj c) -> p ki kj ht c",
                          p=128, ki=3, c=128)[:, :, j:j + 1, :, :]),
          sm, 16)
    for jg in range(8):
        sm = "dSt1" if jg % 2 == 0 else "dSu1"
        W("gpsimd", "dSt0", 64)
        W("gpsimd", "dSu0", 64)
        W("gpsimd", sm, 16 * (jg // 2))
        I("gpsimd", lambda eng, j=jg: eng.dma_start(
            sW1[j].rearrange("p (ki kj ht c) -> p ki kj ht c",
                             ki=3, kj=1, c=128),
            iW1.rearrange("(ht p) (ki kj c) -> p ki kj ht c",
                          p=128, ki=3, c=128)[:, :, j:j + 1, :, :]),
          sm, 16)
    for eg in range(NE):
        sm = "dStT" if eg % 2 == 0 else "dSuT"
        W("gpsimd", "dSt1", 64)
        W("gpsimd", "dSu1", 64)
        W("gpsimd", sm, 16 * (eg // 2))
        I("gpsimd", lambda eng, e_=eg: eng.dma_start(
            sWT[e_].rearrange("p (ot dt c) -> p ot dt c", ot=8, c=128),
            iWT[e_].rearrange("(dt p) (ot c) -> p ot dt c", p=128, c=128)),
          sm, 16)

    NG01 = NCH * 16
    w01_load = [None] * NG01
    w01_cons = [None] * NG01

    def w01_issue(gidx):
        if gidx >= NG01:
            return
        ps, j = divmod(gidx % 16, 8)
        if j % 2 == 0:
            W("sync", "dSt1" if ps else "dSt0", 16 * (j // 2 + 1))
        else:
            W("sync", "dSu1" if ps else "dSu0", 16 * (j // 2 + 1))
        if gidx >= NW:
            W("sync", "sP", w01_cons[gidx - NW])
        sm = f"dWp{gidx % NW}"
        scr = sW1 if ps else sW0
        w01_load[gidx] = (sm, I("sync", lambda eng, s=scr, j_=j,
                                sl=wpool[gidx % NW]:
                                eng.dma_start(sl[:], s[j_]), sm, 16))

    NGWT = NCH * NE
    wt_load = [None] * NGWT
    wt_cons = [None] * NGWT

    def wt_issue(gidx):
        if gidx >= NGWT:
            return
        e = gidx % NE
        if e % 2 == 0:
            W("scalar", "dStT", 16 * (e // 2 + 1))
        else:
            W("scalar", "dSuT", 16 * (e // 2 + 1))
        if gidx >= NWT:
            W("scalar", "sP", wt_cons[gidx - NWT])
        sm = f"dWT{gidx % NWT}"
        wt_load[gidx] = (sm, I("scalar", lambda eng, e_=e,
                               sl=wtpool[gidx % NWT]:
                               eng.dma_start(sl[:], sWT[e_]), sm, 16))

    for g in range(NW):
        w01_issue(g)

    bank_free = [0] * 8
    ln_bcast_done = [0]
    ssq_mm_prev = [None] * HT
    st_hist = []
    outdma = [0]
    h0bufs = [h0b, h0b2]
    h0d_cur = [None] * HT
    try:

        def ln_stats(c, second, vj2, from_buf):
            xcur = xbuf[c % 2]
            W("tensor", "sV", bank_free[6])
            W("tensor", "sV", bank_free[7])
            if not second:
                W("tensor", *xv[c])
            for ht in range(HT):
                if second:
                    W("tensor", "sV", vj2[ht])
                    I("tensor", lambda eng, h=ht, b_=from_buf: eng.matmul(
                        p[6][0:1, :], onecb[:], b_[:, h * TC:(h + 1) * TC],
                        start=(h == 0), stop=(h == HT - 1)))
                else:
                    I("tensor", lambda eng, h=ht, x_=xcur: eng.matmul(
                        p[6][0:1, :], onec[:], x_[:, h * TC:(h + 1) * TC],
                        start=(h == 0), stop=(h == HT - 1)))
            ssq_cur = [None] * HT
            for ht in range(HT):
                if second:
                    W("scalar", "sV", vj2[ht])
                else:
                    W("scalar", *xv[c])
                W("scalar", "sP", ssq_cur[ht - 2] if ht >= 2 else ssq_mm_prev[ht])
                if second:
                    sq = I("scalar", lambda eng, h=ht, b_=from_buf:
                           eng.activation(
                        xsq[h % 2][:], b_[:, h * TC:(h + 1) * TC], AF.Square),
                        "sA")
                else:
                    sq = I("scalar", lambda eng, h=ht, x_=xcur: eng.activation(
                        xsq[h % 2][:], x_[:, h * TC:(h + 1) * TC], AF.Square),
                        "sA")
                W("tensor", "sA", sq)
                ssq_cur[ht] = I("tensor", lambda eng, h=ht: eng.matmul(
                    p[7][0:1, :], onecb[:], xsq[h % 2][:],
                    start=(h == 0), stop=(h == HT - 1)), "sP")
            for ht in range(HT):
                ssq_mm_prev[ht] = ssq_cur[ht]
            stats_done = ssq_cur[HT - 1]
            mean = rows[0:1, 0:TC]
            var = rows[0:1, TC:2 * TC]
            rstd = rows[0:1, 2 * TC:3 * TC]
            W("vector", "sP", stats_done)
            W("vector", "sP", ln_bcast_done[0])
            I("vector", lambda eng: eng.tensor_scalar(
                mean, p[6][0:1, :], 1.0 / H, None, op0=ALU.mult))
            I("vector", lambda eng: eng.tensor_mul(var, mean, mean))
            I("vector", lambda eng: eng.tensor_scalar(
                rstd, p[7][0:1, :], 1.0 / H, None, op0=ALU.mult))
            var_done = I("vector", lambda eng: eng.tensor_sub(var, rstd, var), "sV")
            W("scalar", "sV", var_done)
            I("scalar", lambda eng: eng.activation(rstd, var, AF.Ln,
                                                   bias=epsc[0:1, :]))
            rstd_done = I("scalar", lambda eng: eng.activation(
                rstd, rstd, AF.Exp, scale=-0.5), "sA")
            return rstd_done

        def ln_bcast(rstd_done):
            mean = rows[0:1, 0:TC]
            rstd = rows[0:1, 2 * TC:3 * TC]
            W("tensor", "sA", rstd_done)
            I("tensor", lambda eng: eng.matmul(
                p[6][:], oner[:], mean, start=True, stop=True))
            ln_bcast_done[0] = I("tensor", lambda eng: eng.matmul(
                p[7][:], oner[:], rstd, start=True, stop=True), "sP")

        def ln1_copy():
            W("vector", "sP", ln_bcast_done[0])
            I("vector", lambda eng: eng.tensor_copy(mrb2[:, 0:TC], p[6][:]))
            bf67 = I("vector", lambda eng: eng.tensor_copy(
                mrb2[:, TC:2 * TC], p[7][:]), "sV")
            bank_free[6] = bank_free[7] = bf67

        def h0_build(c, hts):
            hbn = h0bufs[c % 2]
            xcur = xbuf[c % 2]
            W("vector", *xv[c])
            for ht in hts:
                I("vector", lambda eng, h=ht, x_=xcur: eng.tensor_sub(
                    t2t[:], x_[:, h * TC:(h + 1) * TC], mrb2[:, 0:TC]))
                h0d_cur[ht] = I("vector", lambda eng, h=ht, hb_=hbn:
                                eng.tensor_mul(
                    hb_[:, h * TC:(h + 1) * TC], t2t[:],
                    mrb2[:, TC:2 * TC]), "sV")

        def ln2(c, vj2, hb):
            rstd_done = ln_stats(c, True, vj2, hb)
            mean = rows[0:1, 0:TC]
            rstd = rows[0:1, 2 * TC:3 * TC]
            W("tensor", "sV", bank_free[5])
            W("tensor", "sA", rstd_done)
            for i in range(4):
                I("tensor", lambda eng, i_=i: eng.transpose(
                    p[5][:, i_:i_ + 1], mean[:, i_ * 128:(i_ + 1) * 128],
                    sid[0:1, 0:1]))
            tcol_done = None
            for i in range(4):
                tcol_done = I("tensor", lambda eng, i_=i: eng.transpose(
                    p[5][:, 4 + i_:5 + i_], rstd[:, i_ * 128:(i_ + 1) * 128],
                    sid[0:1, 0:1]), "sP" if i == 3 else None)
            W("vector", "sP", tcol_done)
            bank_free[5] = VCH(lambda eng: eng.tensor_copy(
                m2c[:], p[5][:, 0:8]))
            ln_bcast(rstd_done)
            W("vector", "sP", ln_bcast_done[0])
            I("vector", lambda eng: eng.tensor_copy(mrb[:, 0:TC], p[6][:]))
            bf67 = I("vector", lambda eng: eng.tensor_copy(
                mrb[:, TC:2 * TC], p[7][:]), "sV")
            bank_free[6] = bank_free[7] = bf67

        def _chk(c, tag):
            if STOP is not None and (c, tag) >= STOP:
                raise _StopEmit

        rd0 = ln_stats(0, False, None, None)
        ln_bcast(rd0)
        ln1_copy()
        h0_build(0, range(HT))

        for c in range(NCH):
            xcur = xbuf[c % 2]
            hb = h0bufs[c % 2]
            vj1 = [None] * HT
            vj2 = [None] * HT
            dep1 = list(h0d_cur)

            def mixer_pass(ps, rhs_buf, carry, vj_out, dep_list):
                for j in range(HT):
                    gidx = c * 16 + ps * 8 + j
                    bs = 3 * (j % 2)
                    W("tensor", *w01_load[gidx])
                    if j >= 2:
                        W("tensor", "sV", vj_out[j - 2])
                    else:
                        for b in (bs, bs + 1, bs + 2):
                            W("tensor", "sV", bank_free[b])
                    if j > 0:
                        W("tensor", "sV", dep_list[HT - 1])
                    zd = [None] * 3
                    for ki in (1, 0, 2):
                        for ht in range(HT):
                            if j == 0 and ki == 1:
                                W("tensor", "sV", dep_list[ht])
                            zd[ki] = I("tensor", lambda eng, k=ki, h=ht,
                                       g=gidx, b=bs, r=rhs_buf: eng.matmul(
                                p[b + k][:],
                                wpool[g % NW][:, (k * 8 + h) * 128:(k * 8 + h + 1) * 128],
                                r[:, h * TC:(h + 1) * TC],
                                start=(h == 0), stop=(h == HT - 1)),
                                "sP" if ht == HT - 1 else None)
                    z_drain = zd[2]
                    w01_cons[gidx] = z_drain
                    w01_issue(gidx + NW)
                    par = j % 2
                    if j >= 2:
                        W("scalar", "sV", vj_out[j - 2])
                    W("scalar", "sP", zd[1])
                    I("scalar", lambda eng, pa=par, b=bs: eng.activation(
                        sgb[pa][:], p[b + 1][:], AF.Sigmoid))
                    I("scalar", lambda eng, pa=par, b=bs: eng.activation(
                        ab[pa][:], p[b + 1][:], AF.Sigmoid, scale=-1.0))
                    W("scalar", "sP", zd[0])
                    sh_done = I("scalar", lambda eng, pa=par, b=bs:
                                eng.activation(
                        shb[pa][:], p[b][:], AF.Sigmoid), "sA")
                    W("scalar", "sP", zd[2])
                    s_done = I("scalar", lambda eng, pa=par, b=bs: eng.activation(
                        gpb[pa][:], p[b + 2][:], AF.Sigmoid), "sA")
                    W("vector", "sA", sh_done)
                    W("vector", "sP", zd[0])
                    I("vector", lambda eng, pa=par, b=bs: eng.scalar_tensor_tensor(
                        gtb[:], p[b][:], 0.5, shb[pa][:], op0=ALU.add, op1=ALU.max))
                    I("vector", lambda eng, pa=par: eng.tensor_mul(
                        vb[:], sgb[pa][:], gtb[:]))
                    if c == 0:
                        I("vector", lambda eng, pa=par: eng.tensor_tensor_scan(
                            Hb[:], ab[pa][:], vb[:], 0.0, op0=ALU.mult, op1=ALU.add))
                    else:
                        I("vector", lambda eng, pa=par, jj=j, ca=carry:
                          eng.tensor_tensor_scan(
                            Hb[:], ab[pa][:], vb[:], ca[:, jj:jj + 1],
                            op0=ALU.mult, op1=ALU.add))
                    I("vector", lambda eng, jj=j, ca=carry: eng.tensor_copy(
                        ca[:, jj:jj + 1], Hb[:, TC - 1:TC]))
                    W("vector", "sA", s_done)
                    if ps == 0:
                        I("vector", lambda eng, jj=j, hb_=hb: eng.tensor_sub(
                            db[:], Hb[:], hb_[:, jj * TC:(jj + 1) * TC]))
                        I("vector", lambda eng, pa=par: eng.tensor_mul(
                            db[:], gpb[pa][:], db[:]))
                        vj_out[j] = I("vector", lambda eng, jj=j, hb_=hb:
                                      eng.tensor_add(
                            h1b[:, jj * TC:(jj + 1) * TC], db[:],
                            hb_[:, jj * TC:(jj + 1) * TC]), "sV")
                    else:
                        I("vector", lambda eng, jj=j: eng.tensor_sub(
                            db[:], Hb[:], h1b[:, jj * TC:(jj + 1) * TC]))
                        I("vector", lambda eng, pa=par: eng.tensor_mul(
                            db[:], gpb[pa][:], db[:]))
                        I("vector", lambda eng, jj=j: eng.tensor_add(
                            h2b[:], db[:], h1b[:, jj * TC:(jj + 1) * TC]))
                        I("vector", lambda eng, jj=j, x_=xcur: eng.tensor_add(
                            x_[:, jj * TC:(jj + 1) * TC],
                            x_[:, jj * TC:(jj + 1) * TC], h2b[:]))
                        vj_out[j] = I("vector", lambda eng, jj=j, x_=xcur,
                                      hb_=hb: eng.tensor_copy(
                            hb_[:, jj * TC:(jj + 1) * TC],
                            x_[:, jj * TC:(jj + 1) * TC]), "sV")
                    for k in range(3):
                        bank_free[bs + k] = vj_out[j]

            _chk(c, "a_ln1")
            mixer_pass(0, hb, carry1, vj1, dep1)
            _chk(c, "b_pass1")
            if DBG and c == 0:
                W("gpsimd", "sV", vj1[HT - 1])
                I("gpsimd", lambda eng: eng.dma_start(dbg_h0, h0b[:]), "dG0", 16)
                I("gpsimd", lambda eng: eng.dma_start(dbg_h1, h1b[:]), "dG0", 16)
                I("gpsimd", lambda eng: eng.dma_start(dbg_mrb, mrb[:]), "dG0", 16)
                I("gpsimd", lambda eng: eng.dma_start(dbg_csb, csbb[:]), "dG0", 16)
                I("gpsimd", lambda eng: eng.dma_start(dbg_bgb, bgbb[:]), "dG0", 16)
                I("gpsimd", lambda eng: eng.dma_start(dbg_wgt, swgt[:]), "dG0", 16)
                dbgv = I("gpsimd", lambda eng: eng.dma_start(dbg_rows, rows[:]),
                         "dG0", 16)
                W("vector", "dG0", dbgv)
                W("tensor", "dG0", dbgv)
            mixer_pass(1, h1b, carry2, vj2, vj1)
            if DBG and c == 0:
                W("gpsimd", "sV", vj2[HT - 1])
                I("gpsimd", lambda eng: eng.dma_start(dbg_x1, xbuf[0][:]),
                  "dG0", 16)
                dbgv2 = I("gpsimd", lambda eng: eng.dma_start(
                    dbg_x1b, h0b[:]), "dG0", 16)
                W("vector", "dG0", dbgv2)
                W("tensor", "dG0", dbgv2)
                W("scalar", "dG0", dbgv2)
            _chk(c, "c_pass2")

            ln2(c, vj2, hb)
            _chk(c, "d_ln2")

            W("tensor", "sV", bank_free[0])
            W("tensor", "sV", bank_free[1])
            lgT_done = None
            for ht in range(HT):
                lgT_done = I("tensor", lambda eng, h=ht, hb_=hb: eng.matmul(
                    p[0][0:NE, :], swgtb[:, h * NE:(h + 1) * NE],
                    hb_[:, h * TC:(h + 1) * TC],
                    start=(h == 0), stop=(h == HT - 1)),
                    "sP" if ht == HT - 1 else None)
            W("vector", "sP", lgT_done)
            ltg_ev = I("vector", lambda eng: eng.tensor_copy(
                ltg[:], p[0][:]), "sV")
            bank_free[0] = ltg_ev
            W("tensor", "sV", ltg_ev)
            glt_done = None
            for tt in range(4):
                glt_done = I("tensor", lambda eng, t_=tt: eng.transpose(
                    p[1][:, t_ * NE:(t_ + 1) * NE],
                    ltg[0:NE, t_ * 128:(t_ + 1) * 128], sid[0:NE, 0:NE]),
                    "sP" if tt == 3 else None)
            LG = gate[:, 0:32]
            EB = gate[:, 32:64]
            TMP = gate[:, 64:96]
            WU = gate[:, 96:128]
            V3 = lambda t: t.rearrange("p (t e) -> p t e", t=4)
            B4 = lambda c: c.rearrange("p (t o) -> p t o", o=1).to_broadcast(
                [128, 4, NE])
            m4 = m2c[:, 0:4]
            r4 = m2c[:, 4:8]
            W("vector", "sP", glt_done)
            VSYNC()
            VCH(lambda eng: eng.tensor_tensor(
                V3(TMP), V3(csbb[:]), B4(m4), op=ALU.mult))
            VCH(lambda eng: eng.tensor_sub(LG, p[1][:, 0:32], TMP))
            VCH(lambda eng: eng.tensor_tensor(
                V3(LG), V3(LG), B4(r4), op=ALU.mult))
            VCH(lambda eng: eng.tensor_add(LG, LG, bgbb[:]))
            VCH(lambda eng: eng.reduce_max(
                rows4[:, 0:4].rearrange("p (t o) -> p t o", o=1),
                V3(LG), axis=AX, negate=True))
            VCH(lambda eng: eng.tensor_tensor(
                V3(LG), V3(LG), B4(rows4[:, 0:4]), op=ALU.add))
            lgv = I("vector", lambda eng: eng.tensor_copy(
                ssink[:], rows4[:, 0:1]), "sV")
            W("scalar", "sV", lgv)
            I("scalar", lambda eng: eng.activation(EB, LG, AF.Exp))
            gexpv = I("scalar", lambda eng: eng.copy(
                epsc[0:1, :], epsc[0:1, :]), "sA")
            W("vector", "sA", gexpv)
            VCH(lambda eng: eng.reduce_max(
                rows4[:, 12:16].rearrange("p (t o) -> p t o", o=1),
                V3(EB), axis=AX))
            VCH(lambda eng: eng.reduce_sum(
                rows4[:, 4:8].rearrange("p (t o) -> p t o", o=1),
                V3(EB), axis=AX))
            VCH(lambda eng: eng.reciprocal(rows4[:, 8:12], rows4[:, 4:8]))
            VCH(lambda eng: eng.tensor_tensor(
                V3(TMP), V3(EB), B4(rows4[:, 12:16]), op=ALU.is_lt))
            VCH(lambda eng: eng.tensor_mul(TMP, EB, TMP))
            VCH(lambda eng: eng.reduce_max(
                rows4[:, 16:20].rearrange("p (t o) -> p t o", o=1),
                V3(TMP), axis=AX))
            VCH(lambda eng: eng.tensor_tensor(
                V3(TMP), V3(EB), B4(rows4[:, 16:20]), op=ALU.is_ge))
            VCH(lambda eng: eng.tensor_mul(WU, EB, TMP))
            VCH(lambda eng: eng.tensor_tensor(
                V3(wq[:]), V3(WU), B4(rows4[:, 8:12]), op=ALU.mult))
            gvq = I("vector", lambda eng: eng.tensor_copy(
                ssink[:], wq[:, 0:1]), "sV")
            bank_free[1] = gvq

            W("tensor", "sV", bank_free[1])
            wTp_done = None
            W("tensor", "sV", gvq)
            for tt in range(4):
                wTp_done = I("tensor", lambda eng, t_=tt: eng.transpose(
                    p[1][0:NE, t_ * 128:(t_ + 1) * 128],
                    wq[:, t_ * NE:(t_ + 1) * NE], sid[:]),
                    "sP" if tt == 3 else None)
            W("vector", "sP", wTp_done)
            swtb_done = I("vector", lambda eng: eng.tensor_copy(
                swTb[:], p[1][:]), "sV")
            bank_free[1] = swtb_done

            rd_next = (ln_stats(c + 1, False, None, None)
                       if c + 1 < NCH else None)

            wb_ev = [None] * NE
            for e in range(NE):
                bk = 2 + e % 2
                W("tensor", "sV", bank_free[bk])
                W("tensor", "sV", swtb_done)
                wbv = I("tensor", lambda eng, e_=e, b=bk: eng.matmul(
                    p[b][:], ssel[:, e_ * 128:(e_ + 1) * 128], swTb[0:NE, :],
                    start=True, stop=True), "sP")
                W("vector", "sP", wbv)
                wb_ev[e] = I("vector", lambda eng, e_=e, b=bk: eng.tensor_copy(
                    wbb[:, e_ * TC:(e_ + 1) * TC], p[b][:]), "sV")
                bank_free[bk] = wb_ev[e]

            if rd_next is not None:
                ln_bcast(rd_next)

            y_done = None
            for dt in range(HT):
                I("vector", lambda eng, d_=dt, hb_=hb: eng.tensor_sub(
                    t2t[:], hb_[:, d_ * TC:(d_ + 1) * TC], mrb[:, 0:TC]))
                y_done = I("vector", lambda eng, d_=dt: eng.tensor_mul(
                    h1b[:, d_ * TC:(d_ + 1) * TC], t2t[:], mrb[:, TC:2 * TC]), "sV")
            if rd_next is not None:
                ln1_copy()

            xe_done = [None] * NE

            def build_xe(e):
                for dt in range(HT):
                    v = I("vector", lambda eng, e_=e, d_=dt: eng.tensor_mul(
                        xeb[e_ % 2][:, d_ * TC:(d_ + 1) * TC],
                        h1b[:, d_ * TC:(d_ + 1) * TC],
                        wbb[:, e_ * TC:(e_ + 1) * TC]), "sV")
                xe_done[e] = v

            if DBG and c == 0:
                W("gpsimd", "sV", wb_ev[NE - 1])
                W("gpsimd", "sV", y_done)
                I("gpsimd", lambda eng: eng.dma_start(dbg_wq, wq[:]), "dG0", 16)
                I("gpsimd", lambda eng: eng.dma_start(dbg_swtb, swTb[:]),
                  "dG0", 16)
                I("gpsimd", lambda eng: eng.dma_start(dbg_wbb, wbb[:]), "dG0", 16)
                I("gpsimd", lambda eng: eng.dma_start(dbg_y, h1b[:]), "dG0", 16)
                I("gpsimd", lambda eng: eng.dma_start(dbg_stg, dbgstg[:]),
                  "dG0", 16)
                dbgv3 = I("gpsimd", lambda eng: eng.dma_start(
                    dbg_m2c, m2c[:]), "dG0", 16)
                W("vector", "dG0", dbgv3)
                W("tensor", "dG0", dbgv3)
            build_xe(0)
            _chk(c, "f_prep")

            if c == 0:
                for g in range(NWT):
                    wt_issue(g)
            for b in range(8):
                W("tensor", "sV", bank_free[b])
            W("tensor", "sV", wb_ev[NE - 1])
            W("tensor", "dB", bfv)
            for ot in range(8):
                I("tensor", lambda eng, o=ot: eng.matmul(
                    p[o][:], bfb[:, o * 128:(o + 1) * 128], swTb[0:NE, :],
                    start=True, stop=False))
            ffn_e = [None] * NE
            for e in range(NE):
                gwt = c * NE + e
                W("tensor", *wt_load[gwt])
                W("tensor", "sV", xe_done[e])
                for ot in range(8):
                    for dt in range(HT):
                        ffn_e[e] = I("tensor", lambda eng, e_=e, o=ot, d_=dt,
                                     g=gwt: eng.matmul(
                            p[o][:],
                            wtpool[g % NWT][:, (o * 8 + d_) * 128:(o * 8 + d_ + 1) * 128],
                            xeb[e_ % 2][:, d_ * TC:(d_ + 1) * TC],
                            start=False, stop=(e_ == NE - 1 and d_ == HT - 1)),
                            "sP" if (ot == 7 and dt == HT - 1) else None)
                wt_cons[gwt] = ffn_e[e]
                wt_issue(gwt + NWT)
                if e + 1 < NE:
                    if e >= 1:
                        W("vector", "sP", ffn_e[e - 1])
                    build_xe(e + 1)
                    if rd_next is not None and e < 4:
                        h0_build(c + 1, [2 * e, 2 * e + 1])

            for ot in (6, 7, 0, 1, 2, 3, 4, 5):
                W("vector", "sP", ffn_e[NE - 1])
                ev = I("vector", lambda eng, o=ot, x_=xcur: eng.tensor_add(
                    x_[:, o * TC:(o + 1) * TC], p[o][:],
                    x_[:, o * TC:(o + 1) * TC]), "sV")
                bank_free[ot] = ev
                W("scalar", "sV", ev)
                dst = oT.rearrange("(ot p) t -> p ot t", p=128)[
                    :, ot:ot + 1, c * TC:(c + 1) * TC]
                sm = f"dG{ot % 2}"
                st = (sm, I("scalar", lambda eng, d=dst, o=ot, x_=xcur:
                            eng.dma_start(
                    d, x_[:, o * TC:(o + 1) * TC].rearrange(
                        "p (a t) -> p a t", a=1)), sm, 16))
                st_hist.append(st)
                outdma[0] += 1

            _chk(c, "g_chunk")
            if c + 2 < NCH:
                W("sync", "sV", bank_free[5])
                W("sync", *st_hist[-1])
                W("sync", *st_hist[-2])
                xv[c + 2] = x_load(c + 2)

    except _StopEmit:
        pass

    W("scalar", "dG0", CNT["dG0"])
    W("scalar", "dG1", CNT["dG1"])

    with nc.Block() as blk:
        @blk.sync
        def _(eng):
            for f in OPS["sync"]:
                f(eng)

        @blk.tensor
        def _(eng):
            for f in OPS["tensor"]:
                f(eng)

        @blk.vector
        def _(eng):
            for f in OPS["vector"]:
                f(eng)

        @blk.scalar
        def _(eng):
            for f in OPS["scalar"]:
                f(eng)

        @blk.gpsimd
        def _(eng):
            for f in OPS["gpsimd"]:
                f(eng)

    es.close()
    return nc


def kernel(L0_attn_h, L0_attn_W0, L0_attn_W1, L0_ffn_W, L0_ffn_b,
           L0_ffn_Wg, L0_ffn_bg):
    if "nc" not in _CACHE:
        _CACHE["nc"] = build()
    nc = _CACHE["nc"]

    f32 = lambda a: np.ascontiguousarray(np.asarray(a, dtype=np.float32))
    xh = f32(L0_attn_h)
    W0 = f32(L0_attn_W0)
    W1 = f32(L0_attn_W1)
    WT = np.ascontiguousarray(f32(L0_ffn_W).transpose(0, 2, 1))
    BF = f32(L0_ffn_b)
    WG = np.ascontiguousarray(f32(L0_ffn_Wg).T)
    BG = f32(L0_ffn_bg).reshape(1, NE)
    EYE = np.eye(128, dtype=np.float32)
    SEL = np.zeros((NE, NE * 128), dtype=ml_dtypes.bfloat16)
    for e in range(NE):
        SEL[e, e * 128:(e + 1) * 128] = 1
    shared = dict(W0=W0, W1=W1, WT=WT, BF=BF, WGT=WG, BG=BG, EYE=EYE, SEL=SEL)
    in_maps = []
    for b in range(8):
        m = dict(shared)
        m["xT"] = np.ascontiguousarray(xh[b].T)
        in_maps.append(m)
    res = run_bass_kernel_spmd(nc, in_maps, core_ids=list(range(8)))
    out = np.stack([np.asarray(r["outT"]).T for r in res.results], axis=0)
    return np.ascontiguousarray(out, dtype=np.float32)
